# version 06
# speedup vs baseline: 1.7283x; 1.0716x over previous
import numpy as np
import ml_dtypes
from contextlib import ExitStack

import concourse.bass as bass
import concourse.mybir as mybir
from concourse.bass_utils import run_bass_kernel_spmd

F32 = mybir.dt.float32
BF16 = mybir.dt.bfloat16
I32 = mybir.dt.int32
AF = mybir.ActivationFunctionType
ALU = mybir.AluOpType
AX = mybir.AxisListType
bfdt = ml_dtypes.bfloat16

ENGINES = ("tensor", "scalar", "vector", "gpsimd", "sync")

D = 2048
S = 4096
NH = 16
DH = 128
INTER = 5632
NCH = D // 128
EPS = 1e-5


def cdiv(a, b):
    return (a + b - 1) // b


class Res:

    def __init__(self):
        self.w = None
        self.reads = []

    def wdeps(self):
        d = list(self.reads)
        if self.w is not None:
            d.append(self.w)
        return d

    def rdeps(self):
        return [self.w] if self.w is not None else []

    def note_w(self, tok):
        self.w = tok
        self.reads = []

    def note_r(self, tok):
        self.reads.append(tok)


class Prog:

    def __init__(self):
        self.nc = bass.Bass("TRN2", target_bir_lowering=False)
        self.ops = {e: [] for e in ENGINES}
        self.count = {e: 0 for e in ENGINES}
        self.stack = ExitStack()
        self.dma_count = {}
        self.role_engines = ("sync", "gpsimd")
        self.regs = {}
        self.semmap = {}

    def sbuf(self, name, shape, dtype):
        return self.stack.enter_context(self.nc.sbuf_tensor(name, shape, dtype))

    def psum(self, name, shape, dtype):
        return self.stack.enter_context(self.nc.psum_tensor(name, shape, dtype))

    def inp(self, name, shape, dtype):
        return self.nc.dram_tensor(name, shape, dtype, kind="ExternalInput").ap()

    def outp(self, name, shape, dtype):
        return self.nc.dram_tensor(name, shape, dtype, kind="ExternalOutput").ap()

    def dram(self, name, shape, dtype, **kw):
        return self.nc.dram_tensor(name, shape, dtype, kind="Internal", **kw).ap()

    def op(self, eng, fn, deps=(), inc=1, ch=None):
        ds = []
        for d in deps:
            if d is None:
                continue
            if isinstance(d, list):
                ds.extend(x for x in d if x is not None)
            else:
                ds.append(d)
        dd = {}
        for (k, v) in ds:
            if v > dd.get(k, 0):
                dd[k] = v
        ds = list(dd.items())
        if ch is None:
            self.count[eng] += inc
            tok = (eng, self.count[eng])
        else:
            key = "q_" + ch
            self.dma_count[key] = self.dma_count.get(key, 0) + inc
            tok = (key, self.dma_count[key])
        self.ops[eng].append((ds, fn, inc, tok))
        return tok

    def dma(self, out, in_, deps=(), eng="sync", ch=None, **kw):
        if ch is None:
            ods = str(out.tensor.space)
            ids = str(in_.tensor.space)
            ch = out.tensor.name if "dram" not in ods.lower() \
                else (in_.tensor.name if "dram" not in ids.lower()
                      else out.tensor.name)
        return self.op(eng, lambda e: e.dma_start(out=out, in_=in_, **kw),
                       deps, inc=16, ch=ch)

    def mm(self, out, lhsT, rhs, start, stop, deps=()):
        return self.op("tensor",
                       lambda e: e.matmul(out, lhsT, rhs, start=start, stop=stop),
                       deps)

    def trans(self, out, in_, idn, deps=()):
        return self.op("tensor", lambda e: e.transpose(out, in_, idn), deps)

    def act(self, out, in_, func, deps=(), bias=0.0, scale=1.0):
        return self.op("scalar",
                       lambda e: e.activation(out, in_, func, bias=bias,
                                              scale=scale), deps)

    def tt(self, out, a, b, alu, deps=(), eng="vector"):
        return self.op(eng, lambda e: e.tensor_tensor(out, a, b, alu), deps)

    def ts(self, out, in0, s1, s2, op0, op1, deps=()):
        return self.op("vector",
                       lambda e: e.tensor_scalar(out, in0, s1, s2, op0, op1),
                       deps)

    def stt(self, out, in0, scalar, in1, op0, op1, deps=()):
        return self.op("vector",
                       lambda e: e.scalar_tensor_tensor(out, in0, scalar, in1,
                                                        op0, op1), deps)

    def red(self, out, in_, alu, deps=(), negate=False):
        return self.op("vector",
                       lambda e: e.tensor_reduce(out, in_, AX.X, alu,
                                                 negate=negate), deps)

    def copyv(self, out, in_, deps=()):
        return self.op("vector", lambda e: e.tensor_copy(out, in_), deps)

    def recip(self, out, in_, deps=()):
        return self.op("vector", lambda e: e.reciprocal(out, in_), deps)

    def memset(self, ap, val, deps=(), eng="vector"):
        return self.op(eng, lambda e: e.memset(ap, val), deps)

    def scan(self, out, d0, d1, init, deps=()):
        return self.op("vector",
                       lambda e: e.tensor_tensor_scan(out, d0, d1, init,
                                                      ALU.mult, ALU.add), deps)

    def max8(self, out, in_, deps=()):
        return self.op("vector", lambda e: e.max(out, in_), deps)


    def dma2(self, out_even, in_even, out_odd, in_odd, deps=(), eng="sync",
             ch=None, bit="par", **kw):
        if ch is None:
            t = out_even.tensor if "dram" not in str(
                out_even.tensor.space).lower() else in_even.tensor
            ch = t.name
        key = "q_" + ch
        self.dma_count[key] = self.dma_count.get(key, 0) + 16
        tok = (key, self.dma_count[key])

        def fn(e):
            rbit = self.regs[e._prog_name][bit]
            sem = self.semmap[key]
            with e.If(rbit):
                e.dma_start(out=out_odd, in_=in_odd, **kw).then_inc(sem, 16)
            with e.Else():
                e.dma_start(out=out_even, in_=in_even, **kw).then_inc(sem, 16)
            return None

        ds = []
        for d in deps:
            if d is None:
                continue
            if isinstance(d, list):
                ds.extend(x for x in d if x is not None)
            else:
                ds.append(d)
        dd = {}
        for (k, v) in ds:
            if v > dd.get(k, 0):
                dd[k] = v
        self.ops[eng].append((list(dd.items()), fn, 16, tok))
        return tok

    def poll(self, flag_ap, nonce_ap, deps=(), eng="sync", max_iter=3000000):
        def fn(e):
            R = self.regs[e._prog_name]
            rf, rc, rn, rv = R["s0"], R["s1"], R["s2"], R["s3"]
            e.load(rv, nonce_ap)
            e.reg_mov(rn, max_iter)
            e.load(rf, flag_ap)
            e.reg_alu(rc, rf, rv, ALU.not_equal)
            e.reg_alu(rc, rc, rn, ALU.logical_and)
            with e.While(rc):
                e.load(rf, flag_ap)
                e.reg_sub(rn, rn, 1)
                e.reg_alu(rc, rf, rv, ALU.not_equal)
                e.reg_alu(rc, rc, rn, ALU.logical_and)
            return e.nop()

        return self.op(eng, fn, deps, inc=1)

    def build(self):
        nc = self.nc
        sems = {e: self.stack.enter_context(nc.semaphore(f"sem_{e}"))
                for e in ENGINES}
        for key in self.dma_count:
            sems[key] = self.stack.enter_context(nc.semaphore(key))
        self.semmap = sems
        block = self.stack.enter_context(nc.Block())
        for e in ENGINES:
            if not self.ops[e]:
                continue

            def body(eng, e=e):
                eng._prog_name = e
                if e in self.role_engines:
                    R = {}
                    for nm in ("pid", "par", "off", "s0", "s1", "s2", "s3"):
                        R[nm] = self.stack.enter_context(
                            eng.register(f"{e}_{nm}"))
                    eng.load(R["pid"], self.nc.partition_id_tensor.ap())
                    eng.reg_alu(R["par"], R["pid"], 1, ALU.bitwise_and)
                    eng.reg_alu(R["off"], R["pid"], 2, ALU.bitwise_and)
                    self.regs[e] = R
                hw = {}
                for deps, fn, inc, tok in self.ops[e]:
                    for (de, dv) in deps:
                        if de == e and e == "tensor":
                            continue
                        if hw.get(de, 0) >= dv:
                            continue
                        eng.wait_ge(sems[de], dv)
                        hw[de] = dv
                    instr = fn(eng)
                    if instr is not None:
                        instr.then_inc(sems[tok[0]], inc)

            getattr(block, e)(body)
        final = dict(self.count)
        finalq = dict(self.dma_count)

        def closer(eng):
            for e2, v in final.items():
                if v > 0:
                    eng.wait_ge(sems[e2], v)
            for k, v in finalq.items():
                eng.wait_ge(sems[k], v)

        block.sync(closer)
        self.stack.close()
        return nc



def alloc_common(p):
    c = {}
    c["ones_col_bf"] = p.sbuf("ones_col_bf", [128, 1], BF16)
    c["ones_row_f"] = p.sbuf("ones_row_f", [1, 128], F32)
    c["idn_bf"] = p.sbuf("idn_bf", [128, 128], BF16)
    co = p.inp("c_ones_col", [128, 1], F32)
    cr = p.inp("c_idn", [128, 128], F32)
    t1 = p.dma(c["ones_col_bf"][:, :], co, eng="gpsimd")
    t2 = p.dma(c["ones_row_f"][0:1, :],
               co.rearrange("p one -> one p"), eng="sync")
    t3 = p.dma(c["idn_bf"][:, :], cr, eng="gpsimd")
    c["const_toks"] = [t1, t2, t3]
    return c


def emit_ln(p, C, x_dram, t0, nt, dst_sb, dst_res, dst_off, out_dtype_bf=True,
            xs=None, xs_res=None, keep_cb=None, tag="", src_deps=(),
            src_view=None, gb=None):
    P = C["P"]
    TT = 512
    sumt, sum_r = P["sum"]
    sqt, sq_r = P["sq"]
    bca, bca_r = P["bcA"]
    bcb, bcb_r = P["bcB"]
    sq_sb, sq_sb_r = C["sq_tile"]
    mrow, mrow_r = C["rows"]["mean"]
    vrow, vrow_r = C["rows"]["var"]
    arow, arow_r = C["rows"]["a"]
    brow, brow_r = C["rows"]["b"]
    toks = []
    for it in range(cdiv(nt, TT)):
        tt = min(TT, nt - it * TT)
        if src_view is None:
            src = x_dram[:, t0 + it * TT: t0 + it * TT + tt] \
                .rearrange("(c p) t -> p c t", p=128)
            ld = p.dma(xs[:, 0:16 * tt].rearrange("p (c t) -> p c t", t=tt),
                       src, deps=xs_res.wdeps() + list(src_deps), eng="gpsimd",
                       ch="lnxs")
            xs_res.note_w(ld)

            def xv(c, tt=tt):
                return xs[:, c * tt:(c + 1) * tt]
        else:
            ld = None

            def xv(c, tt=tt, it=it):
                return src_view(c, it, tt)

            def _noop(tok):
                pass
            xs_res = Res()
            xs_res.note_r = _noop
            xs_res.note_w = _noop
            if src_deps:
                ld = src_deps[-1]
        ones_sum = C["ones_col_bf"] if src_view is not None else C["ones_col_f"]

        for c in range(NCH):
            tok = p.mm(sumt[0:1, 0:tt], ones_sum[0:128, 0:1], xv(c),
                       start=(c == 0), stop=(c == NCH - 1),
                       deps=[ld] + (sum_r.wdeps() if c == 0 else []))
            xs_res.note_r(tok)
        sum_r.note_w(tok)
        mm_sum = tok
        for c in range(NCH):
            sq = p.act(sq_sb[:, 0:tt], xv(c), AF.Square,
                       deps=[ld] + sq_sb_r.wdeps())
            xs_res.note_r(sq)
            sq_sb_r.note_w(sq)
            tok = p.mm(sqt[0:1, 0:tt], C["ones_col_f"][0:128, 0:1],
                       sq_sb[:, 0:tt], start=(c == 0), stop=(c == NCH - 1),
                       deps=[sq] + (sq_r.wdeps() if c == 0 else []))
            sq_sb_r.note_r(tok)
        sq_r.note_w(tok)
        mm_sq = tok
        t_m = p.act(mrow[0:1, 0:tt], sumt[0:1, 0:tt], AF.Copy, scale=1.0 / D,
                    deps=[mm_sum] + mrow_r.wdeps())
        mrow_r.note_w(t_m)
        sum_r.note_r(t_m)
        t_v = p.ts(vrow[0:1, 0:tt], sqt[0:1, 0:tt], 1.0 / D, None, ALU.mult,
                   ALU.bypass, deps=[mm_sq] + vrow_r.wdeps())
        vrow_r.note_w(t_v)
        sq_r.note_r(t_v)
        t_msq = p.act(sq_sb[:, 0:tt][0:1, 0:tt], mrow[0:1, 0:tt], AF.Square,
                      deps=[t_m] + sq_sb_r.wdeps())
        sq_sb_r.note_w(t_msq)
        mrow_r.note_r(t_msq)
        t_v2 = p.tt(vrow[0:1, 0:tt], vrow[0:1, 0:tt], sq_sb[0:1, 0:tt],
                    ALU.subtract, deps=[t_v, t_msq])
        vrow_r.note_w(t_v2)
        sq_sb_r.note_r(t_v2)
        t_sd = p.act(arow[0:1, 0:tt], vrow[0:1, 0:tt], AF.Sqrt,
                     bias=C["eps1"][0:1, 0:1],
                     deps=[t_v2] + arow_r.wdeps() + C.get("ct", []))
        vrow_r.note_r(t_sd)
        t_a = p.recip(arow[0:1, 0:tt], arow[0:1, 0:tt], deps=[t_sd])
        arow_r.note_w(t_a)
        t_b = p.stt(brow[0:1, 0:tt], mrow[0:1, 0:tt], -1.0, arow[0:1, 0:tt],
                    ALU.mult, ALU.mult, deps=[t_a, t_m] + brow_r.wdeps())
        brow_r.note_w(t_b)
        mrow_r.note_r(t_b)
        arow_r.note_r(t_b)
        t_ba = p.mm(bca[:, 0:tt], C["ones_row_f"][0:1, 0:128], arow[0:1, 0:tt],
                    start=True, stop=True, deps=[t_a] + bca_r.wdeps())
        bca_r.note_w(t_ba)
        arow_r.note_r(t_ba)
        t_bb = p.mm(bcb[:, 0:tt], C["ones_row_f"][0:1, 0:128], brow[0:1, 0:tt],
                    start=True, stop=True, deps=[t_b] + bcb_r.wdeps())
        bcb_r.note_w(t_bb)
        brow_r.note_r(t_bb)
        tmp, tmp_r = C["ln_tmp"]
        for c in range(NCH):
            t1 = p.tt(tmp[:, 0:tt], xv(c), bca[:, 0:tt], ALU.mult,
                      deps=[ld, t_ba] + tmp_r.wdeps())
            xs_res.note_r(t1)
            bca_r.note_r(t1)
            tmp_r.note_w(t1)
            dcol = dst_off + c * nt + it * TT
            if gb is None:
                t2 = p.tt(dst_sb[:, dcol:dcol + tt], tmp[:, 0:tt], bcb[:, 0:tt],
                          ALU.add, deps=[t1, t_bb] + dst_res.wdeps())
                tmp_r.note_r(t2)
            else:
                t2a = p.tt(tmp[:, 0:tt], tmp[:, 0:tt], bcb[:, 0:tt],
                           ALU.add, deps=[t1, t_bb])
                tmp_r.note_w(t2a)
                gcol, bcol = gb
                t2 = p.ts(dst_sb[:, dcol:dcol + tt], tmp[:, 0:tt],
                          gcol(c), bcol(c), ALU.mult, ALU.add,
                          deps=[t2a] + dst_res.wdeps() + C.get("ct", []))
                tmp_r.note_r(t2)
            bcb_r.note_r(t2)
            dst_res.note_r(t2)
            toks.append(t2)
            if keep_cb is not None:
                keep_cb(c, it, tt, xv(c), ld)
    for t in toks:
        dst_res.note_r(t)
    dst_res.note_w(toks[-1])
    dst_res.reads = list(toks)
    return toks


def emit_mm_block(p, C, out_cb, wq_dram, wq_cols, rhs_sb, rhs_res, rhs_off,
                  nt, wstage, wstage_res, evac, nk=NCH, kchunk=1024):
    P = C["P"]
    acc = P["acc"]
    TT = 512
    ntt = cdiv(nt, TT)
    for mi, m in enumerate(wq_cols):
        ws, ws_r = wstage[mi % 2]
        src = wq_dram[:, m * 128:(m + 1) * 128] \
            .rearrange("(c p) m -> p c m", p=128)
        lw = p.dma(ws[:, 0:nk * 128].rearrange("p (c m) -> p c m", m=128),
                   src, deps=ws_r.wdeps(), eng="gpsimd")
        ws_r.note_w(lw)
        for it in range(ntt):
            tt = min(TT, nt - it * TT)
            pa, pa_r = acc[(mi * ntt + it) % 2]
            for k in range(nk):
                deps = [lw] + rhs_res.rdeps()
                if k == 0:
                    deps += pa_r.wdeps()
                tok = p.mm(pa[:, 0:tt], ws[:, k * 128:(k + 1) * 128],
                           rhs_sb[:, rhs_off + k * kchunk + it * TT:
                                  rhs_off + k * kchunk + it * TT + tt],
                           start=(k == 0), stop=(k == nk - 1), deps=deps)
                ws_r.note_r(tok)
                rhs_res.note_r(tok)
            pa_r.note_w(tok)
            ev = evac(mi, m, it, tt, pa[:, 0:tt], [tok])
            pa_r.note_r(ev)



def build_launch1():
    p = Prog()
    x_sl = p.inp("x_sl", [D, 1024], F32)
    wqkvT = p.inp("wqkvT", [D, 3 * D], F32)
    bqkv_r = p.inp("bqkv_r", [128, 48], F32)
    wgT = p.inp("wgT", [D, INTER], F32)
    wuT = p.inp("wuT", [D, INTER], F32)
    wdT = p.inp("wdT", [INTER, D], F32)
    gs_r = p.inp("gs_r", [128, 44], F32)
    us_r = p.inp("us_r", [128, 44], F32)
    ds_r = p.inp("ds_r", [128, 16], F32)

    q_out = p.outp("q_out", [D, 1024], BF16)
    k_out = p.outp("k_out", [D, 1024], BF16)
    vt_out = p.outp("vt_out", [1024, D], BF16)
    r1p_out = p.outp("r1p_out", [D, 1024], F32)

    C = {}
    C["ones_col_f"] = p.sbuf("ones_col_f", [128, 1], F32)
    C["ones_row_f"] = p.sbuf("ones_row_f", [1, 128], F32)
    C["idn_bf"] = p.sbuf("idn_bf", [128, 128], BF16)
    co = p.inp("c_ones_col", [128, 1], F32)
    cidn = p.inp("c_idn", [128, 128], F32)
    ct = [p.dma(C["ones_col_f"][:, :], co, eng="gpsimd"),
          p.dma(C["ones_row_f"][0:1, :], co.rearrange("p one -> one p"),
                eng="sync"),
          p.dma(C["idn_bf"][:, :], cidn, eng="gpsimd")]
    biases = p.sbuf("biases", [128, 48 + 44 + 44 + 16], F32)
    ct.append(p.dma(biases[:, 0:48], bqkv_r, eng="sync"))
    ct.append(p.dma(biases[:, 48:92], gs_r, eng="sync"))
    ct.append(p.dma(biases[:, 92:136], us_r, eng="sync"))
    ct.append(p.dma(biases[:, 136:152], ds_r, eng="sync"))
    bq = lambda m: biases[:, m:m + 1]
    gsv = lambda m: biases[:, 48 + m:48 + m + 1]
    usv = lambda m: biases[:, 92 + m:92 + m + 1]
    dsv = lambda m: biases[:, 136 + m:136 + m + 1]

    C["eps1"] = p.sbuf("eps1", [1, 1], F32)
    ct.append(p.memset(C["eps1"][0:1, 0:1], EPS))
    C["ct"] = ct
    for e in ENGINES:
        p.op(e, lambda eng: eng.nop(), deps=ct)

    P = {}
    P["acc"] = [(p.psum("accA", [128, 512], F32), Res()),
                (p.psum("accB", [128, 512], F32), Res())]
    P["bcA"] = (p.psum("bcA", [128, 512], F32), Res())
    P["bcB"] = (p.psum("bcB", [128, 512], F32), Res())
    P["sum"] = (p.psum("psumrow", [1, 512], F32), Res())
    P["sq"] = (p.psum("psqrow", [1, 512], F32), Res())
    P["tr"] = (p.psum("ptr", [128, 512], BF16), Res())
    C["P"] = P

    xs = p.sbuf("xs", [128, 16 * 512], F32)
    xs_res = Res()
    xn = p.sbuf("xn", [128, NCH * 1024], BF16)
    xn_res = Res()
    C["sq_tile"] = (p.sbuf("sq_tile", [128, 512], F32), Res())
    C["ln_tmp"] = (p.sbuf("ln_tmp", [128, 512], F32), Res())
    rows = p.sbuf("rows", [1, 4 * 512], F32)
    C["rows"] = {nm: (rows[0:1, i * 512:(i + 1) * 512], Res())
                 for i, nm in enumerate(("mean", "var", "a", "b"))}
    wstage = [(p.sbuf("wsA", [128, NCH * 128], BF16), Res()),
              (p.sbuf("wsB", [128, NCH * 128], BF16), Res())]
    wstage2 = [(p.sbuf("ws2A", [128, NCH * 128], BF16), Res()),
               (p.sbuf("ws2B", [128, NCH * 128], BF16), Res())]
    wdstage = [(p.sbuf("wdA", [128, 44 * 128], BF16), Res()),
               (p.sbuf("wdB", [128, 44 * 128], BF16), Res())]
    ev_bf = [(p.sbuf("evbfA", [128, 512], BF16), Res()),
             (p.sbuf("evbfB", [128, 512], BF16), Res())]
    g_t = (p.sbuf("g_t", [128, 512], F32), Res())
    u_t = (p.sbuf("u_t", [128, 512], F32), Res())
    d_t = (p.sbuf("d_t", [128, 512], F32), Res())
    xre = [(p.sbuf("xreA", [128, 512], F32), Res()),
           (p.sbuf("xreB", [128, 512], F32), Res())]
    r1t = [(p.sbuf("r1tA", [128, 512], F32), Res()),
           (p.sbuf("r1tB", [128, 512], F32), Res())]
    h_sb = p.sbuf("h_sb", [128, 44 * 512], BF16)
    h_res = Res()
    vt_ev = [(p.sbuf("vtevA", [128, 128], BF16), Res()),
             (p.sbuf("vtevB", [128, 128], BF16), Res())]

    for (_, r) in (C["sq_tile"], C["ln_tmp"]):
        pass
    xs_res.note_w(ct[0])
    for t in ct:
        xs_res.note_r(t)

    emit_ln(p, C, x_sl, 0, 1024, xn, xn_res, 0, xs=xs, xs_res=xs_res)

    def evac_qkv(mi, m, it, tt, pa, deps):
        eb, eb_r = ev_bf[it % 2]
        tok = p.act(eb[:, 0:tt], pa, AF.Copy, bias=0.0, scale=1.0,
                    deps=deps + eb_r.wdeps())
        return tok

    def evac_qkv2(mi, m, it, tt, pa, deps):
        eb, eb_r = ev_bf[it % 2]
        tok = p.ts(eb[:, 0:tt], pa, bq(m), None, ALU.add, ALU.bypass,
                   deps=deps + eb_r.wdeps())
        eb_r.note_w(tok)
        if m < 16:
            d = p.dma(q_out[m * 128:(m + 1) * 128,
                            it * 512:it * 512 + tt], eb[:, 0:tt],
                      deps=[tok], eng="sync")
            eb_r.note_r(d)
        elif m < 32:
            mm = m - 16
            d = p.dma(k_out[mm * 128:(mm + 1) * 128,
                            it * 512:it * 512 + tt], eb[:, 0:tt],
                      deps=[tok], eng="sync")
            eb_r.note_r(d)
        else:
            mm = m - 32
            trp, trp_r = P["tr"]
            for j in range(cdiv(tt, 128)):
                tj = p.trans(trp[:, j * 128:(j + 1) * 128],
                             eb[:, j * 128:(j + 1) * 128], C["idn_bf"][:, :],
                             deps=[tok] + trp_r.wdeps())
                eb_r.note_r(tj)
                ve, ve_r = vt_ev[j % 2]
                cv = p.copyv(ve[:, :], trp[:, j * 128:(j + 1) * 128],
                             deps=[tj] + ve_r.wdeps())
                ve_r.note_w(cv)
                trp_r.note_w(tj)
                trp_r.note_r(cv)
                d = p.dma(vt_out[it * 512 + j * 128: it * 512 + j * 128 + 128,
                                 mm * 128:(mm + 1) * 128], ve[:, :],
                          deps=[cv], eng="sync")
                ve_r.note_r(d)
        return tok

    emit_mm_block(p, C, None, wqkvT, list(range(48)), xn, xn_res, 0, 1024,
                  wstage, None, evac_qkv2)

    for it in range(2):
        for m in range(44):
            wsg, wsg_r = wstage[m % 2]
            wsu, wsu_r = wstage2[m % 2]
            lg = p.dma(wsg[:, 0:NCH * 128].rearrange("p (c m) -> p c m", m=128),
                       wgT[:, m * 128:(m + 1) * 128]
                       .rearrange("(c p) m -> p c m", p=128),
                       deps=wsg_r.wdeps(), eng="gpsimd")
            wsg_r.note_w(lg)
            lu = p.dma(wsu[:, 0:NCH * 128].rearrange("p (c m) -> p c m", m=128),
                       wuT[:, m * 128:(m + 1) * 128]
                       .rearrange("(c p) m -> p c m", p=128),
                       deps=wsu_r.wdeps(), eng="gpsimd")
            wsu_r.note_w(lu)
            pa, pa_r = P["acc"][0]
            pb, pb_r = P["acc"][1]
            for k in range(NCH):
                tok = p.mm(pa[:, 0:512], wsg[:, k * 128:(k + 1) * 128],
                           xn[:, k * 1024 + it * 512: k * 1024 + it * 512 + 512],
                           start=(k == 0), stop=(k == NCH - 1),
                           deps=[lg] + xn_res.rdeps()
                           + (pa_r.wdeps() if k == 0 else []))
                wsg_r.note_r(tok)
                xn_res.note_r(tok)
            pa_r.note_w(tok)
            tg = tok
            for k in range(NCH):
                tok = p.mm(pb[:, 0:512], wsu[:, k * 128:(k + 1) * 128],
                           xn[:, k * 1024 + it * 512: k * 1024 + it * 512 + 512],
                           start=(k == 0), stop=(k == NCH - 1),
                           deps=[lu] + xn_res.rdeps()
                           + (pb_r.wdeps() if k == 0 else []))
                wsu_r.note_r(tok)
                xn_res.note_r(tok)
            pb_r.note_w(tok)
            tu = tok
            gt, gt_r = g_t
            ut, ut_r = u_t
            t1 = p.act(gt[:, :], pa[:, 0:512], AF.Silu, scale=gsv(m),
                       deps=[tg] + gt_r.wdeps())
            gt_r.note_w(t1)
            pa_r.note_r(t1)
            t2 = p.act(ut[:, :], pb[:, 0:512], AF.Copy, scale=usv(m),
                       deps=[tu] + ut_r.wdeps())
            ut_r.note_w(t2)
            pb_r.note_r(t2)
            t3 = p.tt(h_sb[:, m * 512:(m + 1) * 512], gt[:, :], ut[:, :],
                      ALU.mult, deps=[t1, t2] + h_res.wdeps())
            gt_r.note_r(t3)
            ut_r.note_r(t3)
            h_res.note_r(t3)
            h_res.reads.append(t3)
        h_res.note_w(t3)
        h_res.reads = []
        h_last = t3
        for m2 in range(16):
            wd, wd_r = wdstage[m2 % 2]
            lw = p.dma(wd[:, 0:44 * 128].rearrange("p (c m) -> p c m", m=128),
                       wdT[:, m2 * 128:(m2 + 1) * 128]
                       .rearrange("(c p) m -> p c m", p=128),
                       deps=wd_r.wdeps(), eng="gpsimd")
            wd_r.note_w(lw)
            pa, pa_r = P["acc"][m2 % 2]
            for k in range(44):
                tok = p.mm(pa[:, 0:512], wd[:, k * 128:(k + 1) * 128],
                           h_sb[:, k * 512:(k + 1) * 512],
                           start=(k == 0), stop=(k == 43),
                           deps=[lw, h_last]
                           + (pa_r.wdeps() if k == 0 else []))
                wd_r.note_r(tok)
                h_res.note_r(tok)
            pa_r.note_w(tok)
            dt, dt_r = d_t
            t1 = p.act(dt[:, :], pa[:, 0:512], AF.Copy, scale=dsv(m2),
                       deps=[tok] + dt_r.wdeps())
            dt_r.note_w(t1)
            pa_r.note_r(t1)
            xr, xr_r = xre[m2 % 2]
            lx = p.dma(xr[:, :],
                       x_sl[m2 * 128:(m2 + 1) * 128, it * 512:(it + 1) * 512],
                       deps=xr_r.wdeps(), eng="sync")
            xr_r.note_w(lx)
            rt, rt_r = r1t[m2 % 2]
            t2 = p.tt(rt[:, :], dt[:, :], xr[:, :], ALU.add,
                      deps=[t1, lx] + rt_r.wdeps())
            rt_r.note_w(t2)
            dt_r.note_r(t2)
            xr_r.note_r(t2)
            d = p.dma(r1p_out[m2 * 128:(m2 + 1) * 128,
                              it * 512:(it + 1) * 512], rt[:, :],
                      deps=[t2], eng="sync")
            rt_r.note_r(d)

    return p.build()



def _prep_weights_l1(inputs):
    w = {}
    w["wqkvT"] = np.ascontiguousarray(inputs["L0_attn_w_qkv"].T)
    w["bqkv_r"] = np.ascontiguousarray(
        inputs["L0_attn_b_qkv"].reshape(48, 128).T)
    w["wgT"] = np.ascontiguousarray(inputs["L0_ffn_gate_w"].T)
    w["wuT"] = np.ascontiguousarray(inputs["L0_ffn_up_w"].T)
    w["wdT"] = np.ascontiguousarray(inputs["L0_ffn_down_w"].T)
    w["gs_r"] = np.ascontiguousarray(
        inputs["L0_ffn_gate_s"].reshape(44, 128).T)
    w["us_r"] = np.ascontiguousarray(
        inputs["L0_ffn_up_s"].reshape(44, 128).T)
    w["ds_r"] = np.ascontiguousarray(
        inputs["L0_ffn_down_s"].reshape(16, 128).T)
    w["c_ones_col"] = np.ones((128, 1), np.float32)
    w["c_idn"] = np.eye(128, dtype=np.float32)
    return w


def run_launch1(inputs):
    x = np.asarray(inputs["L0_attn_x"])
    x_fm = np.ascontiguousarray(x.transpose(0, 2, 1))
    w = _prep_weights_l1(inputs)
    nc = build_launch1()
    in_maps = []
    for c in range(8):
        b, g = c // 4, c % 4
        m = {"x_sl": np.ascontiguousarray(x_fm[b][:, g * 1024:(g + 1) * 1024])}
        m.update(w)
        in_maps.append(m)
    res = run_bass_kernel_spmd(nc, in_maps, core_ids=list(range(8)))
    return res.results





def build_launch2():
    p = Prog()
    q_in = p.inp("q_in", [D, 1024], BF16)
    k_full = p.inp("k_full", [D, S], BF16)
    vt_full = p.inp("vt_full", [S, D], BF16)
    r1p_in = p.inp("r1p_in", [D, 1024], F32)
    woT = p.inp("woT", [D, D], F32)
    bo_r = p.inp("bo_r", [128, 16], F32)
    wv1 = p.inp("wv1", [D, D], F32)
    beta_in = p.inp("beta_in", [1, 16], F32)

    r1_out = p.outp("r1_out", [D, 1024], F32)
    xn1_out = p.outp("xn1_out", [D, 1024], BF16)
    v_out = p.outp("v_out", [D, 1024], BF16)
    carry_out = p.outp("carry_out", [16, 128], F32)

    C = {}
    C["ones_col_f"] = p.sbuf("ones_col_f", [128, 1], F32)
    C["ones_col_bf"] = p.sbuf("ones_col_bf", [128, 1], BF16)
    C["ones_row_f"] = p.sbuf("ones_row_f", [1, 128], F32)
    C["idn_f"] = p.sbuf("idn_f", [128, 128], F32)
    co = p.inp("c_ones_col", [128, 1], F32)
    cidn = p.inp("c_idn", [128, 128], F32)
    ct = [p.dma(C["ones_col_f"][:, :], co, eng="sync"),
          p.dma(C["ones_col_bf"][:, :], co, eng="gpsimd"),
          p.dma(C["ones_row_f"][0:1, :], co.rearrange("p one -> one p"),
                eng="sync"),
          p.dma(C["idn_f"][:, :], cidn, eng="sync")]
    biases = p.sbuf("biases2", [128, 16], F32)
    ct.append(p.dma(biases[:, 0:16], bo_r, eng="sync"))
    bov = lambda m: biases[:, m:m + 1]
    beta_sb = p.sbuf("beta_sb", [1, 16], F32)
    ct.append(p.dma(beta_sb[0:1, :], beta_in, eng="sync"))
    C["eps1"] = p.sbuf("eps1", [1, 1], F32)
    ct.append(p.memset(C["eps1"][0:1, 0:1], EPS))
    C["ct"] = ct
    for e in ENGINES:
        p.op(e, lambda eng: eng.nop(), deps=ct)

    P = {}
    P["acc"] = [(p.psum("accA", [128, 512], F32), Res()),
                (p.psum("accB", [128, 512], F32), Res())]
    P["bcA"] = (p.psum("bcA", [128, 512], F32), Res())
    P["bcB"] = (p.psum("bcB", [128, 512], F32), Res())
    P["sum"] = (p.psum("psumrow", [1, 512], F32), Res())
    P["sq"] = (p.psum("psqrow", [1, 512], F32), Res())
    P["tr32"] = (p.psum("tr32", [16, 128], F32), Res())
    C["P"] = P
    scoresP = P["bcA"]
    avP = P["bcB"]
    sigP = P["sum"]
    nbcP = P["sq"]

    kh = p.sbuf("kh", [128, S], BF16)
    kh_r = Res()
    vh = p.sbuf("vh", [128, 32 * 128], BF16)
    vh_r = Res()
    qh = p.sbuf("qh", [128, 1024], BF16)
    qh_r = Res()
    sqq = p.sbuf("sqq", [128, 1024], BF16)
    sqq_r = Res()
    probs = [(p.sbuf("probsA", [128, 512], BF16), Res()),
             (p.sbuf("probsB", [128, 512], BF16), Res())]
    attn = p.sbuf("attn", [128, 16 * 1024], BF16)
    attn_r = Res()
    av_sb = (p.sbuf("av_sb", [128, 512], F32), Res())
    rowbuf = p.sbuf("rowbuf", [1, 1024], F32)
    rowbuf_r = Res()
    tiny = p.sbuf("tiny", [1, 8], F32)
    tiny_r = Res()
    cbias = p.sbuf("cbias", [128, 1], F32)
    cbias_r = Res()
    srow = p.sbuf("srow", [1, 512], F32)
    srow_r = Res()

    xs = p.sbuf("xs", [128, 16 * 512], F32)
    xs_res = Res()
    xn1 = p.sbuf("xn1", [128, NCH * 1024], BF16)
    xn1_res = Res()
    C["sq_tile"] = (p.sbuf("sq_tile", [128, 512], F32), Res())
    C["ln_tmp"] = (p.sbuf("ln_tmp", [128, 512], F32), Res())
    rows = p.sbuf("rows", [1, 4 * 512], F32)
    C["rows"] = {nm: (rows[0:1, i * 512:(i + 1) * 512], Res())
                 for i, nm in enumerate(("mean", "var", "a", "b"))}
    wstage = [(p.sbuf("wsA", [128, NCH * 128], BF16), Res()),
              (p.sbuf("wsB", [128, NCH * 128], BF16), Res())]
    ev_bf = [(p.sbuf("evbfA", [128, 512], BF16), Res()),
             (p.sbuf("evbfB", [128, 512], BF16), Res())]
    r1tile = [(p.sbuf("r1tA", [128, 512], F32), Res()),
              (p.sbuf("r1tB", [128, 512], F32), Res())]
    rptile = [(p.sbuf("rptA", [128, 512], F32), Res()),
              (p.sbuf("rptB", [128, 512], F32), Res())]
    v_sb = p.sbuf("v_sb", [128, 16 * 1024], BF16)
    v_sb_r = Res()
    lam_big = p.sbuf("lam_big", [128, 1024], F32)
    lam_big_r = Res()
    ones_big = p.sbuf("ones_big", [128, 1024], F32)
    ct2 = p.memset(ones_big[:, :], 1.0)
    lam_row = p.sbuf("lam_row", [1, 16], F32)
    lam_row_r = Res()
    lamcol = p.sbuf("lamcol", [128, 1], F32)
    lamcol_r = Res()
    st_sc = p.sbuf("st_sc", [128, 1024], F32)
    st_sc_r = Res()
    carry_sb = p.sbuf("carry_sb", [128, 16], F32)
    carry_sb_r = Res()
    carry_ev = p.sbuf("carry_ev", [16, 128], F32)
    carry_ev_r = Res()

    SC = 1.0 / np.sqrt(128.0)

    for h in range(NH):
        lk = p.dma(kh[:, :], k_full[h * 128:(h + 1) * 128, :],
                   deps=kh_r.wdeps(), eng="sync")
        kh_r.note_w(lk)
        lv = p.dma(vh[:, :].rearrange("p (c dh) -> p c dh", dh=128),
                   vt_full[:, h * 128:(h + 1) * 128]
                   .rearrange("(c p) dh -> p c dh", p=128),
                   deps=vh_r.wdeps(), eng="sync")
        vh_r.note_w(lv)
        lq = p.dma(qh[:, :], q_in[h * 128:(h + 1) * 128, :],
                   deps=qh_r.wdeps(), eng="sync")
        qh_r.note_w(lq)
        t_sq = p.act(sqq[:, 0:1024], qh[:, :], AF.Square,
                     deps=[lq] + sqq_r.wdeps())
        sqq_r.note_w(t_sq)
        qh_r.note_r(t_sq)
        sgt, sg_r = sigP
        for half in range(2):
            tok = p.mm(sgt[0:1, 0:512], C["ones_col_bf"][0:128, 0:1],
                       sqq[:, half * 512:(half + 1) * 512],
                       start=True, stop=True,
                       deps=[t_sq] + sg_r.wdeps())
            sg_r.note_w(tok)
            sqq_r.note_r(tok)
            cv = p.copyv(rowbuf[0:1, half * 512:(half + 1) * 512],
                         sgt[0:1, 0:512], deps=[tok] + rowbuf_r.wdeps())
            sg_r.note_r(cv)
            rowbuf_r.note_w(cv)
        t_mq = p.red(tiny[0:1, 0:1], rowbuf[0:1, 0:1024], ALU.max,
                     deps=rowbuf_r.rdeps() + tiny_r.wdeps())
        tiny_r.note_w(t_mq)
        rowbuf_r.note_r(t_mq)
        t_mk_parts = []
        for piece in range(4):
            t_sqk = p.act(sqq[:, 0:1024], kh[:, piece * 1024:(piece + 1) * 1024],
                          AF.Square, deps=[lk, t_mq] + sqq_r.wdeps())
            sqq_r.note_w(t_sqk)
            kh_r.note_r(t_sqk)
            for half in range(2):
                tok = p.mm(sgt[0:1, 0:512], C["ones_col_bf"][0:128, 0:1],
                           sqq[:, half * 512:(half + 1) * 512],
                           start=True, stop=True,
                           deps=[t_sqk] + sg_r.wdeps())
                sg_r.note_w(tok)
                sqq_r.note_r(tok)
                cv = p.copyv(rowbuf[0:1, half * 512:(half + 1) * 512],
                             sgt[0:1, 0:512], deps=[tok] + rowbuf_r.wdeps())
                sg_r.note_r(cv)
                rowbuf_r.note_w(cv)
            t_mk = p.red(tiny[0:1, 1 + piece:2 + piece], rowbuf[0:1, 0:1024],
                         ALU.max, deps=rowbuf_r.rdeps() + [t_mq])
            rowbuf_r.note_r(t_mk)
            t_mk_parts.append(t_mk)
        t_mk4 = p.red(tiny[0:1, 5:6], tiny[0:1, 1:5], ALU.max,
                      deps=t_mk_parts)
        t_c1 = p.tt(tiny[0:1, 6:7], tiny[0:1, 0:1], tiny[0:1, 5:6], ALU.mult,
                    deps=[t_mq, t_mk4])
        t_c2 = p.act(tiny[0:1, 7:8], tiny[0:1, 6:7], AF.Sqrt, scale=1.0 / 128,
                     deps=[t_c1])
        t_c3 = p.ts(tiny[0:1, 7:8], tiny[0:1, 7:8], -1.0, None, ALU.mult,
                    ALU.bypass, deps=[t_c2])
        tiny_r.note_w(t_c3)
        pa0, pa0_r = P["acc"][0]
        t_bc = p.mm(pa0[:, 0:1], C["ones_row_f"][0:1, 0:128], tiny[0:1, 7:8],
                    start=True, stop=True, deps=[t_c3] + pa0_r.wdeps())
        pa0_r.note_w(t_bc)
        tiny_r.note_r(t_bc)
        t_cb = p.copyv(cbias[:, 0:1], pa0[:, 0:1],
                       deps=[t_bc] + cbias_r.wdeps())
        cbias_r.note_w(t_cb)
        pa0_r.note_r(t_cb)
        spt, sp_r = scoresP
        avt, av_r = avP
        for qp in range(2):
            for kc in range(32):
                t_s = p.mm(spt[:, 0:512], kh[:, kc * 128:(kc + 1) * 128],
                           qh[:, qp * 512:(qp + 1) * 512],
                           start=True, stop=True,
                           deps=[lk, lq] + sp_r.wdeps())
                sp_r.note_w(t_s)
                kh_r.note_r(t_s)
                qh_r.note_r(t_s)
                pb, pb_r = probs[kc % 2]
                t_e = p.act(pb[:, :], spt[:, 0:512], AF.Exp, bias=cbias[:, 0:1],
                            scale=SC, deps=[t_s, t_cb] + pb_r.wdeps())
                pb_r.note_w(t_e)
                sp_r.note_r(t_e)
                cbias_r.note_r(t_e)
                t_av = p.mm(avt[:, 0:512], vh[:, kc * 128:(kc + 1) * 128],
                            pb[:, :], start=(kc == 0), stop=(kc == 31),
                            deps=[lv, t_e] + (av_r.wdeps() if kc == 0 else []))
                pb_r.note_r(t_av)
                vh_r.note_r(t_av)
                t_sg = p.mm(sgt[0:1, 0:512], C["ones_col_bf"][0:128, 0:1],
                            pb[:, :], start=(kc == 0), stop=(kc == 31),
                            deps=[t_e] + (sg_r.wdeps() if kc == 0 else []))
                pb_r.note_r(t_sg)
            av_r.note_w(t_av)
            sg_r.note_w(t_sg)
            t_r = p.recip(srow[0:1, 0:512], sgt[0:1, 0:512],
                          deps=[t_sg] + srow_r.wdeps())
            srow_r.note_w(t_r)
            sg_r.note_r(t_r)
            t_bcr = p.mm(pa0[:, 0:512], C["ones_row_f"][0:1, 0:128],
                         srow[0:1, 0:512], start=True, stop=True,
                         deps=[t_r] + pa0_r.wdeps())
            pa0_r.note_w(t_bcr)
            srow_r.note_r(t_bcr)
            avs, avs_r = av_sb
            t_ev = p.act(avs[:, :], avt[:, 0:512], AF.Copy,
                         deps=[t_av] + avs_r.wdeps())
            avs_r.note_w(t_ev)
            av_r.note_r(t_ev)
            t_mul = p.tt(attn[:, h * 1024 + qp * 512: h * 1024 + qp * 512 + 512],
                         avs[:, :], pa0[:, 0:512], ALU.mult,
                         deps=[t_ev, t_bcr] + attn_r.rdeps())
            avs_r.note_r(t_mul)
            pa0_r.note_r(t_mul)
            attn_r.note_r(t_mul)
    attn_r.note_w(t_mul)

    r1_store_toks = []

    def evac_o(mi, m, it, tt, pa, deps):
        eb, eb_r = r1tile[it % 2]
        tok = p.ts(eb[:, 0:tt], pa, bov(m), None, ALU.add, ALU.bypass,
                   deps=deps + eb_r.wdeps())
        eb_r.note_w(tok)
        rp, rp_r = rptile[it % 2]
        lr = p.dma(rp[:, 0:tt],
                   r1p_in[m * 128:(m + 1) * 128, it * 512:it * 512 + tt],
                   deps=rp_r.wdeps(), eng="sync")
        rp_r.note_w(lr)
        t2 = p.tt(eb[:, 0:tt], eb[:, 0:tt], rp[:, 0:tt], ALU.add,
                  deps=[tok, lr])
        eb_r.note_w(t2)
        rp_r.note_r(t2)
        d = p.dma(r1_out[m * 128:(m + 1) * 128, it * 512:it * 512 + tt],
                  eb[:, 0:tt], deps=[t2], eng="sync")
        eb_r.note_r(d)
        r1_store_toks.append(d)
        return t2

    emit_mm_block(p, C, None, woT, list(range(16)), attn, attn_r, 0, 1024,
                  wstage, None, evac_o)

    emit_ln(p, C, r1_out, 0, 1024, xn1, xn1_res, 0, xs=xs, xs_res=xs_res,
            src_deps=r1_store_toks[-2:])
    for c in range(NCH):
        d = p.dma(xn1_out[c * 128:(c + 1) * 128, :],
                  xn1[:, c * 1024:(c + 1) * 1024], deps=xn1_res.rdeps(),
                  eng="sync")
        xn1_res.note_r(d)

    def evac_v(mi, m, it, tt, pa, deps):
        eb, eb_r = ev_bf[it % 2]
        tok = p.act(eb[:, 0:tt], pa, AF.Copy, deps=deps + eb_r.wdeps())
        eb_r.note_w(tok)
        cv = p.copyv(v_sb[:, m * 1024 + it * 512: m * 1024 + it * 512 + tt],
                     eb[:, 0:tt], deps=[tok] + v_sb_r.rdeps())
        eb_r.note_r(cv)
        v_sb_r.note_r(cv)
        d = p.dma(v_out[m * 128:(m + 1) * 128, it * 512:it * 512 + tt],
                  eb[:, 0:tt], deps=[tok], eng="sync")
        eb_r.note_r(d)
        return cv

    emit_mm_block(p, C, None, wv1, list(range(16)), xn1, xn1_res, 0, 1024,
                  wstage, None, evac_v)
    v_sb_r.note_w(v_sb_r.reads[-1] if v_sb_r.reads else None)

    vhead[0][1].reads.extend(res_chain(h_res).reads)
    t_lam = p.act(lam_row[0:1, 0:16], beta_sb[0:1, 0:16], AF.Sigmoid,
                  deps=C["ct"] + lam_row_r.wdeps())
    lam_row_r.note_w(t_lam)
    pa0, pa0_r = P["acc"][0]
    for h in range(NH):
        t_bc = p.mm(pa0[:, 0:1], C["ones_row_f"][0:1, 0:128],
                    lam_row[0:1, h:h + 1], start=True, stop=True,
                    deps=[t_lam] + pa0_r.wdeps())
        pa0_r.note_w(t_bc)
        lam_row_r.note_r(t_bc)
        t_lc = p.copyv(lamcol[:, 0:1], pa0[:, 0:1],
                       deps=[t_bc] + lamcol_r.wdeps())
        lamcol_r.note_w(t_lc)
        pa0_r.note_r(t_lc)
        t_lb = p.ts(lam_big[:, :], ones_big[:, :], lamcol[:, 0:1], None,
                    ALU.mult, ALU.bypass,
                    deps=[t_lc, ct2] + lam_big_r.wdeps())
        lam_big_r.note_w(t_lb)
        lamcol_r.note_r(t_lb)
        t_scan = p.scan(st_sc[:, :], lam_big[:, :],
                        v_sb[:, h * 1024:(h + 1) * 1024], 0.0,
                        deps=[t_lb] + v_sb_r.rdeps() + st_sc_r.wdeps())
        st_sc_r.note_w(t_scan)
        lam_big_r.note_r(t_scan)
        v_sb_r.note_r(t_scan)
        t_cp = p.copyv(carry_sb[:, h:h + 1], st_sc[:, 1023:1024],
                       deps=[t_scan] + carry_sb_r.rdeps())
        st_sc_r.note_r(t_cp)
        carry_sb_r.note_r(t_cp)
    carry_sb_r.note_w(t_cp)
    carry_sb_r.reads = []
    trt, trt_r = P["tr32"]
    t_tr = p.trans(trt[0:16, 0:128], carry_sb[:, 0:16], C["idn_f"][:, :],
                   deps=[t_cp] + trt_r.wdeps())
    trt_r.note_w(t_tr)
    carry_sb_r.note_r(t_tr)
    t_ce = p.copyv(carry_ev[0:16, :], trt[0:16, 0:128],
                   deps=[t_tr] + carry_ev_r.wdeps())
    carry_ev_r.note_w(t_ce)
    trt_r.note_r(t_ce)
    d = p.dma(carry_out, carry_ev[0:16, :], deps=[t_ce], eng="sync")
    carry_ev_r.note_r(d)

    return p.build()


def run_launch2(inputs, l1res):
    x = np.asarray(inputs["L0_attn_x"])
    nc = build_launch2()
    w = {
        "woT": np.ascontiguousarray(inputs["L0_attn_w_o"].T),
        "bo_r": np.ascontiguousarray(inputs["L0_attn_b_o"].reshape(16, 128).T),
        "wv1": np.ascontiguousarray(inputs["L1_attn_Wv"]),
        "beta_in": np.ascontiguousarray(inputs["L1_attn_beta"].reshape(1, 16)),
        "c_ones_col": np.ones((128, 1), np.float32),
        "c_idn": np.eye(128, dtype=np.float32),
    }
    kf = {}
    vf = {}
    for b in range(2):
        kf[b] = np.concatenate([l1res[4 * b + g]["k_out"] for g in range(4)],
                               axis=1)
        vf[b] = np.concatenate([l1res[4 * b + g]["vt_out"] for g in range(4)],
                               axis=0)
    in_maps = []
    for c in range(8):
        b, g = c // 4, c % 4
        m = {"q_in": l1res[c]["q_out"], "k_full": kf[b], "vt_full": vf[b],
             "r1p_in": l1res[c]["r1p_out"]}
        m.update(w)
        in_maps.append(m)
    res = run_bass_kernel_spmd(nc, in_maps, core_ids=list(range(8)))
    return res.results





HSH = 8192
NE = 8


def build_launch3():
    p = Prog()
    r1_in = p.inp("r1_in", [D, 1024], F32)
    xn1_in = p.inp("xn1_in", [D, 1024], BF16)
    v_in = p.inp("v_in", [D, 1024], BF16)
    car_in = p.inp("car_in", [3, 16, 128], F32)
    wq1 = p.inp("wq1", [D, D], F32)
    wg1 = p.inp("wg1", [D, D], F32)
    wo1 = p.inp("wo1", [D, D], F32)
    beta_row = p.inp("beta_row", [1, 16], F32)
    beta_col = p.inp("beta_col", [16, 1], F32)
    lng_r = p.inp("lng_r", [128, 16], F32)
    lnb_r = p.inp("lnb_r", [128, 16], F32)
    w12T = p.inp("w12T", [NE, D, 2048], F32)
    w3T = p.inp("w3T", [NE, 1024, D], F32)
    routerT = p.inp("routerT", [D, 8], F32)
    ebias_col = p.inp("ebias_col", [8, 1], F32)
    sw1T = p.inp("sw1T", [D, HSH], F32)
    sw2T = p.inp("sw2T", [D, HSH], F32)
    sw3T = p.inp("sw3T", [HSH, D], F32)

    out_sl = p.outp("out_sl", [D, 1024], F32)
    comb_dram = p.dram("comb_dram", [8, 1024], F32)

    C = {}
    C["ones_col_f"] = p.sbuf("ones_col_f", [128, 1], F32)
    C["ones_col_bf"] = p.sbuf("ones_col_bf", [128, 1], BF16)
    C["ones_row_f"] = p.sbuf("ones_row_f", [1, 128], F32)
    C["idn_f"] = p.sbuf("idn_f", [128, 128], F32)
    co = p.inp("c_ones_col", [128, 1], F32)
    cidn = p.inp("c_idn", [128, 128], F32)
    ct = [p.dma(C["ones_col_f"][:, :], co, eng="sync"),
          p.dma(C["ones_col_bf"][:, :], co, eng="gpsimd"),
          p.dma(C["ones_row_f"][0:1, :], co.rearrange("p one -> one p"),
                eng="sync"),
          p.dma(C["idn_f"][:, :], cidn, eng="sync")]
    gbias = p.sbuf("gbias", [128, 32], F32)
    ct.append(p.dma(gbias[:, 0:16], lng_r, eng="sync"))
    ct.append(p.dma(gbias[:, 16:32], lnb_r, eng="sync"))
    gcol = lambda c: gbias[:, c:c + 1]
    bcol = lambda c: gbias[:, 16 + c:16 + c + 1]
    beta_sb = p.sbuf("beta_sb", [1, 16], F32)
    ct.append(p.dma(beta_sb[0:1, :], beta_row, eng="sync"))
    betac_sb = p.sbuf("betac_sb", [16, 1], F32)
    ct.append(p.dma(betac_sb[0:16, :], beta_col, eng="sync"))
    ebias_sb = p.sbuf("ebias_sb", [8, 1], F32)
    ct.append(p.dma(ebias_sb[0:8, :], ebias_col, eng="sync"))
    C["eps1"] = p.sbuf("eps1", [1, 1], F32)
    ct.append(p.memset(C["eps1"][0:1, 0:1], EPS))
    C["ct"] = ct
    for e in ENGINES:
        p.op(e, lambda eng: eng.nop(), deps=ct)

    P = {}
    P["acc"] = [(p.psum("accA", [128, 512], F32), Res()),
                (p.psum("accB", [128, 512], F32), Res())]
    P["bcA"] = (p.psum("bcA", [128, 512], F32), Res())
    P["bcB"] = (p.psum("bcB", [128, 512], F32), Res())
    P["sum"] = (p.psum("psumrow", [1, 512], F32), Res())
    P["sq"] = (p.psum("psqrow", [1, 512], F32), Res())
    P["tr"] = (p.psum("ptr", [128, 128], F32), Res())
    C["P"] = P

    xn1 = p.sbuf("xn1", [128, NCH * 1024], BF16)
    xn1_res = Res()
    vyn = p.sbuf("vyn", [128, NCH * 1024], BF16)
    vyn_res = Res()
    yz = p.sbuf("yz", [128, NCH * 1024], BF16)
    yz_res = Res()
    acc_rt = p.sbuf("acc_rt", [128, NCH * 512], F32)
    acc_rt_res = Res()
    C["sq_tile"] = (p.sbuf("sq_tile", [128, 512], F32), Res())
    C["ln_tmp"] = (p.sbuf("ln_tmp", [128, 512], F32), Res())
    rows = p.sbuf("rows", [1, 4 * 512], F32)
    C["rows"] = {nm: (rows[0:1, i * 512:(i + 1) * 512], Res())
                 for i, nm in enumerate(("mean", "var", "a", "b"))}
    wstage = [(p.sbuf("wsA", [128, NCH * 128], BF16), Res()),
              (p.sbuf("wsB", [128, NCH * 128], BF16), Res())]
    w3stage = [(p.sbuf("w3A", [128, 8 * 128], BF16), Res()),
               (p.sbuf("w3B", [128, 8 * 128], BF16), Res())]
    _sw3 = (p.sbuf("sw3A", [128, 32 * 128], BF16), Res())
    sw3stage = [_sw3, _sw3]
    g_t = (p.sbuf("g_t", [128, 512], F32), Res())
    u_t = (p.sbuf("u_t", [128, 512], F32), Res())
    o_t = [(p.sbuf("o_tA", [128, 512], F32), Res()),
           (p.sbuf("o_tB", [128, 512], F32), Res())]
    rtile = [(p.sbuf("rtA", [128, 512], F32), Res()),
             (p.sbuf("rtB", [128, 512], F32), Res())]
    hrt = p.sbuf("hrt", [128, 8 * 512], BF16)
    hrt_res = Res()
    st_sc = [(p.sbuf("st0", [128, 1024], F32), Res()),
             (p.sbuf("st1", [128, 1024], F32), Res())]
    lam_big = acc_rt[:, 1024:2048]
    lam_big_r = Res()
    ones_big = acc_rt[:, 0:1024]
    ct2 = p.memset(ones_big[:, :], 1.0)
    lam_row = p.sbuf("lam_row", [1, 16], F32)
    lam_row_r = Res()
    lamcol = p.sbuf("lamcol", [128, 1], F32)
    lamcol_r = Res()
    car_sb = p.sbuf("car_sb", [16, 3 * 128], F32)
    car_r = Res()
    lamc = p.sbuf("lamc", [16, 4], F32)
    lamc_r = Res()
    fac = p.sbuf("fac", [16, 3], F32)
    fac_r = Res()
    sinT = p.sbuf("sinT", [16, 128], F32)
    sinT_r = Res()
    s_in = p.sbuf("s_in", [128, 16], F32)
    s_in_r = Res()
    scr8 = A3[0:8, 0:1024]
    scr8_r = Res()
    sel8 = A3[0:8, 1024:2048]
    sel8_r = Res()
    selT = p.sbuf("selT", [128, 8], F32)
    selT_r = Res()
    scrT = p.sbuf("scrT", [128, 8], F32)
    scrT_r = Res()
    m8 = p.sbuf("m8", [128, 8], F32)
    m8_r = Res()
    msk = p.sbuf("msk", [128, 24], F32)
    msk_r = Res()
    s12 = p.sbuf("s12", [128, 2], F32)
    s12_r = Res()
    combT = p.sbuf("combT", [128, 8], F32)
    combT_r = Res()
    comb8 = A3[0:8, 2048:3072]
    comb8_r = Res()
    comb_row = p.sbuf("comb_row", [1, 512], F32)
    comb_row_r = Res()

    lds = []
    for c in range(NCH):
        lds.append(p.dma(xn1[:, c * 1024:(c + 1) * 1024],
                         xn1_in[c * 128:(c + 1) * 128, :], eng="sync"))
        lds.append(p.dma(vyn[:, c * 1024:(c + 1) * 1024],
                         v_in[c * 128:(c + 1) * 128, :], eng="sync"))
    xn1_res.note_w(lds[-2])
    vyn_res.note_w(lds[-1])

    lcar = p.dma(car_sb[0:16, :].rearrange("p (j dh) -> p j dh", dh=128),
                 car_in.rearrange("j p dh -> p j dh"), eng="sync")
    car_r.note_w(lcar)
    t_lamc = p.act(lamc[0:16, 0:1], betac_sb[0:16, 0:1], AF.Sigmoid,
                   deps=C["ct"] + lamc_r.wdeps())
    t_lnl = p.act(lamc[0:16, 1:2], lamc[0:16, 0:1], AF.Ln, deps=[t_lamc])
    lamc_r.note_w(t_lnl)
    facto = []
    for j in range(3):
        tok = p.act(fac[0:16, j:j + 1], lamc[0:16, 1:2], AF.Exp,
                    scale=float(1024 * (j + 1)),
                    deps=[t_lnl] + fac_r.wdeps())
        lamc_r.note_r(tok)
        facto.append(tok)
    fac_r.note_w(facto[-1])
    t_s0 = p.ts(sinT[0:16, :], car_sb[0:16, 0:128], fac[0:16, 0:1], None,
                ALU.mult, ALU.bypass,
                deps=[lcar, facto[0]] + sinT_r.wdeps())
    sinT_r.note_w(t_s0)
    car_r.note_r(t_s0)
    fac_r.note_r(t_s0)
    for j in range(1, 3):
        t_s0 = p.stt(sinT[0:16, :], car_sb[0:16, j * 128:(j + 1) * 128],
                     fac[0:16, j:j + 1], sinT[0:16, :], ALU.mult, ALU.add,
                     deps=[lcar, facto[j], t_s0])
        sinT_r.note_w(t_s0)
        car_r.note_r(t_s0)
        fac_r.note_r(t_s0)
    trt, trt_r = P["tr"]
    t_tr = p.trans(trt[0:128, 0:16], sinT[0:16, :], C["idn_f"][0:16, 0:16],
                   deps=[t_s0] + trt_r.wdeps() + C["ct"])
    trt_r.note_w(t_tr)
    sinT_r.note_r(t_tr)
    t_si = p.copyv(s_in[:, 0:16], trt[0:128, 0:16],
                   deps=[t_tr] + s_in_r.wdeps())
    s_in_r.note_w(t_si)
    trt_r.note_r(t_si)

    vhead[0][1].reads.extend(res_chain(h_res).reads)
    t_lam = p.act(lam_row[0:1, 0:16], beta_sb[0:1, 0:16], AF.Sigmoid,
                  deps=C["ct"] + lam_row_r.wdeps())
    lam_row_r.note_w(t_lam)

    for h in range(NH):
        pa0, pa0_r = P["bcA"]
        t_bc = p.mm(pa0[:, 0:1], C["ones_row_f"][0:1, 0:128],
                    lam_row[0:1, h:h + 1], start=True, stop=True,
                    deps=[t_lam] + pa0_r.wdeps())
        pa0_r.note_w(t_bc)
        lam_row_r.note_r(t_bc)
        t_lc = p.copyv(lamcol[:, 0:1], pa0[:, 0:1],
                       deps=[t_bc] + lamcol_r.wdeps())
        lamcol_r.note_w(t_lc)
        pa0_r.note_r(t_lc)
        t_lb = p.ts(lam_big[:, :], ones_big[:, :], lamcol[:, 0:1], None,
                    ALU.mult, ALU.bypass,
                    deps=[t_lc, ct2] + lam_big_r.wdeps())
        lam_big_r.note_w(t_lb)
        lamcol_r.note_r(t_lb)
        st, st_r = st_sc[h % 2]
        t_scan = p.scan(st[:, :], lam_big[:, :],
                        vyn[:, h * 1024:(h + 1) * 1024], s_in[:, h:h + 1],
                        deps=[t_lb, t_si] + vyn_res.rdeps() + st_r.wdeps())
        st_r.note_w(t_scan)
        lam_big_r.note_r(t_scan)
        vyn_res.note_r(t_scan)
        s_in_r.note_r(t_scan)
        ws, ws_r = wstage[h % 2]
        lw = p.dma(ws[:, 0:NCH * 128].rearrange("p (c m) -> p c m", m=128),
                   wq1[:, h * 128:(h + 1) * 128]
                   .rearrange("(c p) m -> p c m", p=128),
                   deps=ws_r.wdeps(), eng="gpsimd")
        ws_r.note_w(lw)
        for it in range(2):
            pq, pq_r = P["acc"][it % 2]
            for k in range(NCH):
                tok = p.mm(pq[:, 0:512], ws[:, k * 128:(k + 1) * 128],
                           xn1[:, k * 1024 + it * 512:k * 1024 + it * 512 + 512],
                           start=(k == 0), stop=(k == NCH - 1),
                           deps=[lw] + xn1_res.rdeps()
                           + (pq_r.wdeps() if k == 0 else []))
                ws_r.note_r(tok)
                xn1_res.note_r(tok)
            pq_r.note_w(tok)
            t_y = p.tt(yz[:, h * 1024 + it * 512: h * 1024 + it * 512 + 512],
                       pq[:, 0:512], st[:, it * 512:(it + 1) * 512], ALU.mult,
                       deps=[tok, t_scan] + yz_res.rdeps())
            pq_r.note_r(t_y)
            st_r.note_r(t_y)
            yz_res.note_r(t_y)
    y_done = t_y
    yz_res.note_w(t_y)

    vyn_res2 = Res()
    vyn_res2.reads = list(vyn_res.reads)
    emit_ln(p, C, None, 0, 1024, vyn, vyn_res2, 0,
            src_view=lambda c, it, tt: yz[:, c * 1024 + it * 512:
                                          c * 1024 + it * 512 + tt],
            src_deps=[y_done], gb=(gcol, bcol))

    for h in range(NCH):
        ws, ws_r = wstage[h % 2]
        lw = p.dma(ws[:, 0:NCH * 128].rearrange("p (c m) -> p c m", m=128),
                   wg1[:, h * 128:(h + 1) * 128]
                   .rearrange("(c p) m -> p c m", p=128),
                   deps=ws_r.wdeps(), eng="gpsimd")
        ws_r.note_w(lw)
        for it in range(2):
            pq, pq_r = P["acc"][it % 2]
            for k in range(NCH):
                tok = p.mm(pq[:, 0:512], ws[:, k * 128:(k + 1) * 128],
                           xn1[:, k * 1024 + it * 512:k * 1024 + it * 512 + 512],
                           start=(k == 0), stop=(k == NCH - 1),
                           deps=[lw] + xn1_res.rdeps()
                           + (pq_r.wdeps() if k == 0 else []))
                ws_r.note_r(tok)
                xn1_res.note_r(tok)
            pq_r.note_w(tok)
            gt, gt_r = g_t
            t_g = p.act(gt[:, :], pq[:, 0:512], AF.Silu,
                        deps=[tok] + gt_r.wdeps())
            gt_r.note_w(t_g)
            pq_r.note_r(t_g)
            t_z = p.tt(yz[:, h * 1024 + it * 512: h * 1024 + it * 512 + 512],
                       gt[:, :],
                       vyn[:, h * 1024 + it * 512: h * 1024 + it * 512 + 512],
                       ALU.mult,
                       deps=[t_g] + vyn_res2.rdeps() + yz_res.wdeps())
            gt_r.note_r(t_z)
            vyn_res2.note_r(t_z)
            yz_res.note_r(t_z)
    z_done = t_z
    yz_res.note_w(t_z)

    out_res = [[Res() for _ in range(2)] for _ in range(NCH)]

    def evac_o(mi, m, it, tt, pa, deps):
        ot, ot_r = o_t[it % 2]
        tok = p.act(ot[:, 0:tt], pa, AF.Copy, deps=deps + ot_r.wdeps())
        ot_r.note_w(tok)
        rp, rp_r = rtile[it % 2]
        lr = p.dma(rp[:, 0:tt],
                   r1_in[m * 128:(m + 1) * 128, it * 512:it * 512 + tt],
                   deps=rp_r.wdeps(), eng="sync")
        rp_r.note_w(lr)
        t2 = p.tt(ot[:, 0:tt], ot[:, 0:tt], rp[:, 0:tt], ALU.add,
                  deps=[tok, lr])
        ot_r.note_w(t2)
        rp_r.note_r(t2)
        d = p.dma(out_sl[m * 128:(m + 1) * 128, it * 512:it * 512 + tt],
                  ot[:, 0:tt], deps=[t2] + out_res[m][it].wdeps(), eng="sync")
        ot_r.note_r(d)
        out_res[m][it].note_w(d)
        return t2

    emit_mm_block(p, C, None, wo1, list(range(16)), yz, yz_res, 0, 1024,
                  wstage, None, evac_o)

    pa, pa_r = P["acc"][0]
    ws, ws_r = wstage[0]
    lw = p.dma(ws[:, 0:NCH * 8].rearrange("p (c m) -> p c m", m=8),
               routerT.rearrange("(c p) m -> p c m", p=128),
               deps=ws_r.wdeps(), eng="gpsimd")
    ws_r.note_w(lw)
    for it in range(2):
        for k in range(NCH):
            tok = p.mm(pa[0:8, 0:512], ws[:, k * 8:(k + 1) * 8],
                       xn1[:, k * 1024 + it * 512:k * 1024 + it * 512 + 512],
                       start=(k == 0), stop=(k == NCH - 1),
                       deps=[lw] + xn1_res.rdeps()
                       + (pa_r.wdeps() if k == 0 else []))
            ws_r.note_r(tok)
            xn1_res.note_r(tok)
        pa_r.note_w(tok)
        t_sc = p.act(scr8[0:8, it * 512:(it + 1) * 512], pa[0:8, 0:512],
                     AF.Sigmoid, deps=[tok] + scr8_r.wdeps())
        scr8_r.note_w(t_sc)
        pa_r.note_r(t_sc)
    t_sel = p.ts(sel8[0:8, :], scr8[0:8, :], ebias_sb[0:8, 0:1], None,
                 ALU.add, ALU.bypass,
                 deps=scr8_r.rdeps() + C["ct"] + sel8_r.wdeps())
    sel8_r.note_w(t_sel)
    scr8_r.note_r(t_sel)
    trt, trt_r = P["tr"]
    for ctk in range(8):
        t_t1 = p.trans(trt[0:128, 0:8], sel8[0:8, ctk * 128:(ctk + 1) * 128],
                       C["idn_f"][0:8, 0:8], deps=[t_sel] + trt_r.wdeps())
        trt_r.note_w(t_t1)
        sel8_r.note_r(t_t1)
        t_c1 = p.copyv(selT[:, 0:8], trt[0:128, 0:8],
                       deps=[t_t1] + selT_r.wdeps())
        selT_r.note_w(t_c1)
        trt_r.note_r(t_c1)
        t_t2 = p.trans(trt[0:128, 0:8], scr8[0:8, ctk * 128:(ctk + 1) * 128],
                       C["idn_f"][0:8, 0:8],
                       deps=scr8_r.rdeps() + trt_r.wdeps())
        trt_r.note_w(t_t2)
        scr8_r.note_r(t_t2)
        t_c2 = p.copyv(scrT[:, 0:8], trt[0:128, 0:8],
                       deps=[t_t2] + scrT_r.wdeps())
        scrT_r.note_w(t_c2)
        trt_r.note_r(t_c2)
        t_m8 = p.max8(m8[:, 0:8], selT[:, 0:8], deps=[t_c1] + m8_r.wdeps())
        m8_r.note_w(t_m8)
        selT_r.note_r(t_m8)
        t_k1 = p.ts(msk[:, 0:8], selT[:, 0:8], m8[:, 0:1], None,
                    ALU.is_equal, ALU.bypass, deps=[t_m8, t_c1] + msk_r.wdeps())
        t_k2 = p.ts(msk[:, 8:16], selT[:, 0:8], m8[:, 1:2], None,
                    ALU.is_equal, ALU.bypass, deps=[t_m8, t_c1, t_k1])
        t_k3 = p.tt(msk[:, 0:8], msk[:, 0:8], msk[:, 8:16], ALU.add,
                    deps=[t_k1, t_k2])
        t_num = p.tt(msk[:, 16:24], msk[:, 0:8], scrT[:, 0:8], ALU.mult,
                     deps=[t_k3, t_c2])
        msk_r.note_w(t_num)
        m8_r.note_r(t_k2)
        selT_r.note_r(t_k2)
        scrT_r.note_r(t_num)
        t_s12 = p.red(s12[:, 0:1], msk[:, 16:24], ALU.add,
                      deps=[t_num] + s12_r.wdeps())
        t_d = p.ts(s12[:, 1:2], s12[:, 0:1], 1e-9, None, ALU.add, ALU.bypass,
                   deps=[t_s12])
        t_rc = p.recip(s12[:, 1:2], s12[:, 1:2], deps=[t_d])
        s12_r.note_w(t_rc)
        msk_r.note_r(t_s12)
        t_cmb = p.ts(combT[:, 0:8], msk[:, 16:24], s12[:, 1:2], None,
                     ALU.mult, ALU.bypass,
                     deps=[t_rc, t_num] + combT_r.wdeps())
        combT_r.note_w(t_cmb)
        s12_r.note_r(t_cmb)
        msk_r.note_r(t_cmb)
        t_t3 = p.trans(trt[0:8, 0:128], combT[:, 0:8], C["idn_f"][:, :],
                       deps=[t_cmb] + trt_r.wdeps())
        trt_r.note_w(t_t3)
        combT_r.note_r(t_t3)
        t_c3 = p.copyv(comb8[0:8, ctk * 128:(ctk + 1) * 128], trt[0:8, 0:128],
                       deps=[t_t3] + comb8_r.rdeps())
        comb8_r.note_r(t_c3)
        trt_r.note_r(t_c3)
    comb8_r.note_w(t_c3)
    d1 = p.dma(comb_dram, comb8[0:8, :], deps=comb8_r.rdeps(), eng="sync")
    comb8_r.note_r(d1)
    comb_done = d1

    hsh_res = Res()
    hsh_res.reads = list(yz_res.reads)
    NTT = 4
    for it in range(NTT):
        for m in range(HSH // 128):
            wsg, wsg_r = wstage[m % 2]
            lg = p.dma(wsg[:, 0:NCH * 128].rearrange("p (c m) -> p c m", m=128),
                       sw1T[:, m * 128:(m + 1) * 128]
                       .rearrange("(c p) m -> p c m", p=128),
                       deps=wsg_r.wdeps(), eng="gpsimd")
            wsg_r.note_w(lg)
            wsu, wsu_r = w3stage[m % 2]
            wsu, wsu_r = sw3stage[m % 2]
            lu = p.dma(wsu[:, 0:NCH * 128].rearrange("p (c m) -> p c m", m=128),
                       sw2T[:, m * 128:(m + 1) * 128]
                       .rearrange("(c p) m -> p c m", p=128),
                       deps=wsu_r.wdeps(), eng="gpsimd")
            wsu_r.note_w(lu)
            pa, pa_r = P["acc"][0]
            pb, pb_r = P["acc"][1]
            for k in range(NCH):
                tok = p.mm(pa[:, 0:256], wsg[:, k * 128:(k + 1) * 128],
                           xn1[:, k * 1024 + it * 256:k * 1024 + it * 256 + 256],
                           start=(k == 0), stop=(k == NCH - 1),
                           deps=[lg] + xn1_res.rdeps()
                           + (pa_r.wdeps() if k == 0 else []))
                wsg_r.note_r(tok)
                xn1_res.note_r(tok)
            pa_r.note_w(tok)
            tg = tok
            for k in range(NCH):
                tok = p.mm(pb[:, 0:256], wsu[:, k * 128:(k + 1) * 128],
                           xn1[:, k * 1024 + it * 256:k * 1024 + it * 256 + 256],
                           start=(k == 0), stop=(k == NCH - 1),
                           deps=[lu] + xn1_res.rdeps()
                           + (pb_r.wdeps() if k == 0 else []))
                wsu_r.note_r(tok)
                xn1_res.note_r(tok)
            pb_r.note_w(tok)
            tu = tok
            gt, gt_r = g_t
            ut, ut_r = u_t
            t1 = p.act(gt[:, 0:256], pa[:, 0:256], AF.Silu,
                       deps=[tg] + gt_r.wdeps())
            gt_r.note_w(t1)
            pa_r.note_r(t1)
            t2 = p.act(ut[:, 0:256], pb[:, 0:256], AF.Copy,
                       deps=[tu] + ut_r.wdeps())
            ut_r.note_w(t2)
            pb_r.note_r(t2)
            t3 = p.tt(yz[:, m * 256:(m + 1) * 256], gt[:, 0:256],
                      ut[:, 0:256], ALU.mult,
                      deps=[t1, t2] + hsh_res.wdeps())
            gt_r.note_r(t3)
            ut_r.note_r(t3)
            hsh_res.note_r(t3)
        hsh_res.note_w(t3)
        hsh_res.reads = []
        h_last = t3
        for m2 in range(NCH):
            wd, wd_r = sw3stage[m2 % 2]
            lw3 = p.dma(wd[:, 0:32 * 128]
                        .rearrange("p (c m) -> p c m", m=128),
                        sw3T[0:HSH // 2, m2 * 128:(m2 + 1) * 128]
                        .rearrange("(c p) m -> p c m", p=128),
                        deps=wd_r.wdeps(), eng="gpsimd")
            wd_r.note_w(lw3)
            pa, pa_r = P["acc"][m2 % 2]
            for k in range(32):
                tok = p.mm(pa[:, 0:256], wd[:, k * 128:(k + 1) * 128],
                           yz[:, k * 256:(k + 1) * 256],
                           start=(k == 0), stop=False,
                           deps=[lw3, h_last]
                           + (pa_r.wdeps() if k == 0 else []))
                wd_r.note_r(tok)
                hsh_res.note_r(tok)
            lw3b = p.dma(wd[:, 0:32 * 128]
                         .rearrange("p (c m) -> p c m", m=128),
                         sw3T[HSH // 2:HSH, m2 * 128:(m2 + 1) * 128]
                         .rearrange("(c p) m -> p c m", p=128),
                         deps=wd_r.wdeps(), eng="gpsimd")
            wd_r.note_w(lw3b)
            for k in range(32):
                tok = p.mm(pa[:, 0:256], wd[:, k * 128:(k + 1) * 128],
                           yz[:, (32 + k) * 256:(33 + k) * 256],
                           start=False, stop=(k == 31),
                           deps=[lw3b, h_last])
                wd_r.note_r(tok)
                hsh_res.note_r(tok)
            pa_r.note_w(tok)
            ot, ot_r = o_t[m2 % 2]
            rp, rp_r = rtile[m2 % 2]
            lr = p.dma(rp[:, 0:256],
                       out_sl[m2 * 128:(m2 + 1) * 128,
                              it * 256:(it + 1) * 256],
                       deps=rp_r.wdeps() + out_res[m2][it // 2].rdeps(),
                       eng="sync")
            rp_r.note_w(lr)
            t4 = p.tt(ot[:, 0:256], rp[:, 0:256], pa[:, 0:256], ALU.add,
                      deps=[lr, tok] + ot_r.wdeps())
            ot_r.note_w(t4)
            rp_r.note_r(t4)
            pa_r.note_r(t4)
            d = p.dma(out_sl[m2 * 128:(m2 + 1) * 128,
                             it * 256:(it + 1) * 256], ot[:, 0:256],
                      deps=[t4] + out_res[m2][it // 2].wdeps(), eng="sync")
            ot_r.note_r(d)
            out_res[m2][it // 2].note_w(d)

    for it in range(2):
        zt = p.memset(acc_rt[:, :], 0.0,
                      deps=acc_rt_res.wdeps() + [y_done])
        acc_rt_res.note_w(zt)
        for e in range(NE):
            lcr = p.dma(comb_row[0:1, 0:512],
                        comb_dram[e:e + 1, it * 512:(it + 1) * 512],
                        deps=[comb_done] + comb_row_r.wdeps(), eng="sync")
            comb_row_r.note_w(lcr)
            pc, pc_r = P["bcB"]
            t_cb = p.mm(pc[:, 0:512], C["ones_row_f"][0:1, 0:128],
                        comb_row[0:1, 0:512],
                        start=True, stop=True,
                        deps=[lcr] + pc_r.wdeps())
            pc_r.note_w(t_cb)
            comb_row_r.note_r(t_cb)
            for m in range(8):
                wsg, wsg_r = wstage[m % 2]
                lg = p.dma(wsg[:, 0:NCH * 128]
                           .rearrange("p (c m) -> p c m", m=128),
                           w12T[e, :, m * 128:(m + 1) * 128]
                           .rearrange("(c p) m -> p c m", p=128),
                           deps=wsg_r.wdeps(), eng="gpsimd")
                wsg_r.note_w(lg)
                wsu, wsu_r = sw3stage[m % 2]
                lu = p.dma(wsu[:, 0:NCH * 128]
                           .rearrange("p (c m) -> p c m", m=128),
                           w12T[e, :, 1024 + m * 128:1024 + (m + 1) * 128]
                           .rearrange("(c p) m -> p c m", p=128),
                           deps=wsu_r.wdeps(), eng="gpsimd")
                wsu_r.note_w(lu)
                pa, pa_r = P["acc"][0]
                pb, pb_r = P["acc"][1]
                for k in range(NCH):
                    tok = p.mm(pa[:, 0:512], wsg[:, k * 128:(k + 1) * 128],
                               xn1[:, k * 1024 + it * 512:
                                   k * 1024 + it * 512 + 512],
                               start=(k == 0), stop=(k == NCH - 1),
                               deps=[lg] + xn1_res.rdeps()
                               + (pa_r.wdeps() if k == 0 else []))
                    wsg_r.note_r(tok)
                    xn1_res.note_r(tok)
                pa_r.note_w(tok)
                tg = tok
                for k in range(NCH):
                    tok = p.mm(pb[:, 0:512], wsu[:, k * 128:(k + 1) * 128],
                               xn1[:, k * 1024 + it * 512:
                                   k * 1024 + it * 512 + 512],
                               start=(k == 0), stop=(k == NCH - 1),
                               deps=[lu] + xn1_res.rdeps()
                               + (pb_r.wdeps() if k == 0 else []))
                    wsu_r.note_r(tok)
                    xn1_res.note_r(tok)
                pb_r.note_w(tok)
                tu = tok
                gt, gt_r = g_t
                ut, ut_r = u_t
                t1 = p.act(gt[:, :], pa[:, 0:512], AF.Silu,
                           deps=[tg] + gt_r.wdeps())
                gt_r.note_w(t1)
                pa_r.note_r(t1)
                t2 = p.act(ut[:, :], pb[:, 0:512], AF.Copy,
                           deps=[tu] + ut_r.wdeps())
                ut_r.note_w(t2)
                pb_r.note_r(t2)
                t3 = p.tt(gt[:, :], gt[:, :], ut[:, :], ALU.mult,
                          deps=[t1, t2])
                gt_r.note_w(t3)
                ut_r.note_r(t3)
                t4 = p.tt(hrt[:, m * 512:(m + 1) * 512], gt[:, :],
                          pc[:, 0:512], ALU.mult,
                          deps=[t3, t_cb] + hrt_res.wdeps())
                gt_r.note_r(t4)
                pc_r.note_r(t4)
                hrt_res.note_r(t4)
            hrt_res.note_w(t4)
            hrt_res.reads = []
            hrt_last = t4
            for m2 in range(NCH):
                wd, wd_r = w3stage[m2 % 2]
                lw3 = p.dma(wd[:, 0:8 * 128]
                            .rearrange("p (c m) -> p c m", m=128),
                            w3T[e, :, m2 * 128:(m2 + 1) * 128]
                            .rearrange("(c p) m -> p c m", p=128),
                            deps=wd_r.wdeps(), eng="gpsimd")
                wd_r.note_w(lw3)
                pa, pa_r = P["acc"][m2 % 2]
                for k in range(8):
                    tok = p.mm(pa[:, 0:512], wd[:, k * 128:(k + 1) * 128],
                               hrt[:, k * 512:(k + 1) * 512],
                               start=(k == 0), stop=(k == 7),
                               deps=[lw3, hrt_last]
                               + (pa_r.wdeps() if k == 0 else []))
                    wd_r.note_r(tok)
                    hrt_res.note_r(tok)
                pa_r.note_w(tok)
                t5 = p.tt(acc_rt[:, m2 * 512:(m2 + 1) * 512],
                          acc_rt[:, m2 * 512:(m2 + 1) * 512], pa[:, 0:512],
                          ALU.add, deps=[tok, zt] + acc_rt_res.wdeps())
                acc_rt_res.note_r(t5)
                pa_r.note_r(t5)
        acc_rt_res.note_w(t5)
        for m2 in range(NCH):
            ot, ot_r = o_t[m2 % 2]
            rp, rp_r = rtile[m2 % 2]
            lr = p.dma(rp[:, 0:512],
                       out_sl[m2 * 128:(m2 + 1) * 128,
                              it * 512:(it + 1) * 512],
                       deps=rp_r.wdeps() + out_res[m2][it].rdeps(),
                       eng="sync")
            rp_r.note_w(lr)
            t6 = p.tt(ot[:, 0:512], rp[:, 0:512],
                      acc_rt[:, m2 * 512:(m2 + 1) * 512], ALU.add,
                      deps=[lr, t5] + ot_r.wdeps())
            ot_r.note_w(t6)
            rp_r.note_r(t6)
            acc_rt_res.note_r(t6)
            d = p.dma(out_sl[m2 * 128:(m2 + 1) * 128,
                             it * 512:(it + 1) * 512], ot[:, 0:512],
                      deps=[t6] + out_res[m2][it].wdeps(), eng="sync")
            ot_r.note_r(d)
            out_res[m2][it].note_w(d)

    return p.build()


def run_launch3(inputs, l2res):
    nc = build_launch3()
    w = {
        "wq1": np.ascontiguousarray(inputs["L1_attn_Wq"]),
        "wg1": np.ascontiguousarray(inputs["L1_attn_Wg"]),
        "wo1": np.ascontiguousarray(inputs["L1_attn_Wo"]),
        "beta_row": np.ascontiguousarray(
            inputs["L1_attn_beta"].reshape(1, 16)),
        "beta_col": np.ascontiguousarray(
            inputs["L1_attn_beta"].reshape(16, 1)),
        "lng_r": np.ascontiguousarray(
            inputs["L1_attn_ln_g"].reshape(16, 128).T),
        "lnb_r": np.ascontiguousarray(
            inputs["L1_attn_ln_b"].reshape(16, 128).T),
        "w12T": np.ascontiguousarray(inputs["L1_ffn_w12"].transpose(0, 2, 1)),
        "w3i": np.ascontiguousarray(
            inputs["L1_ffn_w3"].transpose(0, 2, 1).reshape(NE * 1024, D)),
        "routerT": np.ascontiguousarray(inputs["L1_ffn_router_w"].T),
        "ebias_col": np.ascontiguousarray(
            inputs["L1_ffn_expert_bias"].reshape(8, 1)),
        "sw1T": np.ascontiguousarray(inputs["L1_ffn_sw1"].T),
        "sw2T": np.ascontiguousarray(inputs["L1_ffn_sw2"].T),
        "sw3T": np.ascontiguousarray(inputs["L1_ffn_sw3"].T),
        "c_ones_col": np.ones((128, 1), np.float32),
        "c_idn": np.eye(128, dtype=np.float32),
    }
    in_maps = []
    for c in range(8):
        b, g = c // 4, c % 4
        car = np.zeros((3, 16, 128), np.float32)
        for j in range(3):
            gp = g - 1 - j
            if gp >= 0:
                car[j] = l2res[4 * b + gp]["carry_out"]
        m = {"r1_in": l2res[c]["r1_out"], "xn1_in": l2res[c]["xn1_out"],
             "v_in": l2res[c]["v_out"], "car_in": car}
        m.update(w)
        in_maps.append(m)
    res = run_bass_kernel_spmd(nc, in_maps, core_ids=list(range(8)))
    return res.results


def kernel(**inputs):
    inputs = {k: np.asarray(v) for k, v in inputs.items()}
    l1 = run_launch1(inputs)
    l2 = run_launch2(inputs, l1)
    l3 = run_launch3(inputs, l2)
    outs = []
    for b in range(2):
        outs.append(np.concatenate(
            [l3[4 * b + g]["out_sl"] for g in range(4)], axis=1))
    out_fm = np.stack(outs, axis=0)
    return np.ascontiguousarray(out_fm.transpose(0, 2, 1))





HT = 2048


def res_chain(*olds):
    r = Res()
    for o in olds:
        if o is None:
            continue
        if o.w is not None:
            r.reads.append(o.w)
        r.reads.extend(o.reads)
    return r


def build_fused():
    p = Prog()
    x_half = p.inp("x_half", [D, HT], F32)
    parcol = p.inp("parcol", [16, 1], F32)
    nonce = p.inp("nonce", [1, 4], I32)
    wqkvT = p.inp("wqkvT", [D, 3 * D], F32)
    bqkv_r = p.inp("bqkv_r", [128, 48], F32)
    wguT = p.inp("wguT", [D, 2 * INTER], F32)
    wdT = p.inp("wdT", [INTER, D], F32)
    gs_r = p.inp("gs_r", [128, 44], F32)
    us_r = p.inp("us_r", [128, 44], F32)
    ds_r = p.inp("ds_r", [128, 16], F32)
    woT = p.inp("woT", [D, D], F32)
    bo_r = p.inp("bo_r", [128, 16], F32)
    wv1 = p.inp("wv1", [D, D], F32)
    wq1 = p.inp("wq1", [D, D], F32)
    wg1 = p.inp("wg1", [D, D], F32)
    wo1 = p.inp("wo1", [D, D], F32)
    beta_row = p.inp("beta_row", [1, 16], F32)
    lng_r = p.inp("lng_r", [128, 16], F32)
    lnb_r = p.inp("lnb_r", [128, 16], F32)
    w12T = p.inp("w12T", [8, D, 2048], F32)
    w3i = p.inp("w3i", [NE * 1024, D], F32)
    routerT = p.inp("routerT", [D, 8], F32)
    ebias_col = p.inp("ebias_col", [8, 1], F32)
    swguT = p.inp("swguT", [D, 2 * HSH], F32)
    sw3T = p.inp("sw3T", [HSH, D], F32)
    co = p.inp("c_ones_col", [128, 1], F32)
    cidn = p.inp("c_idn", [128, 128], F32)

    out_sl = p.outp("out_sl", [D, 1024], F32)

    K_sh = p.dram("K_sh", [D, S], BF16, addr_space="Shared")
    Vt_sh = p.dram("Vt_sh", [S, D], BF16, addr_space="Shared")
    car_sh = p.dram("car_sh", [2, 16, 128], F32, addr_space="Shared")
    kvflag = p.dram("kvflag", [2, 128], I32, addr_space="Shared")
    carflag = p.dram("carflag", [2, 128], I32, addr_space="Shared")
    q_dram = p.dram("q_dram", [D, HT], BF16)
    r1_dram = p.dram("r1_dram", [D, HT], F32)
    xn1_dram = p.dram("xn1_dram", [D, HT], BF16)
    v_dram = p.dram("v_dram", [D, HT], BF16)
    comb_dram = p.dram("comb_dram", [8, 1024], F32)

    C = {}
    C["ones_col_f"] = p.sbuf("ones_col_f", [128, 1], F32)
    C["ones_col_bf"] = p.sbuf("ones_col_bf", [128, 1], BF16)
    C["ones_row_f"] = p.sbuf("ones_row_f", [1, 128], F32)
    C["idn_f"] = p.sbuf("idn_f", [128, 128], F32)
    C["idn_bf"] = p.sbuf("idn_bf", [128, 128], BF16)
    ct = [p.dma(C["ones_col_f"][:, :], co, eng="sync"),
          p.dma(C["ones_col_bf"][:, :], co, eng="gpsimd"),
          p.dma(C["ones_row_f"][0:1, :], co.rearrange("p one -> one p"),
                eng="sync"),
          p.dma(C["idn_f"][:, :], cidn, eng="sync"),
          p.dma(C["idn_bf"][:, :], cidn, eng="gpsimd")]
    biases = p.sbuf("biases", [128, 200], F32)
    boff = {}
    _o = 0
    for nm, ap_, w_ in (("bqkv", bqkv_r, 48), ("gs", gs_r, 44),
                        ("us", us_r, 44), ("ds", ds_r, 16),
                        ("bo", bo_r, 16), ("lng", lng_r, 16),
                        ("lnb", lnb_r, 16)):
        ct.append(p.dma(biases[:, _o:_o + w_], ap_, eng="sync"))
        boff[nm] = _o
        _o += w_
    bview = lambda nm, m: biases[:, boff[nm] + m: boff[nm] + m + 1]
    beta_sb = p.sbuf("beta_sb", [1, 16], F32)
    ct.append(p.dma(beta_sb[0:1, :], beta_row, eng="sync"))
    parc_sb = p.sbuf("parc_sb", [16, 1], F32)
    ct.append(p.dma(parc_sb[0:16, :], parcol, eng="sync"))
    ebias_sb = p.sbuf("ebias_sb", [8, 1], F32)
    ct.append(p.dma(ebias_sb[0:8, :], ebias_col, eng="sync"))
    C["eps1"] = p.sbuf("eps1", [1, 1], F32)
    ct.append(p.memset(C["eps1"][0:1, 0:1], EPS))
    C["ct"] = ct
    for e in ENGINES:
        p.op(e, lambda eng: eng.nop(), deps=ct)

    P = {}
    P["acc"] = [(p.psum("accA", [128, 512], F32), Res()),
                (p.psum("accB", [128, 512], F32), Res())]
    P["bcA"] = (p.psum("bcA", [128, 512], F32), Res())
    P["bcB"] = (p.psum("bcB", [128, 512], F32), Res())
    P["sum"] = (p.psum("psumrow", [1, 512], F32), Res())
    P["sq"] = (p.psum("psqrow", [1, 512], F32), Res())
    P["tr"] = (p.psum("ptr", [128, 128], F32), Res())
    P["trbf"] = (p.psum("ptrbf", [128, 512], BF16), Res())
    C["P"] = P

    A1 = p.sbuf("A1", [128, 32768], BF16)
    A2 = p.sbuf("A2", [128, 22528], BF16)
    A3 = p.sbuf("A3", [128, 8192], F32)
    A3bf = A3.bitcast(BF16)
    wgustage = [(A3bf[:, 4096:8192], Res()), (A3bf[:, 8192:12288], Res()),
                (A3bf[:, 12288:16384], Res())]
    _stc = (p.sbuf("stcA", [128, 1024], F32), Res())
    stc = [_stc, _stc]
    C["sq_tile"] = (p.sbuf("sq_tile", [128, 512], F32), Res())
    C["ln_tmp"] = (p.sbuf("ln_tmp", [128, 512], F32), Res())
    rows = p.sbuf("rows", [1, 4 * 512], F32)
    C["rows"] = {nm: (rows[0:1, i * 512:(i + 1) * 512], Res())
                 for i, nm in enumerate(("mean", "var", "a", "b"))}
    wstage = [(p.sbuf("wsA", [128, NCH * 128], BF16), Res()),
              (p.sbuf("wsB", [128, NCH * 128], BF16), Res())]
    _big = p.sbuf("bigA", [128, 44 * 128], BF16)
    bigstage = (_big, Res())
    bigust = [(_big[:, 0:2048], Res()), (_big[:, 2048:4096], Res())]
    bighalf = [(_big[:, 0:2816], Res()), (_big[:, 2816:5632], Res())]
    w3stage = [(p.sbuf("w3A", [128, 8 * 128], BF16), Res()),
               (p.sbuf("w3B", [128, 8 * 128], BF16), Res())]
    ev_bf = [(p.sbuf("evbfA", [128, 512], BF16), Res()),
             (p.sbuf("evbfB", [128, 512], BF16), Res())]
    vt_ev = [(p.sbuf("vtevA", [128, 128], BF16), Res()),
             (p.sbuf("vtevB", [128, 128], BF16), Res())]
    g_t = (p.sbuf("g_t", [128, 512], F32), Res())
    u_t = (p.sbuf("u_t", [128, 512], F32), Res())
    o_t = [(p.sbuf("o_tA", [128, 512], F32), Res()),
           (p.sbuf("o_tB", [128, 512], F32), Res())]
    rtile = [(p.sbuf("rtA", [128, 512], F32), Res()),
             (p.sbuf("rtB", [128, 512], F32), Res())]
    _xre = (p.sbuf("xreA", [128, 512], F32), Res())
    xre = [_xre, _xre]
    sqq = p.sbuf("sqq", [128, 512], BF16)
    sqq_r = Res()
    probs = [(p.sbuf("probsA", [128, 512], BF16), Res()),
             (p.sbuf("probsB", [128, 512], BF16), Res())]
    av_sb = (A3[:, 1536:2048], Res())
    rowbuf = A3[0:1, 0:1024]
    rowbuf_r = Res()
    tiny = p.sbuf("tiny", [1, 16], F32)
    tiny_r = Res()
    cbias = p.sbuf("cbias", [128, 1], F32)
    cbias_r = Res()
    srow = A3[0:1, 1024:1536]
    srow_r = Res()
    lam_row = p.sbuf("lam_row", [1, 16], F32)
    lam_row_r = Res()
    lamcol = p.sbuf("lamcol", [128, 1], F32)
    lamcol_r = Res()
    carry_sb = p.sbuf("carry_sb", [128, 16], F32)
    carry_sb_r = Res()
    carry_ev = p.sbuf("carry_ev", [16, 128], F32)
    carry_ev_r = Res()
    car0 = p.sbuf("car0", [16, 128], F32)
    car0_r = Res()
    sinT = p.sbuf("sinT", [16, 128], F32)
    sinT_r = Res()
    s_in = p.sbuf("s_in", [128, 16], F32)
    s_in_r = Res()
    scr8 = A3[0:8, 0:1024]
    scr8_r = Res()
    sel8 = A3[0:8, 1024:2048]
    sel8_r = Res()
    selT = p.sbuf("selT", [128, 8], F32)
    selT_r = Res()
    scrT = p.sbuf("scrT", [128, 8], F32)
    scrT_r = Res()
    m8 = p.sbuf("m8", [128, 8], F32)
    m8_r = Res()
    msk = p.sbuf("msk", [128, 24], F32)
    msk_r = Res()
    s12 = p.sbuf("s12", [128, 2], F32)
    s12_r = Res()
    combT = p.sbuf("combT", [128, 8], F32)
    combT_r = Res()
    comb8 = A3[0:8, 2048:3072]
    comb8_r = Res()
    comb_row = p.sbuf("comb_row", [1, 512], F32)
    comb_row_r = Res()
    _vh = (A2[:, 20480:22528], Res())
    vhead = [_vh, _vh]

    SC = 1.0 / np.sqrt(128.0)

    xn0_res = Res()
    xs_res = Res()
    emit_ln(p, C, x_half, 0, HT, A1, xn0_res, 0,
            xs=A3[:, 0:8192], xs_res=xs_res)

    kv_store_toks = []

    def evac_qkv(mi, m, it, tt, pa, deps):
        eb, eb_r = ev_bf[it % 2]
        tok = p.ts(eb[:, 0:tt], pa, bview("bqkv", m), None, ALU.add,
                   ALU.bypass, deps=deps + eb_r.wdeps())
        eb_r.note_w(tok)
        if m < 16:
            d = p.dma(q_dram[m * 128:(m + 1) * 128, it * 512:it * 512 + tt],
                      eb[:, 0:tt], deps=[tok], eng="sync")
            eb_r.note_r(d)
            kv_store_toks.append(d)
        elif m < 32:
            mm = m - 16
            d = p.dma2(K_sh[mm * 128:(mm + 1) * 128, it * 512:it * 512 + tt],
                       eb[:, 0:tt],
                       K_sh[mm * 128:(mm + 1) * 128,
                            2048 + it * 512:2048 + it * 512 + tt],
                       eb[:, 0:tt], deps=[tok], eng="sync")
            eb_r.note_r(d)
            kv_store_toks.append(d)
        else:
            mm = m - 32
            trp, trp_r = P["trbf"]
            for j in range(cdiv(tt, 128)):
                tj = p.trans(trp[:, j * 128:(j + 1) * 128],
                             eb[:, j * 128:(j + 1) * 128], C["idn_bf"][:, :],
                             deps=[tok] + trp_r.wdeps())
                eb_r.note_r(tj)
                trp_r.note_w(tj)
                ve, ve_r = vt_ev[j % 2]
                cv = p.copyv(ve[:, :], trp[:, j * 128:(j + 1) * 128],
                             deps=[tj] + ve_r.wdeps())
                ve_r.note_w(cv)
                trp_r.note_r(cv)
                r0 = it * 512 + j * 128
                d = p.dma2(Vt_sh[r0:r0 + 128, mm * 128:(mm + 1) * 128],
                           ve[:, :],
                           Vt_sh[2048 + r0:2048 + r0 + 128,
                                 mm * 128:(mm + 1) * 128],
                           ve[:, :], deps=[cv], eng="sync")
                ve_r.note_r(d)
                kv_store_toks.append(d)
        return tok

    emit_mm_block(p, C, None, wqkvT, list(range(48)), A1, xn0_res, 0, HT,
                  wstage, None, evac_qkv, kchunk=HT)
    fkv = p.dma2(kvflag[0:1, 0:4], nonce, kvflag[1:2, 0:4], nonce,
                 deps=kv_store_toks, eng="sync", ch="flags")

    for _j in range(3):
        wgustage[_j][1].reads.extend(xs_res.wdeps())
    h_res = Res()
    r1w = [[Res() for _ in range(4)] for _ in range(NCH)]
    for it in range(4):
        for _j in range(2):
            bigust[_j][1].reads.extend(bighalf[0][1].wdeps())
            bigust[_j][1].reads.extend(bighalf[1][1].wdeps())
        for m in range(44):
            wsg, wsg_r = wgustage[m % 3]
            lg = p.dma(wsg[:, 0:NCH * 256].rearrange("p (c m) -> p c m", m=256),
                       wguT[:, m * 256:(m + 1) * 256]
                       .rearrange("(c p) m -> p c m", p=128),
                       deps=wsg_r.wdeps(), eng="gpsimd", ch=f"wgu{m % 3}")
            wsg_r.note_w(lg)
            lu = lg
            pa, pa_r = P["acc"][0]
            pb, pb_r = P["acc"][1]
            for k in range(NCH):
                tok = p.mm(pa[:, 0:512], wsg[:, k * 256:k * 256 + 128],
                           A1[:, k * HT + it * 512: k * HT + it * 512 + 512],
                           start=(k == 0), stop=(k == NCH - 1),
                           deps=[lg] + xn0_res.rdeps()
                           + (pa_r.wdeps() if k == 0 else []))
                wsg_r.note_r(tok)
                xn0_res.note_r(tok)
            pa_r.note_w(tok)
            tg = tok
            for k in range(NCH):
                tok = p.mm(pb[:, 0:512], wsg[:, k * 256 + 128:(k + 1) * 256],
                           A1[:, k * HT + it * 512: k * HT + it * 512 + 512],
                           start=(k == 0), stop=(k == NCH - 1),
                           deps=[lu] + xn0_res.rdeps()
                           + (pb_r.wdeps() if k == 0 else []))
                wsg_r.note_r(tok)
                xn0_res.note_r(tok)
            pb_r.note_w(tok)
            tu = tok
            gt, gt_r = g_t
            ut, ut_r = u_t
            t1 = p.act(gt[:, :], pa[:, 0:512], AF.Silu, scale=bview("gs", m),
                       deps=[tg] + gt_r.wdeps())
            gt_r.note_w(t1)
            pa_r.note_r(t1)
            t2 = p.act(ut[:, :], pb[:, 0:512], AF.Copy, scale=bview("us", m),
                       deps=[tu] + ut_r.wdeps())
            ut_r.note_w(t2)
            pb_r.note_r(t2)
            t3 = p.tt(A2[:, m * 512:(m + 1) * 512], gt[:, :], ut[:, :],
                      ALU.mult, deps=[t1, t2] + h_res.wdeps())
            gt_r.note_r(t3)
            ut_r.note_r(t3)
            h_res.note_r(t3)
        h_res.note_w(t3)
        h_res.reads = []
        h_last = t3
        for _j in range(2):
            bighalf[_j][1].reads.extend(bigust[0][1].wdeps())
            bighalf[_j][1].reads.extend(bigust[1][1].wdeps())
        for m2 in range(NCH):
            wda, wda_r = bighalf[0]
            wdb, wdb_r = bighalf[1]
            lwa = p.dma(wda[:, 0:22 * 128].rearrange("p (c m) -> p c m", m=128),
                        wdT[0:2816, m2 * 128:(m2 + 1) * 128]
                        .rearrange("(c p) m -> p c m", p=128),
                        deps=wda_r.wdeps(), eng="gpsimd", ch="wda")
            wda_r.note_w(lwa)
            lwb = p.dma(wdb[:, 0:22 * 128].rearrange("p (c m) -> p c m", m=128),
                        wdT[2816:5632, m2 * 128:(m2 + 1) * 128]
                        .rearrange("(c p) m -> p c m", p=128),
                        deps=wdb_r.wdeps(), eng="gpsimd", ch="wdb")
            wdb_r.note_w(lwb)
            pa, pa_r = P["acc"][m2 % 2]
            for k in range(44):
                if k < 22:
                    wd, wd_r, lw = wda, wda_r, lwa
                    kk = k
                else:
                    wd, wd_r, lw = wdb, wdb_r, lwb
                    kk = k - 22
                tok = p.mm(pa[:, 0:512], wd[:, kk * 128:(kk + 1) * 128],
                           A2[:, k * 512:(k + 1) * 512],
                           start=(k == 0), stop=(k == 43),
                           deps=[lw, h_last]
                           + (pa_r.wdeps() if k == 0 else []))
                wd_r.note_r(tok)
                h_res.note_r(tok)
            pa_r.note_w(tok)
            dt, dt_r = o_t[m2 % 2]
            t1 = p.act(dt[:, :], pa[:, 0:512], AF.Copy, scale=bview("ds", m2),
                       deps=[tok] + dt_r.wdeps())
            dt_r.note_w(t1)
            pa_r.note_r(t1)
            xr, xr_r = xre[m2 % 2]
            lx = p.dma(xr[:, :],
                       x_half[m2 * 128:(m2 + 1) * 128,
                              it * 512:(it + 1) * 512],
                       deps=xr_r.wdeps(), eng="sync")
            xr_r.note_w(lx)
            rt, rt_r = rtile[m2 % 2]
            t2 = p.tt(rt[:, :], dt[:, :], xr[:, :], ALU.add,
                      deps=[t1, lx] + rt_r.wdeps())
            rt_r.note_w(t2)
            dt_r.note_r(t2)
            xr_r.note_r(t2)
            d = p.dma(r1_dram[m2 * 128:(m2 + 1) * 128,
                              it * 512:(it + 1) * 512], rt[:, :],
                      deps=[t2] + r1w[m2][it].wdeps(), eng="sync")
            rt_r.note_r(d)
            r1w[m2][it].note_w(d)

    pk0 = p.poll(kvflag[0:1, 0:1], nonce[0:1, 0:1], deps=[fkv], eng="sync")
    pk1 = p.poll(kvflag[1:2, 0:1], nonce[0:1, 0:1], deps=[fkv], eng="sync")
    attn_res = res_chain(xn0_res)
    for _r in (av_sb[1], rowbuf_r, srow_r):
        _r.reads.extend(xs_res.wdeps())
    kv_arena = res_chain(h_res)
    kh_r = res_chain()
    vh_r = res_chain()
    qh_r = res_chain()
    kh_r.reads = list(kv_arena.reads)
    vh_r.reads = list(kv_arena.reads)
    qh_r.reads = list(kv_arena.reads)
    sgt, sg_r = P["sum"]
    kh = A2[:, 0:4096]
    vh = A2[:, 4096:8192]
    qh = A2[:, 8192:10240]
    for h in range(NH):
        lk = p.dma(kh, K_sh[h * 128:(h + 1) * 128, :],
                   deps=kh_r.wdeps() + [pk0, pk1], eng="sync", ch="kh")
        kh_r.note_w(lk)
        lv = p.dma(vh.rearrange("p (c dh) -> p c dh", dh=128),
                   Vt_sh[:, h * 128:(h + 1) * 128]
                   .rearrange("(c p) dh -> p c dh", p=128),
                   deps=vh_r.wdeps() + [pk0, pk1], eng="sync", ch="vh")
        vh_r.note_w(lv)
        lq = p.dma(qh, q_dram[h * 128:(h + 1) * 128, :],
                   deps=qh_r.wdeps(), eng="sync", ch="qh")
        qh_r.note_w(lq)
        def emit_norms(src_fn, npieces, tiny_off, base_deps):
            outs = []
            for piece in range(npieces):
                ap_, dps = src_fn(piece)
                t_sqp = p.act(sqq[:, 0:512], ap_, AF.Square,
                              deps=dps + base_deps + sqq_r.wdeps())
                sqq_r.note_w(t_sqp)
                tok = p.mm(sgt[0:1, 0:512], C["ones_col_bf"][0:128, 0:1],
                           sqq[:, 0:512], start=True, stop=True,
                           deps=[t_sqp] + sg_r.wdeps())
                sg_r.note_w(tok)
                sqq_r.note_r(tok)
                cv = p.copyv(rowbuf[0:1, (piece % 2) * 512:
                                    ((piece % 2) + 1) * 512],
                             sgt[0:1, 0:512], deps=[tok] + rowbuf_r.wdeps())
                sg_r.note_r(cv)
                rowbuf_r.note_w(cv)
                if piece % 2 == 1:
                    ti = tiny_off + piece // 2
                    t_m = p.red(tiny[0:1, ti:ti + 1], rowbuf[0:1, 0:1024],
                                ALU.max,
                                deps=rowbuf_r.rdeps() + tiny_r.wdeps())
                    rowbuf_r.note_r(t_m)
                    outs.append(t_m)
            return outs

        qms = emit_norms(lambda pc: (qh[:, pc * 512:(pc + 1) * 512], [lq]),
                         4, 0, [])
        for t in qms:
            qh_r.note_r(t)
        t_mq2 = p.red(tiny[0:1, 8:9], tiny[0:1, 0:2], ALU.max, deps=qms)
        kms = emit_norms(lambda pc: (kh[:, pc * 512:(pc + 1) * 512], [lk]),
                         8, 2, [t_mq2])
        for t in kms:
            kh_r.note_r(t)
        t_mk4 = p.red(tiny[0:1, 9:10], tiny[0:1, 2:6], ALU.max, deps=kms)
        t_c1 = p.tt(tiny[0:1, 10:11], tiny[0:1, 8:9], tiny[0:1, 9:10],
                    ALU.mult, deps=[t_mq2, t_mk4])
        t_c2 = p.act(tiny[0:1, 11:12], tiny[0:1, 10:11], AF.Sqrt,
                     scale=1.0 / 128, deps=[t_c1])
        t_c3 = p.ts(tiny[0:1, 11:12], tiny[0:1, 11:12], -1.0, None, ALU.mult,
                    ALU.bypass, deps=[t_c2])
        tiny_r.note_w(t_c3)
        trt, trt_r = P["tr"]
        t_bc = p.mm(trt[0:128, 0:1], C["ones_row_f"][0:1, 0:128],
                    tiny[0:1, 11:12], start=True, stop=True,
                    deps=[t_c3] + trt_r.wdeps())
        trt_r.note_w(t_bc)
        tiny_r.note_r(t_bc)
        t_cb = p.copyv(cbias[:, 0:1], trt[0:128, 0:1],
                       deps=[t_bc] + cbias_r.wdeps())
        cbias_r.note_w(t_cb)
        trt_r.note_r(t_cb)
        scores_banks = [P["bcA"], P["acc"][1]]
        avt, av_r = P["bcB"]
        for qp in range(4):
            pend = None
            for kc in range(33):
                if kc < 32:
                    spt, sp_r = scores_banks[kc % 2]
                    t_s = p.mm(spt[:, 0:512], kh[:, kc * 128:(kc + 1) * 128],
                               qh[:, qp * 512:(qp + 1) * 512],
                               start=True, stop=True,
                               deps=[lk, lq] + sp_r.wdeps())
                    sp_r.note_w(t_s)
                    kh_r.note_r(t_s)
                    qh_r.note_r(t_s)
                    pb, pb_r = probs[kc % 2]
                    t_e = p.act(pb[:, :], spt[:, 0:512], AF.Exp,
                                bias=cbias[:, 0:1], scale=SC,
                                deps=[t_s, t_cb] + pb_r.wdeps())
                    pb_r.note_w(t_e)
                    sp_r.note_r(t_e)
                    cbias_r.note_r(t_e)
                if pend is not None:
                    kp, t_ep, pbp, pbp_r = pend
                    t_av = p.mm(avt[:, 0:512], vh[:, kp * 128:(kp + 1) * 128],
                                pbp[:, :], start=(kp == 0), stop=(kp == 31),
                                deps=[lv, t_ep]
                                + (av_r.wdeps() if kp == 0 else []))
                    pbp_r.note_r(t_av)
                    vh_r.note_r(t_av)
                    t_sg = p.mm(sgt[0:1, 0:512], C["ones_col_bf"][0:128, 0:1],
                                pbp[:, :], start=(kp == 0), stop=(kp == 31),
                                deps=[t_ep] + (sg_r.wdeps() if kp == 0 else []))
                    pbp_r.note_r(t_sg)
                pend = (kc, t_e, pb, pb_r) if kc < 32 else None
            av_r.note_w(t_av)
            sg_r.note_w(t_sg)
            t_cps = p.copyv(srow[0:1, 0:512], sgt[0:1, 0:512],
                            deps=[t_sg] + srow_r.wdeps())
            sg_r.note_r(t_cps)
            t_r = p.recip(srow[0:1, 0:512], srow[0:1, 0:512], deps=[t_cps])
            srow_r.note_w(t_r)
            pa0, pa0_r = P["acc"][0]
            t_bcr = p.mm(pa0[:, 0:512], C["ones_row_f"][0:1, 0:128],
                         srow[0:1, 0:512], start=True, stop=True,
                         deps=[t_r] + pa0_r.wdeps())
            pa0_r.note_w(t_bcr)
            srow_r.note_r(t_bcr)
            avs, avs_r = av_sb
            t_ev = p.act(avs[:, :], avt[:, 0:512], AF.Copy,
                         deps=[t_av] + avs_r.wdeps())
            avs_r.note_w(t_ev)
            av_r.note_r(t_ev)
            t_mul = p.tt(A1[:, h * HT + qp * 512: h * HT + qp * 512 + 512],
                         avs[:, :], pa0[:, 0:512], ALU.mult,
                         deps=[t_ev, t_bcr] + attn_res.wdeps())
            avs_r.note_r(t_mul)
            pa0_r.note_r(t_mul)
            attn_res.note_r(t_mul)
    attn_res.note_w(t_mul)
    attn_res.reads = []
    xs_res.note_r(t_mul)
    for _r in (av_sb[1], rowbuf_r, srow_r):
        for _t in _r.wdeps():
            xs_res.note_r(_t)

    def evac_o(mi, m, it, tt, pa, deps):
        ot, ot_r = o_t[it % 2]
        tok = p.ts(ot[:, 0:tt], pa, bview("bo", m), None, ALU.add, ALU.bypass,
                   deps=deps + ot_r.wdeps())
        ot_r.note_w(tok)
        rp, rp_r = rtile[it % 2]
        lr = p.dma(rp[:, 0:tt],
                   r1_dram[m * 128:(m + 1) * 128, it * 512:it * 512 + tt],
                   deps=rp_r.wdeps() + r1w[m][it].rdeps(), eng="sync")
        rp_r.note_w(lr)
        t2 = p.tt(ot[:, 0:tt], ot[:, 0:tt], rp[:, 0:tt], ALU.add,
                  deps=[tok, lr])
        ot_r.note_w(t2)
        rp_r.note_r(t2)
        d = p.dma(r1_dram[m * 128:(m + 1) * 128, it * 512:it * 512 + tt],
                  ot[:, 0:tt], deps=[t2] + r1w[m][it].wdeps(), eng="sync")
        ot_r.note_r(d)
        r1w[m][it].note_w(d)
        return t2

    emit_mm_block(p, C, None, woT, list(range(16)), A1, attn_res, 0, HT,
                  wstage, None, evac_o, kchunk=HT)

    xn1h_res = res_chain(attn_res)
    r1_deps = [r1w[m][it].rdeps()[0] for m in range(NCH) for it in range(4)]
    emit_ln(p, C, r1_dram, 0, HT, A1, xn1h_res, 0,
            xs=A3[:, 0:8192], xs_res=xs_res, src_deps=r1_deps)
    for c in range(NCH):
        d = p.dma(xn1_dram[c * 128:(c + 1) * 128, :],
                  A1[:, c * HT:(c + 1) * HT], deps=xn1h_res.rdeps(),
                  eng="sync", ch="xn1st")
        xn1h_res.note_r(d)

    t_lam = p.act(lam_row[0:1, 0:16], beta_sb[0:1, 0:16], AF.Sigmoid,
                  deps=C["ct"] + lam_row_r.wdeps())
    lam_row_r.note_w(t_lam)
    ones_big = A3[:, 0:2048]
    lam_big = A3[:, 2048:4096]
    a3_res = res_chain(xs_res, C["sq_tile"][1], wgustage[0][1],
                       wgustage[1][1])
    st_half_pre = wgustage[2][1]
    ct2 = p.memset(ones_big, 1.0, deps=a3_res.wdeps())
    a3_res.note_w(ct2)
    lam_big_r = Res()
    st_half = [(A3[:, 4096:6144], Res()), (A3[:, 6144:8192], Res())]
    st_half[0][1].reads = [ct2]
    st_half[1][1].reads = [ct2] + st_half_pre.wdeps()

    def lam_prep(h):
        trt, trt_r = P["tr"]
        t_bc = p.mm(trt[0:128, 0:1], C["ones_row_f"][0:1, 0:128],
                    lam_row[0:1, h:h + 1], start=True, stop=True,
                    deps=[t_lam] + trt_r.wdeps())
        trt_r.note_w(t_bc)
        lam_row_r.note_r(t_bc)
        t_lc = p.copyv(lamcol[:, 0:1], trt[0:128, 0:1],
                       deps=[t_bc] + lamcol_r.wdeps())
        lamcol_r.note_w(t_lc)
        trt_r.note_r(t_lc)
        t_lb = p.ts(lam_big, ones_big, lamcol[:, 0:1], None,
                    ALU.mult, ALU.bypass,
                    deps=[t_lc, ct2] + lam_big_r.wdeps())
        lam_big_r.note_w(t_lb)
        lamcol_r.note_r(t_lb)
        return t_lb

    vres, vres_r = vhead[0]
    vres_r.reads.extend(res_chain(h_res).reads)

    def evac_v(mi, m, it, tt, pa, deps):
        tok = p.act(vres[:, it * 512:it * 512 + tt], pa, AF.Copy,
                    deps=deps + (vres_r.wdeps() if it == 0 else []))
        vres_r.note_w(tok)
        d = p.dma(v_dram[m * 128:(m + 1) * 128, it * 512:it * 512 + tt],
                  vres[:, it * 512:it * 512 + tt], deps=[tok], eng="sync",
                  ch="vst")
        vres_r.note_r(d)
        if it == 3:
            t_lb = lam_prep(m)
            st, st_r = st_half[m % 2]
            t_scan = p.scan(st, lam_big, vres[:, :], 0.0,
                            deps=[t_lb, tok] + st_r.wdeps())
            st_r.note_w(t_scan)
            lam_big_r.note_r(t_scan)
            vres_r.note_r(t_scan)
            t_cp = p.copyv(carry_sb[:, m:m + 1], st[:, HT - 1:HT],
                           deps=[t_scan] + carry_sb_r.rdeps())
            st_r.note_r(t_cp)
            carry_sb_r.note_r(t_cp)
            p._last_cp = t_cp
        return tok

    emit_mm_block(p, C, None, wv1, list(range(16)), A1, xn1h_res, 0, HT,
                  wstage, None, evac_v, kchunk=HT)
    t_cp = p._last_cp
    carry_sb_r.note_w(t_cp)
    carry_sb_r.reads = []
    trt, trt_r = P["tr"]
    t_tr = p.trans(trt[0:16, 0:128], carry_sb[:, 0:16], C["idn_f"][:, :],
                   deps=[t_cp] + trt_r.wdeps())
    trt_r.note_w(t_tr)
    carry_sb_r.note_r(t_tr)
    t_ce = p.copyv(carry_ev[0:16, :], trt[0:16, 0:128],
                   deps=[t_tr] + carry_ev_r.wdeps())
    carry_ev_r.note_w(t_ce)
    trt_r.note_r(t_ce)
    dcar = p.dma2(car_sh[0, :, :], carry_ev[0:16, :],
                  car_sh[1, :, :], carry_ev[0:16, :],
                  deps=[t_ce], eng="sync")
    carry_ev_r.note_r(dcar)
    fcar = p.dma2(carflag[0:1, 0:4], nonce, carflag[1:2, 0:4], nonce,
                  deps=[dcar], eng="sync", ch="flags")

    xn1_res = res_chain(xn1h_res)
    xn1st_last = xn1h_res.reads[-1] if xn1h_res.reads else None
    for c in range(NCH):
        d = p.dma2(A1[:, c * 1024:(c + 1) * 1024],
                   xn1_dram[c * 128:(c + 1) * 128, 0:1024],
                   A1[:, c * 1024:(c + 1) * 1024],
                   xn1_dram[c * 128:(c + 1) * 128, 1024:2048],
                   deps=xn1_res.wdeps() + [xn1st_last], eng="sync",
                   ch="xn1can", bit="off")
        xn1_res.note_w(d)
    xn1_can_last = d

    pcar = p.poll(carflag[0:1, 0:1], nonce[0:1, 0:1], deps=[fcar], eng="sync")
    lc0 = p.dma(car0[0:16, :], car_sh[0, :, :], deps=[pcar] + car0_r.wdeps(),
                eng="sync")
    car0_r.note_w(lc0)
    t_s0 = p.ts(sinT[0:16, :], car0[0:16, :], parc_sb[0:16, 0:1], None,
                ALU.mult, ALU.bypass,
                deps=[lc0] + C["ct"] + sinT_r.wdeps())
    sinT_r.note_w(t_s0)
    car0_r.note_r(t_s0)
    trt, trt_r = P["tr"]
    t_tr2 = p.trans(trt[0:128, 0:16], sinT[0:16, :], C["idn_f"][0:16, 0:16],
                    deps=[t_s0] + trt_r.wdeps())
    trt_r.note_w(t_tr2)
    sinT_r.note_r(t_tr2)
    t_si = p.copyv(s_in[:, 0:16], trt[0:128, 0:16],
                   deps=[t_tr2] + s_in_r.wdeps())
    s_in_r.note_w(t_si)
    trt_r.note_r(t_si)

    yz = A1[:, 16384:32768]
    yz_res = Res()
    for h in range(NH):
        t_lb = lam_prep(h)
        vhd, vhd_r = vhead[h % 2]
        lvh = p.dma(vhd[:, :], v_dram[h * 128:(h + 1) * 128, :],
                    deps=vhd_r.wdeps(), eng="sync", ch="vhead")
        vhd_r.note_w(lvh)
        st, st_r = st_half[h % 2]
        t_scan = p.scan(st, lam_big, vhd[:, :], s_in[:, h:h + 1],
                        deps=[t_lb, lvh, t_si] + st_r.wdeps())
        st_r.note_w(t_scan)
        lam_big_r.note_r(t_scan)
        vhd_r.note_r(t_scan)
        s_in_r.note_r(t_scan)
        sc, sc_r = stc[h % 2]
        dsc = p.dma2(sc[:, :], st[:, 0:1024], sc[:, :], st[:, 1024:2048],
                     deps=sc_r.wdeps() + [t_scan], eng="sync",
                     ch=f"stc{h % 2}", bit="off")
        sc_r.note_w(dsc)
        st_r.note_r(dsc)
        ws, ws_r = wstage[h % 2]
        lw = p.dma(ws[:, 0:NCH * 128].rearrange("p (c m) -> p c m", m=128),
                   wq1[:, h * 128:(h + 1) * 128]
                   .rearrange("(c p) m -> p c m", p=128),
                   deps=ws_r.wdeps(), eng="gpsimd")
        ws_r.note_w(lw)
        for it in range(2):
            pq, pq_r = P["acc"][it % 2]
            for k in range(NCH):
                tok = p.mm(pq[:, 0:512], ws[:, k * 128:(k + 1) * 128],
                           A1[:, k * 1024 + it * 512:k * 1024 + it * 512 + 512],
                           start=(k == 0), stop=(k == NCH - 1),
                           deps=[lw, xn1_can_last]
                           + (pq_r.wdeps() if k == 0 else []))
                ws_r.note_r(tok)
                xn1_res.note_r(tok)
            pq_r.note_w(tok)
            t_y = p.tt(yz[:, h * 1024 + it * 512: h * 1024 + it * 512 + 512],
                       pq[:, 0:512], sc[:, it * 512:(it + 1) * 512], ALU.mult,
                       deps=[tok, dsc] + yz_res.rdeps())
            pq_r.note_r(t_y)
            sc_r.note_r(t_y)
            yz_res.note_r(t_y)
    y_done = t_y
    yz_res.note_w(t_y)
    yz_res.reads = []

    yn_res = res_chain(kh_r, vh_r, qh_r)
    yn = A2[:, 0:16384]
    emit_ln(p, C, None, 0, 1024, A2, yn_res, 0,
            src_view=lambda c, it, tt: yz[:, c * 1024 + it * 512:
                                          c * 1024 + it * 512 + tt],
            src_deps=[y_done],
            gb=(lambda c: bview("lng", c), lambda c: bview("lnb", c)))

    for h in range(NCH):
        ws, ws_r = wstage[h % 2]
        lw = p.dma(ws[:, 0:NCH * 128].rearrange("p (c m) -> p c m", m=128),
                   wg1[:, h * 128:(h + 1) * 128]
                   .rearrange("(c p) m -> p c m", p=128),
                   deps=ws_r.wdeps(), eng="gpsimd")
        ws_r.note_w(lw)
        for it in range(2):
            pq, pq_r = P["acc"][it % 2]
            for k in range(NCH):
                tok = p.mm(pq[:, 0:512], ws[:, k * 128:(k + 1) * 128],
                           A1[:, k * 1024 + it * 512:k * 1024 + it * 512 + 512],
                           start=(k == 0), stop=(k == NCH - 1),
                           deps=[lw, xn1_can_last]
                           + (pq_r.wdeps() if k == 0 else []))
                ws_r.note_r(tok)
                xn1_res.note_r(tok)
            pq_r.note_w(tok)
            gt, gt_r = g_t
            t_g = p.act(gt[:, :], pq[:, 0:512], AF.Silu,
                        deps=[tok] + gt_r.wdeps())
            gt_r.note_w(t_g)
            pq_r.note_r(t_g)
            t_z = p.tt(yz[:, h * 1024 + it * 512: h * 1024 + it * 512 + 512],
                       gt[:, :],
                       yn[:, h * 1024 + it * 512: h * 1024 + it * 512 + 512],
                       ALU.mult,
                       deps=[t_g] + yn_res.rdeps() + yz_res.wdeps())
            gt_r.note_r(t_z)
            yn_res.note_r(t_z)
            yz_res.note_r(t_z)
    z_done = t_z
    yz_res.note_w(t_z)
    yz_res.reads = []

    out_res = [[Res() for _ in range(2)] for _ in range(NCH)]

    def evac_o1(mi, m, it, tt, pa, deps):
        ot, ot_r = o_t[it % 2]
        tok = p.act(ot[:, 0:tt], pa, AF.Copy, deps=deps + ot_r.wdeps())
        ot_r.note_w(tok)
        rp, rp_r = rtile[it % 2]
        lr = p.dma2(rp[:, 0:tt],
                    r1_dram[m * 128:(m + 1) * 128, it * 512:it * 512 + tt],
                    rp[:, 0:tt],
                    r1_dram[m * 128:(m + 1) * 128,
                            1024 + it * 512:1024 + it * 512 + tt],
                    deps=rp_r.wdeps() + r1w[m][it].rdeps()
                    + r1w[m][it + 2].rdeps(), eng="sync",
                    ch=f"rt{it % 2}", bit="off")
        rp_r.note_w(lr)
        t2 = p.tt(ot[:, 0:tt], ot[:, 0:tt], rp[:, 0:tt], ALU.add,
                  deps=[tok, lr])
        ot_r.note_w(t2)
        rp_r.note_r(t2)
        d = p.dma(out_sl[m * 128:(m + 1) * 128, it * 512:it * 512 + tt],
                  ot[:, 0:tt], deps=[t2] + out_res[m][it].wdeps(), eng="sync")
        ot_r.note_r(d)
        out_res[m][it].note_w(d)
        return t2

    emit_mm_block(p, C, None, wo1, list(range(16)), A1, yz_res, 16384, 1024,
                  wstage, None, evac_o1, kchunk=1024)

    for _r in (scr8_r, sel8_r, comb8_r):
        _r.reads.append(y_done)
        _r.reads.extend(a3_res.wdeps())
    pa, pa_r = P["acc"][0]
    ws, ws_r = wstage[0]
    lw = p.dma(ws[:, 0:NCH * 8].rearrange("p (c m) -> p c m", m=8),
               routerT.rearrange("(c p) m -> p c m", p=128),
               deps=ws_r.wdeps(), eng="gpsimd")
    ws_r.note_w(lw)
    for it in range(2):
        for k in range(NCH):
            tok = p.mm(pa[0:8, 0:512], ws[:, k * 8:(k + 1) * 8],
                       A1[:, k * 1024 + it * 512:k * 1024 + it * 512 + 512],
                       start=(k == 0), stop=(k == NCH - 1),
                       deps=[lw, xn1_can_last]
                       + (pa_r.wdeps() if k == 0 else []))
            ws_r.note_r(tok)
            xn1_res.note_r(tok)
        pa_r.note_w(tok)
        t_sc = p.act(scr8[0:8, it * 512:(it + 1) * 512], pa[0:8, 0:512],
                     AF.Sigmoid, deps=[tok] + scr8_r.wdeps())
        scr8_r.note_w(t_sc)
        pa_r.note_r(t_sc)
    t_sel = p.ts(sel8[0:8, :], scr8[0:8, :], ebias_sb[0:8, 0:1], None,
                 ALU.add, ALU.bypass,
                 deps=scr8_r.rdeps() + C["ct"] + sel8_r.wdeps())
    sel8_r.note_w(t_sel)
    scr8_r.note_r(t_sel)
    trt, trt_r = P["tr"]
    for ctk in range(8):
        t_t1 = p.trans(trt[0:128, 0:8], sel8[0:8, ctk * 128:(ctk + 1) * 128],
                       C["idn_f"][0:8, 0:8], deps=[t_sel] + trt_r.wdeps())
        trt_r.note_w(t_t1)
        sel8_r.note_r(t_t1)
        t_c1 = p.copyv(selT[:, 0:8], trt[0:128, 0:8],
                       deps=[t_t1] + selT_r.wdeps())
        selT_r.note_w(t_c1)
        trt_r.note_r(t_c1)
        t_t2 = p.trans(trt[0:128, 0:8], scr8[0:8, ctk * 128:(ctk + 1) * 128],
                       C["idn_f"][0:8, 0:8],
                       deps=scr8_r.rdeps() + trt_r.wdeps())
        trt_r.note_w(t_t2)
        scr8_r.note_r(t_t2)
        t_c2 = p.copyv(scrT[:, 0:8], trt[0:128, 0:8],
                       deps=[t_t2] + scrT_r.wdeps())
        scrT_r.note_w(t_c2)
        trt_r.note_r(t_c2)
        t_m8 = p.max8(m8[:, 0:8], selT[:, 0:8], deps=[t_c1] + m8_r.wdeps())
        m8_r.note_w(t_m8)
        selT_r.note_r(t_m8)
        t_k1 = p.ts(msk[:, 0:8], selT[:, 0:8], m8[:, 0:1], None,
                    ALU.is_equal, ALU.bypass,
                    deps=[t_m8, t_c1] + msk_r.wdeps())
        t_k2 = p.ts(msk[:, 8:16], selT[:, 0:8], m8[:, 1:2], None,
                    ALU.is_equal, ALU.bypass, deps=[t_m8, t_c1, t_k1])
        t_k3 = p.tt(msk[:, 0:8], msk[:, 0:8], msk[:, 8:16], ALU.add,
                    deps=[t_k1, t_k2])
        t_num = p.tt(msk[:, 16:24], msk[:, 0:8], scrT[:, 0:8], ALU.mult,
                     deps=[t_k3, t_c2])
        msk_r.note_w(t_num)
        m8_r.note_r(t_k2)
        selT_r.note_r(t_k2)
        scrT_r.note_r(t_num)
        t_s12 = p.red(s12[:, 0:1], msk[:, 16:24], ALU.add,
                      deps=[t_num] + s12_r.wdeps())
        t_d = p.ts(s12[:, 1:2], s12[:, 0:1], 1e-9, None, ALU.add, ALU.bypass,
                   deps=[t_s12])
        t_rc = p.recip(s12[:, 1:2], s12[:, 1:2], deps=[t_d])
        s12_r.note_w(t_rc)
        msk_r.note_r(t_s12)
        t_cmb = p.ts(combT[:, 0:8], msk[:, 16:24], s12[:, 1:2], None,
                     ALU.mult, ALU.bypass,
                     deps=[t_rc, t_num] + combT_r.wdeps())
        combT_r.note_w(t_cmb)
        s12_r.note_r(t_cmb)
        msk_r.note_r(t_cmb)
        t_t3 = p.trans(trt[0:8, 0:128], combT[:, 0:8], C["idn_f"][:, :],
                       deps=[t_cmb] + trt_r.wdeps())
        trt_r.note_w(t_t3)
        combT_r.note_r(t_t3)
        t_c3 = p.copyv(comb8[0:8, ctk * 128:(ctk + 1) * 128], trt[0:8, 0:128],
                       deps=[t_t3] + comb8_r.rdeps())
        comb8_r.note_r(t_c3)
        trt_r.note_r(t_c3)
    comb8_r.note_w(t_c3)
    comb_done = p.dma(comb_dram, comb8[0:8, :], deps=comb8_r.rdeps(),
                      eng="sync", ch="comb8")
    comb8_r.note_r(comb_done)

    for _j in range(2):
        bigust[_j][1].reads.extend(bighalf[0][1].wdeps())
        bigust[_j][1].reads.extend(bighalf[1][1].wdeps())
    for _j in range(3):
        wgustage[_j][1].reads.append(comb_done)
        wgustage[_j][1].reads.extend(st_half[0][1].wdeps())
        wgustage[_j][1].reads.extend(st_half[1][1].wdeps())
        wgustage[_j][1].reads.extend(lam_big_r.wdeps())
    hsh_res = res_chain(yn_res)
    hsh2_res = res_chain(yz_res)

    def hview(k):
        if k < 32:
            return A2[:, k * 512:(k + 1) * 512]
        return A1[:, 16384 + (k - 32) * 512:16384 + (k - 31) * 512]

    for it in range(2):
        for m in range(HSH // 128):
            wsg, wsg_r = wgustage[m % 3]
            lg = p.dma(wsg[:, 0:NCH * 256].rearrange("p (c m) -> p c m", m=256),
                       swguT[:, m * 256:(m + 1) * 256]
                       .rearrange("(c p) m -> p c m", p=128),
                       deps=wsg_r.wdeps(), eng="gpsimd", ch=f"wgu{m % 3}")
            wsg_r.note_w(lg)
            lu = lg
            pa, pa_r = P["acc"][0]
            pb, pb_r = P["bcA"]
            for k in range(NCH):
                tok = p.mm(pa[:, 0:512], wsg[:, k * 256:k * 256 + 128],
                           A1[:, k * 1024 + it * 512:k * 1024 + it * 512 + 512],
                           start=(k == 0), stop=(k == NCH - 1),
                           deps=[lg, xn1_can_last]
                           + (pa_r.wdeps() if k == 0 else []))
                wsg_r.note_r(tok)
                xn1_res.note_r(tok)
            pa_r.note_w(tok)
            tg = tok
            for k in range(NCH):
                tok = p.mm(pb[:, 0:512], wsg[:, k * 256 + 128:(k + 1) * 256],
                           A1[:, k * 1024 + it * 512:k * 1024 + it * 512 + 512],
                           start=(k == 0), stop=(k == NCH - 1),
                           deps=[lu, xn1_can_last]
                           + (pb_r.wdeps() if k == 0 else []))
                wsg_r.note_r(tok)
                xn1_res.note_r(tok)
            pb_r.note_w(tok)
            tu = tok
            gt, gt_r = g_t
            ut, ut_r = u_t
            t1 = p.act(gt[:, 0:512], pa[:, 0:512], AF.Silu,
                       deps=[tg] + gt_r.wdeps())
            gt_r.note_w(t1)
            pa_r.note_r(t1)
            t2 = p.act(ut[:, 0:512], pb[:, 0:512], AF.Copy,
                       deps=[tu] + ut_r.wdeps())
            ut_r.note_w(t2)
            pb_r.note_r(t2)
            hres_m = hsh_res if m < 32 else hsh2_res
            t3 = p.tt(hview(m), gt[:, 0:512], ut[:, 0:512], ALU.mult,
                      deps=[t1, t2] + hres_m.wdeps())
            gt_r.note_r(t3)
            ut_r.note_r(t3)
            hres_m.note_r(t3)
        hsh_res.note_w(t3)
        hsh_res.reads = []
        hsh2_res.note_w(t3)
        hsh2_res.reads = []
        h_last = t3
        for m2 in range(NCH):
            pa, pa_r = P["acc"][m2 % 2]
            for q in range(4):
                wq_, wq_r = wstage[q % 2]
                lw3 = p.dma(wq_[:, 0:16 * 128]
                            .rearrange("p (c m) -> p c m", m=128),
                            sw3T[q * 2048:(q + 1) * 2048,
                                 m2 * 128:(m2 + 1) * 128]
                            .rearrange("(c p) m -> p c m", p=128),
                            deps=wq_r.wdeps(), eng="gpsimd", ch=f"wsg{q % 2}")
                wq_r.note_w(lw3)
                for kq in range(16):
                    k = q * 16 + kq
                    tok = p.mm(pa[:, 0:512], wq_[:, kq * 128:(kq + 1) * 128],
                               hview(k),
                               start=(k == 0), stop=(k == 63),
                               deps=[lw3, h_last]
                               + (pa_r.wdeps() if k == 0 else []))
                    wq_r.note_r(tok)
                    (hsh_res if k < 32 else hsh2_res).note_r(tok)
            pa_r.note_w(tok)
            ot, ot_r = o_t[m2 % 2]
            rp, rp_r = rtile[m2 % 2]
            lr = p.dma(rp[:, 0:512],
                       out_sl[m2 * 128:(m2 + 1) * 128, it * 512:(it + 1) * 512],
                       deps=rp_r.wdeps() + out_res[m2][it].rdeps(),
                       eng="sync", ch=f"rt{m2 % 2}")
            rp_r.note_w(lr)
            t4 = p.tt(ot[:, 0:512], rp[:, 0:512], pa[:, 0:512], ALU.add,
                      deps=[lr, tok] + ot_r.wdeps())
            ot_r.note_w(t4)
            rp_r.note_r(t4)
            pa_r.note_r(t4)
            d = p.dma(out_sl[m2 * 128:(m2 + 1) * 128,
                             it * 512:(it + 1) * 512], ot[:, 0:512],
                      deps=[t4] + out_res[m2][it].wdeps(), eng="sync")
            ot_r.note_r(d)
            out_res[m2][it].note_w(d)

    hallA = res_chain(hsh_res)
    hallB = res_chain(hsh2_res)

    def hview2(k):
        if k < 32:
            return A2[:, k * 512:(k + 1) * 512]
        return A1[:, 16384 + (k - 32) * 512:16384 + (k - 31) * 512]

    for it in range(2):
        for e in range(NE):
            lcr = p.dma(comb_row[0:1, 0:512],
                        comb_dram[e:e + 1, it * 512:(it + 1) * 512],
                        deps=[comb_done] + comb_row_r.wdeps(), eng="sync")
            comb_row_r.note_w(lcr)
            pc, pc_r = P["bcB"]
            t_cb = p.mm(pc[:, 0:512], C["ones_row_f"][0:1, 0:128],
                        comb_row[0:1, 0:512], start=True, stop=True,
                        deps=[lcr] + pc_r.wdeps())
            pc_r.note_w(t_cb)
            comb_row_r.note_r(t_cb)
            for m in range(8):
                wsg, wsg_r = wstage[m % 2]
                lg = p.dma(wsg[:, 0:NCH * 128]
                           .rearrange("p (c m) -> p c m", m=128),
                           w12T[e, :, m * 128:(m + 1) * 128]
                           .rearrange("(c p) m -> p c m", p=128),
                           deps=wsg_r.wdeps(), eng="gpsimd")
                wsg_r.note_w(lg)
                wsu, wsu_r = bigust[m % 2]
                lu = p.dma(wsu[:, 0:NCH * 128]
                           .rearrange("p (c m) -> p c m", m=128),
                           w12T[e, :, 1024 + m * 128:1024 + (m + 1) * 128]
                           .rearrange("(c p) m -> p c m", p=128),
                           deps=wsu_r.wdeps(), eng="gpsimd", ch=f"wsu{m % 2}")
                wsu_r.note_w(lu)
                pa, pa_r = P["acc"][0]
                pb, pb_r = P["acc"][1]
                for k in range(NCH):
                    tok = p.mm(pa[:, 0:512], wsg[:, k * 128:(k + 1) * 128],
                               A1[:, k * 1024 + it * 512:
                                   k * 1024 + it * 512 + 512],
                               start=(k == 0), stop=(k == NCH - 1),
                               deps=[lg, xn1_can_last]
                               + (pa_r.wdeps() if k == 0 else []))
                    wsg_r.note_r(tok)
                    xn1_res.note_r(tok)
                pa_r.note_w(tok)
                tg = tok
                for k in range(NCH):
                    tok = p.mm(pb[:, 0:512], wsu[:, k * 128:(k + 1) * 128],
                               A1[:, k * 1024 + it * 512:
                                   k * 1024 + it * 512 + 512],
                               start=(k == 0), stop=(k == NCH - 1),
                               deps=[lu, xn1_can_last]
                               + (pb_r.wdeps() if k == 0 else []))
                    wsu_r.note_r(tok)
                    xn1_res.note_r(tok)
                pb_r.note_w(tok)
                tu = tok
                gt, gt_r = g_t
                ut, ut_r = u_t
                t1 = p.act(gt[:, :], pa[:, 0:512], AF.Silu,
                           deps=[tg] + gt_r.wdeps())
                gt_r.note_w(t1)
                pa_r.note_r(t1)
                t2 = p.act(ut[:, :], pb[:, 0:512], AF.Copy,
                           deps=[tu] + ut_r.wdeps())
                ut_r.note_w(t2)
                pb_r.note_r(t2)
                t3 = p.tt(gt[:, :], gt[:, :], ut[:, :], ALU.mult,
                          deps=[t1, t2])
                gt_r.note_w(t3)
                ut_r.note_r(t3)
                kk = e * 8 + m
                hres_k = hallA if kk < 32 else hallB
                t4 = p.tt(hview2(kk), gt[:, :],
                          pc[:, 0:512], ALU.mult,
                          deps=[t3, t_cb] + hres_k.wdeps())
                gt_r.note_r(t4)
                pc_r.note_r(t4)
                hres_k.note_r(t4)
        hallA.note_w(t4)
        hallA.reads = []
        hallB.note_w(t4)
        hallB.reads = []
        hall_last = t4
        for m2 in range(NCH):
            pa, pa_r = P["acc"][m2 % 2]
            for q in range(4):
                wq_, wq_r = wstage[q % 2]
                lw3 = p.dma(wq_[:, 0:16 * 128]
                            .rearrange("p (c m) -> p c m", m=128),
                            w3i[q * 2048:(q + 1) * 2048,
                                m2 * 128:(m2 + 1) * 128]
                            .rearrange("(c p) m -> p c m", p=128),
                            deps=wq_r.wdeps(), eng="gpsimd", ch=f"wsg{q % 2}")
                wq_r.note_w(lw3)
                for kq in range(16):
                    k = q * 16 + kq
                    tok = p.mm(pa[:, 0:512], wq_[:, kq * 128:(kq + 1) * 128],
                               hview2(k), start=(k == 0), stop=(k == 63),
                               deps=[lw3, hall_last]
                               + (pa_r.wdeps() if k == 0 else []))
                    wq_r.note_r(tok)
                    (hallA if k < 32 else hallB).note_r(tok)
            pa_r.note_w(tok)
            ot, ot_r = o_t[m2 % 2]
            rp, rp_r = rtile[m2 % 2]
            lr = p.dma(rp[:, 0:512],
                       out_sl[m2 * 128:(m2 + 1) * 128,
                              it * 512:(it + 1) * 512],
                       deps=rp_r.wdeps() + out_res[m2][it].rdeps(),
                       eng="sync", ch=f"rt{m2 % 2}")
            rp_r.note_w(lr)
            t6 = p.tt(ot[:, 0:512], rp[:, 0:512], pa[:, 0:512], ALU.add,
                      deps=[lr, tok] + ot_r.wdeps())
            ot_r.note_w(t6)
            rp_r.note_r(t6)
            pa_r.note_r(t6)
            d = p.dma(out_sl[m2 * 128:(m2 + 1) * 128,
                             it * 512:(it + 1) * 512], ot[:, 0:512],
                      deps=[t6] + out_res[m2][it].wdeps(), eng="sync")
            ot_r.note_r(d)
            out_res[m2][it].note_w(d)

    return p.build()


def fused_inmaps(inputs):
    inputs = {k: np.asarray(v) for k, v in inputs.items()}
    x = inputs["L0_attn_x"]
    x_fm = np.ascontiguousarray(x.transpose(0, 2, 1))
    w = _prep_weights_l1(inputs)
    _g = w.pop("wgT").reshape(D, INTER // 128, 128)
    _u = w.pop("wuT").reshape(D, INTER // 128, 128)
    w["wguT"] = np.ascontiguousarray(
        np.concatenate([_g, _u], axis=2).reshape(D, 2 * INTER))
    _s1 = np.ascontiguousarray(inputs["L1_ffn_sw1"].T).reshape(
        D, HSH // 128, 128)
    _s2 = np.ascontiguousarray(inputs["L1_ffn_sw2"].T).reshape(
        D, HSH // 128, 128)
    w["swguT"] = np.ascontiguousarray(
        np.concatenate([_s1, _s2], axis=2).reshape(D, 2 * HSH))
    w.update({
        "woT": np.ascontiguousarray(inputs["L0_attn_w_o"].T),
        "bo_r": np.ascontiguousarray(inputs["L0_attn_b_o"].reshape(16, 128).T),
        "wv1": np.ascontiguousarray(inputs["L1_attn_Wv"]),
        "wq1": np.ascontiguousarray(inputs["L1_attn_Wq"]),
        "wg1": np.ascontiguousarray(inputs["L1_attn_Wg"]),
        "wo1": np.ascontiguousarray(inputs["L1_attn_Wo"]),
        "beta_row": np.ascontiguousarray(inputs["L1_attn_beta"].reshape(1, 16)),
        "lng_r": np.ascontiguousarray(
            inputs["L1_attn_ln_g"].reshape(16, 128).T),
        "lnb_r": np.ascontiguousarray(
            inputs["L1_attn_ln_b"].reshape(16, 128).T),
        "w12T": np.ascontiguousarray(inputs["L1_ffn_w12"].transpose(0, 2, 1)),
        "w3i": np.ascontiguousarray(
            inputs["L1_ffn_w3"].transpose(0, 2, 1).reshape(NE * 1024, D)),
        "routerT": np.ascontiguousarray(inputs["L1_ffn_router_w"].T),
        "ebias_col": np.ascontiguousarray(
            inputs["L1_ffn_expert_bias"].reshape(8, 1)),
        "sw3T": np.ascontiguousarray(inputs["L1_ffn_sw3"].T),
        "c_ones_col": np.ones((128, 1), np.float32),
        "c_idn": np.eye(128, dtype=np.float32),
    })
    nonce_val = np.asarray(
        [[np.random.randint(1, 2 ** 30), 0, 0, 0]], dtype=np.int32)
    in_maps = []
    for c in range(8):
        b, par = c // 4, c % 2
        m = {"x_half": np.ascontiguousarray(
                 x_fm[b][:, par * 2048:(par + 1) * 2048]),
             "parcol": np.full((16, 1), float(par), np.float32),
             "nonce": nonce_val}
        m.update(w)
        in_maps.append(m)
    return in_maps


def kernel(**inputs):
    in_maps = fused_inmaps(inputs)
    nc = build_fused()
    res = run_bass_kernel_spmd(nc, in_maps, core_ids=list(range(8)))
    out_fm = np.empty((2, D, S), np.float32)
    for c in range(8):
        b, par, pairbit = c // 4, c % 2, (c % 4) // 2
        t0 = par * 2048 + pairbit * 1024
        out_fm[b][:, t0:t0 + 1024] = res.results[c]["out_sl"]
    return np.ascontiguousarray(out_fm.transpose(0, 2, 1))


# version 07
# speedup vs baseline: 1.7733x; 1.0260x over previous
import numpy as np
import ml_dtypes
from contextlib import ExitStack

import concourse.bass as bass
import concourse.mybir as mybir
from concourse.bass_utils import run_bass_kernel_spmd

F32 = mybir.dt.float32
BF16 = mybir.dt.bfloat16
I32 = mybir.dt.int32
AF = mybir.ActivationFunctionType
ALU = mybir.AluOpType
AX = mybir.AxisListType
bfdt = ml_dtypes.bfloat16

ENGINES = ("tensor", "scalar", "vector", "gpsimd", "sync")

D = 2048
S = 4096
NH = 16
DH = 128
INTER = 5632
NCH = D // 128
EPS = 1e-5


def cdiv(a, b):
    return (a + b - 1) // b


class Res:

    def __init__(self):
        self.w = None
        self.reads = []

    def wdeps(self):
        d = list(self.reads)
        if self.w is not None:
            d.append(self.w)
        return d

    def rdeps(self):
        return [self.w] if self.w is not None else []

    def note_w(self, tok):
        self.w = tok
        self.reads = []

    def note_r(self, tok):
        self.reads.append(tok)


class Prog:

    def __init__(self):
        self.nc = bass.Bass("TRN2", target_bir_lowering=False)
        self.ops = {e: [] for e in ENGINES}
        self.count = {e: 0 for e in ENGINES}
        self.stack = ExitStack()
        self.dma_count = {}
        self.role_engines = ("sync", "gpsimd")
        self.regs = {}
        self.semmap = {}

    def sbuf(self, name, shape, dtype):
        return self.stack.enter_context(self.nc.sbuf_tensor(name, shape, dtype))

    def psum(self, name, shape, dtype):
        return self.stack.enter_context(self.nc.psum_tensor(name, shape, dtype))

    def inp(self, name, shape, dtype):
        return self.nc.dram_tensor(name, shape, dtype, kind="ExternalInput").ap()

    def outp(self, name, shape, dtype):
        return self.nc.dram_tensor(name, shape, dtype, kind="ExternalOutput").ap()

    def dram(self, name, shape, dtype, **kw):
        return self.nc.dram_tensor(name, shape, dtype, kind="Internal", **kw).ap()

    def op(self, eng, fn, deps=(), inc=1, ch=None):
        ds = []
        for d in deps:
            if d is None:
                continue
            if isinstance(d, list):
                ds.extend(x for x in d if x is not None)
            else:
                ds.append(d)
        dd = {}
        for (k, v) in ds:
            if v > dd.get(k, 0):
                dd[k] = v
        ds = list(dd.items())
        if ch is None:
            self.count[eng] += inc
            tok = (eng, self.count[eng])
        else:
            key = "q_" + ch
            self.dma_count[key] = self.dma_count.get(key, 0) + inc
            tok = (key, self.dma_count[key])
        self.ops[eng].append((ds, fn, inc, tok))
        return tok

    def dma(self, out, in_, deps=(), eng="sync", ch=None, **kw):
        if ch is None:
            ods = str(out.tensor.space)
            ids = str(in_.tensor.space)
            ch = out.tensor.name if "dram" not in ods.lower() \
                else (in_.tensor.name if "dram" not in ids.lower()
                      else out.tensor.name)
        return self.op(eng, lambda e: e.dma_start(out=out, in_=in_, **kw),
                       deps, inc=16, ch=ch)

    def mm(self, out, lhsT, rhs, start, stop, deps=()):
        return self.op("tensor",
                       lambda e: e.matmul(out, lhsT, rhs, start=start, stop=stop),
                       deps)

    def trans(self, out, in_, idn, deps=()):
        return self.op("tensor", lambda e: e.transpose(out, in_, idn), deps)

    def act(self, out, in_, func, deps=(), bias=0.0, scale=1.0):
        return self.op("scalar",
                       lambda e: e.activation(out, in_, func, bias=bias,
                                              scale=scale), deps)

    def tt(self, out, a, b, alu, deps=(), eng="vector"):
        return self.op(eng, lambda e: e.tensor_tensor(out, a, b, alu), deps)

    def ts(self, out, in0, s1, s2, op0, op1, deps=()):
        return self.op("vector",
                       lambda e: e.tensor_scalar(out, in0, s1, s2, op0, op1),
                       deps)

    def stt(self, out, in0, scalar, in1, op0, op1, deps=()):
        return self.op("vector",
                       lambda e: e.scalar_tensor_tensor(out, in0, scalar, in1,
                                                        op0, op1), deps)

    def red(self, out, in_, alu, deps=(), negate=False):
        return self.op("vector",
                       lambda e: e.tensor_reduce(out, in_, AX.X, alu,
                                                 negate=negate), deps)

    def copyv(self, out, in_, deps=()):
        return self.op("vector", lambda e: e.tensor_copy(out, in_), deps)

    def recip(self, out, in_, deps=()):
        return self.op("vector", lambda e: e.reciprocal(out, in_), deps)

    def memset(self, ap, val, deps=(), eng="vector"):
        return self.op(eng, lambda e: e.memset(ap, val), deps)

    def scan(self, out, d0, d1, init, deps=()):
        return self.op("vector",
                       lambda e: e.tensor_tensor_scan(out, d0, d1, init,
                                                      ALU.mult, ALU.add), deps)

    def max8(self, out, in_, deps=()):
        return self.op("vector", lambda e: e.max(out, in_), deps)


    def dma2(self, out_even, in_even, out_odd, in_odd, deps=(), eng="sync",
             ch=None, bit="par", **kw):
        if ch is None:
            t = out_even.tensor if "dram" not in str(
                out_even.tensor.space).lower() else in_even.tensor
            ch = t.name
        key = "q_" + ch
        self.dma_count[key] = self.dma_count.get(key, 0) + 16
        tok = (key, self.dma_count[key])

        def fn(e):
            rbit = self.regs[e._prog_name][bit]
            sem = self.semmap[key]
            with e.If(rbit):
                e.dma_start(out=out_odd, in_=in_odd, **kw).then_inc(sem, 16)
            with e.Else():
                e.dma_start(out=out_even, in_=in_even, **kw).then_inc(sem, 16)
            return None

        ds = []
        for d in deps:
            if d is None:
                continue
            if isinstance(d, list):
                ds.extend(x for x in d if x is not None)
            else:
                ds.append(d)
        dd = {}
        for (k, v) in ds:
            if v > dd.get(k, 0):
                dd[k] = v
        self.ops[eng].append((list(dd.items()), fn, 16, tok))
        return tok

    def poll(self, flag_ap, nonce_ap, deps=(), eng="sync", max_iter=3000000):
        def fn(e):
            R = self.regs[e._prog_name]
            rf, rc, rn, rv = R["s0"], R["s1"], R["s2"], R["s3"]
            e.load(rv, nonce_ap)
            e.reg_mov(rn, max_iter)
            e.load(rf, flag_ap)
            e.reg_alu(rc, rf, rv, ALU.not_equal)
            e.reg_alu(rc, rc, rn, ALU.logical_and)
            with e.While(rc):
                e.load(rf, flag_ap)
                e.reg_sub(rn, rn, 1)
                e.reg_alu(rc, rf, rv, ALU.not_equal)
                e.reg_alu(rc, rc, rn, ALU.logical_and)
            return e.nop()

        return self.op(eng, fn, deps, inc=1)

    def build(self):
        nc = self.nc
        sems = {e: self.stack.enter_context(nc.semaphore(f"sem_{e}"))
                for e in ENGINES}
        for key in self.dma_count:
            sems[key] = self.stack.enter_context(nc.semaphore(key))
        self.semmap = sems
        block = self.stack.enter_context(nc.Block())
        for e in ENGINES:
            if not self.ops[e]:
                continue

            def body(eng, e=e):
                eng._prog_name = e
                if e in self.role_engines:
                    R = {}
                    for nm in ("pid", "par", "off", "s0", "s1", "s2", "s3"):
                        R[nm] = self.stack.enter_context(
                            eng.register(f"{e}_{nm}"))
                    eng.load(R["pid"], self.nc.partition_id_tensor.ap())
                    eng.reg_alu(R["par"], R["pid"], 1, ALU.bitwise_and)
                    eng.reg_alu(R["off"], R["pid"], 2, ALU.bitwise_and)
                    self.regs[e] = R
                hw = {}
                for deps, fn, inc, tok in self.ops[e]:
                    for (de, dv) in deps:
                        if de == e and e == "tensor":
                            continue
                        if hw.get(de, 0) >= dv:
                            continue
                        eng.wait_ge(sems[de], dv)
                        hw[de] = dv
                    instr = fn(eng)
                    if instr is not None:
                        instr.then_inc(sems[tok[0]], inc)

            getattr(block, e)(body)
        final = dict(self.count)
        finalq = dict(self.dma_count)

        def closer(eng):
            for e2, v in final.items():
                if v > 0:
                    eng.wait_ge(sems[e2], v)
            for k, v in finalq.items():
                eng.wait_ge(sems[k], v)

        block.sync(closer)
        self.stack.close()
        return nc



def alloc_common(p):
    c = {}
    c["ones_col_bf"] = p.sbuf("ones_col_bf", [128, 1], BF16)
    c["ones_row_f"] = p.sbuf("ones_row_f", [1, 128], F32)
    c["idn_bf"] = p.sbuf("idn_bf", [128, 128], BF16)
    co = p.inp("c_ones_col", [128, 1], F32)
    cr = p.inp("c_idn", [128, 128], F32)
    t1 = p.dma(c["ones_col_bf"][:, :], co, eng="gpsimd")
    t2 = p.dma(c["ones_row_f"][0:1, :],
               co.rearrange("p one -> one p"), eng="sync")
    t3 = p.dma(c["idn_bf"][:, :], cr, eng="gpsimd")
    c["const_toks"] = [t1, t2, t3]
    return c


def emit_ln(p, C, x_dram, t0, nt, dst_sb, dst_res, dst_off, out_dtype_bf=True,
            xs=None, xs_res=None, keep_cb=None, tag="", src_deps=(),
            src_view=None, gb=None):
    P = C["P"]
    TT = 512
    sumt, sum_r = P["sum"]
    sqt, sq_r = P["sq"]
    bca, bca_r = P["bcA"]
    bcb, bcb_r = P["bcB"]
    sq_sb, sq_sb_r = C["sq_tile"]
    mrow, mrow_r = C["rows"]["mean"]
    vrow, vrow_r = C["rows"]["var"]
    arow, arow_r = C["rows"]["a"]
    brow, brow_r = C["rows"]["b"]
    ab_bf, ab_bf_r = C["ab_bf"]
    toks = []
    ntt = cdiv(nt, TT)

    def load(it):
        tt = min(TT, nt - it * TT)
        if src_view is not None:
            return None
        xst, xr = xs[it % 2]
        srcv = x_dram[:, t0 + it * TT: t0 + it * TT + tt] \
            .rearrange("(c p) t -> p c t", p=128)
        ld = p.dma(xst[:, 0:16 * tt].rearrange("p (c t) -> p c t", t=tt),
                   srcv, deps=xr.wdeps() + list(src_deps), eng="gpsimd",
                   ch=f"lnxs{it % 2}")
        xr.note_w(ld)
        return ld

    def xv(it, c):
        tt = min(TT, nt - it * TT)
        if src_view is not None:
            return src_view(c, it, tt)
        return xs[it % 2][0][:, c * tt:(c + 1) * tt]

    def xnote(it, tok):
        if src_view is None:
            xs[it % 2][1].note_r(tok)

    lds = {}
    if src_view is None:
        lds[0] = load(0)
    ones = C["ones_col_bf"]

    def stats(it):
        tt = min(TT, nt - it * TT)
        ld = lds.get(it)
        dep0 = [ld] if ld is not None else list(src_deps)
        for c in range(NCH):
            tok = p.mm(sumt[0:1, 0:tt], ones[0:128, 0:1], xv(it, c),
                       start=(c == 0), stop=(c == NCH - 1),
                       deps=dep0 + (sum_r.wdeps() if c == 0 else []))
            xnote(it, tok)
        sum_r.note_w(tok)
        for c in range(NCH):
            sq = p.act(sq_sb[:, 0:tt], xv(it, c), AF.Square,
                       deps=dep0 + sq_sb_r.wdeps())
            xnote(it, sq)
            sq_sb_r.note_w(sq)
            tok = p.mm(sqt[0:1, 0:tt], C["ones_col_f"][0:128, 0:1],
                       sq_sb[:, 0:tt], start=(c == 0), stop=(c == NCH - 1),
                       deps=[sq] + (sq_r.wdeps() if c == 0 else []))
            sq_sb_r.note_r(tok)
        sq_r.note_w(tok)
        return sum_r.w, sq_r.w

    def rows(it, mm_sum, mm_sq):
        tt = min(TT, nt - it * TT)
        t_m = p.act(mrow[0:1, 0:tt], sumt[0:1, 0:tt], AF.Copy, scale=1.0 / D,
                    deps=[mm_sum] + mrow_r.wdeps())
        mrow_r.note_w(t_m)
        sum_r.note_r(t_m)
        t_v = p.ts(vrow[0:1, 0:tt], sqt[0:1, 0:tt], 1.0 / D, None, ALU.mult,
                   ALU.bypass, deps=[mm_sq] + vrow_r.wdeps())
        vrow_r.note_w(t_v)
        sq_r.note_r(t_v)
        t_msq = p.act(sq_sb[0:1, 0:tt], mrow[0:1, 0:tt], AF.Square,
                      deps=[t_m] + sq_sb_r.wdeps())
        sq_sb_r.note_w(t_msq)
        mrow_r.note_r(t_msq)
        t_v2 = p.tt(vrow[0:1, 0:tt], vrow[0:1, 0:tt], sq_sb[0:1, 0:tt],
                    ALU.subtract, deps=[t_v, t_msq])
        vrow_r.note_w(t_v2)
        sq_sb_r.note_r(t_v2)
        t_sd = p.act(arow[0:1, 0:tt], vrow[0:1, 0:tt], AF.Sqrt,
                     bias=C["eps1"][0:1, 0:1],
                     deps=[t_v2] + arow_r.wdeps() + C.get("ct", []))
        vrow_r.note_r(t_sd)
        t_a = p.recip(arow[0:1, 0:tt], arow[0:1, 0:tt], deps=[t_sd])
        arow_r.note_w(t_a)
        t_b = p.stt(brow[0:1, 0:tt], mrow[0:1, 0:tt], -1.0, arow[0:1, 0:tt],
                    ALU.mult, ALU.mult, deps=[t_a] + brow_r.wdeps())
        brow_r.note_w(t_b)
        mrow_r.note_r(t_b)
        arow_r.note_r(t_b)
        return t_a, t_b

    def bcast(it, t_a, t_b):
        tt = min(TT, nt - it * TT)
        t_ba = p.mm(bca[:, 0:tt], C["ones_row_f"][0:1, 0:128],
                    arow[0:1, 0:tt], start=True, stop=True,
                    deps=[t_a] + bca_r.wdeps())
        bca_r.note_w(t_ba)
        arow_r.note_r(t_ba)
        t_bb = p.mm(bcb[:, 0:tt], C["ones_row_f"][0:1, 0:128],
                    brow[0:1, 0:tt], start=True, stop=True,
                    deps=[t_b] + bcb_r.wdeps())
        bcb_r.note_w(t_bb)
        brow_r.note_r(t_bb)
        e_a = p.act(ab_bf[:, 0:tt], bca[:, 0:tt], AF.Copy,
                    deps=[t_ba] + ab_bf_r.wdeps())
        ab_bf_r.note_w(e_a)
        bca_r.note_r(e_a)
        e_b = p.act(ab_bf[:, 512:512 + tt], bcb[:, 0:tt], AF.Copy,
                    deps=[t_bb, e_a])
        ab_bf_r.note_w(e_b)
        bcb_r.note_r(e_b)
        return e_a, e_b

    def normalize(it, e_a, e_b):
        tt = min(TT, nt - it * TT)
        tmp, tmp_r = C["ln_tmp_bf"]
        for c in range(NCH):
            dcol = dst_off + c * nt + it * TT
            if gb is None:
                t1 = p.tt(tmp[:, 0:tt], xv(it, c), ab_bf[:, 0:tt], ALU.mult,
                          deps=[e_a] + tmp_r.wdeps())
                xnote(it, t1)
                tmp_r.note_w(t1)
                t2 = p.tt(dst_sb[:, dcol:dcol + tt], tmp[:, 0:tt],
                          ab_bf[:, 512:512 + tt], ALU.add,
                          deps=[t1, e_b] + dst_res.wdeps())
                tmp_r.note_r(t2)
            else:
                t1 = p.tt(tmp[:, 0:tt], xv(it, c), ab_bf[:, 0:tt], ALU.mult,
                          deps=[e_a] + tmp_r.wdeps())
                xnote(it, t1)
                tmp_r.note_w(t1)
                t1b = p.tt(tmp[:, 0:tt], tmp[:, 0:tt],
                           ab_bf[:, 512:512 + tt], ALU.add, deps=[t1, e_b])
                tmp_r.note_w(t1b)
                gcol, bcol = gb
                t2 = p.ts(dst_sb[:, dcol:dcol + tt], tmp[:, 0:tt],
                          gcol(c), bcol(c), ALU.mult, ALU.add,
                          deps=[t1b] + dst_res.wdeps() + C.get("ct", []))
                tmp_r.note_r(t2)
            ab_bf_r.note_r(t2)
            dst_res.note_r(t2)
            toks.append(t2)
            if keep_cb is not None:
                keep_cb(c, it, min(TT, nt - it * TT), xv(it, c), None)

    r_prev = None
    for it in range(ntt):
        s_i = stats(it)
        if r_prev is not None:
            b_p = bcast(it - 1, *r_prev)
        r_prev_new = rows(it, *s_i)
        if r_prev is not None:
            normalize(it - 1, *b_p)
        r_prev = r_prev_new
        if src_view is None and it + 1 < ntt:
            lds[it + 1] = load(it + 1)
    b_p = bcast(ntt - 1, *r_prev)
    normalize(ntt - 1, *b_p)
    for t in toks:
        dst_res.note_r(t)
    dst_res.note_w(toks[-1])
    dst_res.reads = list(toks)
    return toks


def emit_mm_block(p, C, out_cb, wq_dram, wq_cols, rhs_sb, rhs_res, rhs_off,
                  nt, wstage, wstage_res, evac, nk=NCH, kchunk=1024):
    P = C["P"]
    acc = P["acc"]
    TT = 512
    ntt = cdiv(nt, TT)
    for mi, m in enumerate(wq_cols):
        ws, ws_r = wstage[mi % 2]
        src = wq_dram[:, m * 128:(m + 1) * 128] \
            .rearrange("(c p) m -> p c m", p=128)
        lw = p.dma(ws[:, 0:nk * 128].rearrange("p (c m) -> p c m", m=128),
                   src, deps=ws_r.wdeps(), eng="gpsimd")
        ws_r.note_w(lw)
        for it in range(ntt):
            tt = min(TT, nt - it * TT)
            pa, pa_r = acc[(mi * ntt + it) % 2]
            for k in range(nk):
                deps = [lw] + rhs_res.rdeps()
                if k == 0:
                    deps += pa_r.wdeps()
                tok = p.mm(pa[:, 0:tt], ws[:, k * 128:(k + 1) * 128],
                           rhs_sb[:, rhs_off + k * kchunk + it * TT:
                                  rhs_off + k * kchunk + it * TT + tt],
                           start=(k == 0), stop=(k == nk - 1), deps=deps)
                ws_r.note_r(tok)
                rhs_res.note_r(tok)
            pa_r.note_w(tok)
            ev = evac(mi, m, it, tt, pa[:, 0:tt], [tok])
            pa_r.note_r(ev)



def build_launch1():
    p = Prog()
    x_sl = p.inp("x_sl", [D, 1024], F32)
    wqkvT = p.inp("wqkvT", [D, 3 * D], F32)
    bqkv_r = p.inp("bqkv_r", [128, 48], F32)
    wgT = p.inp("wgT", [D, INTER], F32)
    wuT = p.inp("wuT", [D, INTER], F32)
    wdT = p.inp("wdT", [INTER, D], F32)
    gs_r = p.inp("gs_r", [128, 44], F32)
    us_r = p.inp("us_r", [128, 44], F32)
    ds_r = p.inp("ds_r", [128, 16], F32)

    q_out = p.outp("q_out", [D, 1024], BF16)
    k_out = p.outp("k_out", [D, 1024], BF16)
    vt_out = p.outp("vt_out", [1024, D], BF16)
    r1p_out = p.outp("r1p_out", [D, 1024], F32)

    C = {}
    C["ones_col_f"] = p.sbuf("ones_col_f", [128, 1], F32)
    C["ones_row_f"] = p.sbuf("ones_row_f", [1, 128], F32)
    C["idn_bf"] = p.sbuf("idn_bf", [128, 128], BF16)
    co = p.inp("c_ones_col", [128, 1], F32)
    cidn = p.inp("c_idn", [128, 128], F32)
    ct = [p.dma(C["ones_col_f"][:, :], co, eng="gpsimd"),
          p.dma(C["ones_row_f"][0:1, :], co.rearrange("p one -> one p"),
                eng="sync"),
          p.dma(C["idn_bf"][:, :], cidn, eng="gpsimd")]
    biases = p.sbuf("biases", [128, 48 + 44 + 44 + 16], F32)
    ct.append(p.dma(biases[:, 0:48], bqkv_r, eng="sync"))
    ct.append(p.dma(biases[:, 48:92], gs_r, eng="sync"))
    ct.append(p.dma(biases[:, 92:136], us_r, eng="sync"))
    ct.append(p.dma(biases[:, 136:152], ds_r, eng="sync"))
    bq = lambda m: biases[:, m:m + 1]
    gsv = lambda m: biases[:, 48 + m:48 + m + 1]
    usv = lambda m: biases[:, 92 + m:92 + m + 1]
    dsv = lambda m: biases[:, 136 + m:136 + m + 1]

    C["eps1"] = p.sbuf("eps1", [1, 1], F32)
    ct.append(p.memset(C["eps1"][0:1, 0:1], EPS))
    C["ct"] = ct
    for e in ENGINES:
        p.op(e, lambda eng: eng.nop(), deps=ct)

    P = {}
    P["acc"] = [(p.psum("accA", [128, 512], F32), Res()),
                (p.psum("accB", [128, 512], F32), Res())]
    P["bcA"] = (p.psum("bcA", [128, 512], F32), Res())
    P["bcB"] = (p.psum("bcB", [128, 512], F32), Res())
    P["sum"] = (p.psum("psumrow", [1, 512], F32), Res())
    P["sq"] = (p.psum("psqrow", [1, 512], F32), Res())
    P["tr"] = (p.psum("ptr", [128, 512], BF16), Res())
    C["P"] = P

    xs = p.sbuf("xs", [128, 16 * 512], F32)
    xs_res = Res()
    xn = p.sbuf("xn", [128, NCH * 1024], BF16)
    xn_res = Res()
    C["sq_tile"] = (p.sbuf("sq_tile", [128, 512], F32), Res())
    C["ln_tmp"] = (p.sbuf("ln_tmp", [128, 512], F32), Res())
    rows = p.sbuf("rows", [1, 4 * 512], F32)
    C["rows"] = {nm: (rows[0:1, i * 512:(i + 1) * 512], Res())
                 for i, nm in enumerate(("mean", "var", "a", "b"))}
    wstage = [(p.sbuf("wsA", [128, NCH * 128], BF16), Res()),
              (p.sbuf("wsB", [128, NCH * 128], BF16), Res())]
    wstage2 = [(p.sbuf("ws2A", [128, NCH * 128], BF16), Res()),
               (p.sbuf("ws2B", [128, NCH * 128], BF16), Res())]
    wdstage = [(p.sbuf("wdA", [128, 44 * 128], BF16), Res()),
               (p.sbuf("wdB", [128, 44 * 128], BF16), Res())]
    ev_bf = [(p.sbuf("evbfA", [128, 512], BF16), Res()),
             (p.sbuf("evbfB", [128, 512], BF16), Res())]
    g_t = (p.sbuf("g_t", [128, 512], F32), Res())
    u_t = (p.sbuf("u_t", [128, 512], F32), Res())
    d_t = (p.sbuf("d_t", [128, 512], F32), Res())
    xre = [(p.sbuf("xreA", [128, 512], F32), Res()),
           (p.sbuf("xreB", [128, 512], F32), Res())]
    r1t = [(p.sbuf("r1tA", [128, 512], F32), Res()),
           (p.sbuf("r1tB", [128, 512], F32), Res())]
    h_sb = p.sbuf("h_sb", [128, 44 * 512], BF16)
    h_res = Res()
    vt_ev = [(p.sbuf("vtevA", [128, 128], BF16), Res()),
             (p.sbuf("vtevB", [128, 128], BF16), Res())]

    for (_, r) in (C["sq_tile"], C["ln_tmp"]):
        pass
    xs_res.note_w(ct[0])
    for t in ct:
        xs_res.note_r(t)

    emit_ln(p, C, x_sl, 0, 1024, xn, xn_res, 0, xs=xs, xs_res=xs_res)

    def evac_qkv(mi, m, it, tt, pa, deps):
        eb, eb_r = ev_bf[it % 2]
        tok = p.act(eb[:, 0:tt], pa, AF.Copy, bias=0.0, scale=1.0,
                    deps=deps + eb_r.wdeps())
        return tok

    def evac_qkv2(mi, m, it, tt, pa, deps):
        eb, eb_r = ev_bf[it % 2]
        tok = p.ts(eb[:, 0:tt], pa, bq(m), None, ALU.add, ALU.bypass,
                   deps=deps + eb_r.wdeps())
        eb_r.note_w(tok)
        if m < 16:
            d = p.dma(q_out[m * 128:(m + 1) * 128,
                            it * 512:it * 512 + tt], eb[:, 0:tt],
                      deps=[tok], eng="sync")
            eb_r.note_r(d)
        elif m < 32:
            mm = m - 16
            d = p.dma(k_out[mm * 128:(mm + 1) * 128,
                            it * 512:it * 512 + tt], eb[:, 0:tt],
                      deps=[tok], eng="sync")
            eb_r.note_r(d)
        else:
            mm = m - 32
            trp, trp_r = P["tr"]
            for j in range(cdiv(tt, 128)):
                tj = p.trans(trp[:, j * 128:(j + 1) * 128],
                             eb[:, j * 128:(j + 1) * 128], C["idn_bf"][:, :],
                             deps=[tok] + trp_r.wdeps())
                eb_r.note_r(tj)
                ve, ve_r = vt_ev[j % 2]
                cv = p.copyv(ve[:, :], trp[:, j * 128:(j + 1) * 128],
                             deps=[tj] + ve_r.wdeps())
                ve_r.note_w(cv)
                trp_r.note_w(tj)
                trp_r.note_r(cv)
                d = p.dma(vt_out[it * 512 + j * 128: it * 512 + j * 128 + 128,
                                 mm * 128:(mm + 1) * 128], ve[:, :],
                          deps=[cv], eng="sync")
                ve_r.note_r(d)
        return tok

    emit_mm_block(p, C, None, wqkvT, list(range(48)), xn, xn_res, 0, 1024,
                  wstage, None, evac_qkv2)

    for it in range(2):
        for m in range(44):
            wsg, wsg_r = wstage[m % 2]
            wsu, wsu_r = wstage2[m % 2]
            lg = p.dma(wsg[:, 0:NCH * 128].rearrange("p (c m) -> p c m", m=128),
                       wgT[:, m * 128:(m + 1) * 128]
                       .rearrange("(c p) m -> p c m", p=128),
                       deps=wsg_r.wdeps(), eng="gpsimd")
            wsg_r.note_w(lg)
            lu = p.dma(wsu[:, 0:NCH * 128].rearrange("p (c m) -> p c m", m=128),
                       wuT[:, m * 128:(m + 1) * 128]
                       .rearrange("(c p) m -> p c m", p=128),
                       deps=wsu_r.wdeps(), eng="gpsimd")
            wsu_r.note_w(lu)
            pa, pa_r = P["acc"][0]
            pb, pb_r = P["acc"][1]
            for k in range(NCH):
                tok = p.mm(pa[:, 0:512], wsg[:, k * 128:(k + 1) * 128],
                           xn[:, k * 1024 + it * 512: k * 1024 + it * 512 + 512],
                           start=(k == 0), stop=(k == NCH - 1),
                           deps=[lg] + xn_res.rdeps()
                           + (pa_r.wdeps() if k == 0 else []))
                wsg_r.note_r(tok)
                xn_res.note_r(tok)
            pa_r.note_w(tok)
            tg = tok
            for k in range(NCH):
                tok = p.mm(pb[:, 0:512], wsu[:, k * 128:(k + 1) * 128],
                           xn[:, k * 1024 + it * 512: k * 1024 + it * 512 + 512],
                           start=(k == 0), stop=(k == NCH - 1),
                           deps=[lu] + xn_res.rdeps()
                           + (pb_r.wdeps() if k == 0 else []))
                wsu_r.note_r(tok)
                xn_res.note_r(tok)
            pb_r.note_w(tok)
            tu = tok
            gt, gt_r = g_t
            ut, ut_r = u_t
            t1 = p.act(gt[:, :], pa[:, 0:512], AF.Silu, scale=gsv(m),
                       deps=[tg] + gt_r.wdeps())
            gt_r.note_w(t1)
            pa_r.note_r(t1)
            t2 = p.act(ut[:, :], pb[:, 0:512], AF.Copy, scale=usv(m),
                       deps=[tu] + ut_r.wdeps())
            ut_r.note_w(t2)
            pb_r.note_r(t2)
            t3 = p.tt(h_sb[:, m * 512:(m + 1) * 512], gt[:, :], ut[:, :],
                      ALU.mult, deps=[t1, t2] + h_res.wdeps())
            gt_r.note_r(t3)
            ut_r.note_r(t3)
            h_res.note_r(t3)
            h_res.reads.append(t3)
        h_res.note_w(t3)
        h_res.reads = []
        h_last = t3
        for m2 in range(16):
            wd, wd_r = wdstage[m2 % 2]
            lw = p.dma(wd[:, 0:44 * 128].rearrange("p (c m) -> p c m", m=128),
                       wdT[:, m2 * 128:(m2 + 1) * 128]
                       .rearrange("(c p) m -> p c m", p=128),
                       deps=wd_r.wdeps(), eng="gpsimd")
            wd_r.note_w(lw)
            pa, pa_r = P["acc"][m2 % 2]
            for k in range(44):
                tok = p.mm(pa[:, 0:512], wd[:, k * 128:(k + 1) * 128],
                           h_sb[:, k * 512:(k + 1) * 512],
                           start=(k == 0), stop=(k == 43),
                           deps=[lw, h_last]
                           + (pa_r.wdeps() if k == 0 else []))
                wd_r.note_r(tok)
                h_res.note_r(tok)
            pa_r.note_w(tok)
            dt, dt_r = d_t
            t1 = p.act(dt[:, :], pa[:, 0:512], AF.Copy, scale=dsv(m2),
                       deps=[tok] + dt_r.wdeps())
            dt_r.note_w(t1)
            pa_r.note_r(t1)
            xr, xr_r = xre[m2 % 2]
            lx = p.dma(xr[:, :],
                       x_sl[m2 * 128:(m2 + 1) * 128, it * 512:(it + 1) * 512],
                       deps=xr_r.wdeps(), eng="sync")
            xr_r.note_w(lx)
            rt, rt_r = r1t[m2 % 2]
            t2 = p.tt(rt[:, :], dt[:, :], xr[:, :], ALU.add,
                      deps=[t1, lx] + rt_r.wdeps())
            rt_r.note_w(t2)
            dt_r.note_r(t2)
            xr_r.note_r(t2)
            d = p.dma(r1p_out[m2 * 128:(m2 + 1) * 128,
                              it * 512:(it + 1) * 512], rt[:, :],
                      deps=[t2], eng="sync")
            rt_r.note_r(d)

    return p.build()



def _prep_weights_l1(inputs):
    w = {}
    w["wqkvT"] = np.ascontiguousarray(inputs["L0_attn_w_qkv"].T)
    w["bqkv_r"] = np.ascontiguousarray(
        inputs["L0_attn_b_qkv"].reshape(48, 128).T)
    w["wgT"] = np.ascontiguousarray(inputs["L0_ffn_gate_w"].T)
    w["wuT"] = np.ascontiguousarray(inputs["L0_ffn_up_w"].T)
    w["wdT"] = np.ascontiguousarray(inputs["L0_ffn_down_w"].T)
    w["gs_r"] = np.ascontiguousarray(
        inputs["L0_ffn_gate_s"].reshape(44, 128).T)
    w["us_r"] = np.ascontiguousarray(
        inputs["L0_ffn_up_s"].reshape(44, 128).T)
    w["ds_r"] = np.ascontiguousarray(
        inputs["L0_ffn_down_s"].reshape(16, 128).T)
    w["c_ones_col"] = np.ones((128, 1), np.float32)
    w["c_idn"] = np.eye(128, dtype=np.float32)
    return w


def run_launch1(inputs):
    x = np.asarray(inputs["L0_attn_x"])
    x_fm = np.ascontiguousarray(x.transpose(0, 2, 1))
    w = _prep_weights_l1(inputs)
    nc = build_launch1()
    in_maps = []
    for c in range(8):
        b, g = c // 4, c % 4
        m = {"x_sl": np.ascontiguousarray(x_fm[b][:, g * 1024:(g + 1) * 1024])}
        m.update(w)
        in_maps.append(m)
    res = run_bass_kernel_spmd(nc, in_maps, core_ids=list(range(8)))
    return res.results





def build_launch2():
    p = Prog()
    q_in = p.inp("q_in", [D, 1024], BF16)
    k_full = p.inp("k_full", [D, S], BF16)
    vt_full = p.inp("vt_full", [S, D], BF16)
    r1p_in = p.inp("r1p_in", [D, 1024], F32)
    woT = p.inp("woT", [D, D], F32)
    bo_r = p.inp("bo_r", [128, 16], F32)
    wv1 = p.inp("wv1", [D, D], F32)
    beta_in = p.inp("beta_in", [1, 16], F32)

    r1_out = p.outp("r1_out", [D, 1024], F32)
    xn1_out = p.outp("xn1_out", [D, 1024], BF16)
    v_out = p.outp("v_out", [D, 1024], BF16)
    carry_out = p.outp("carry_out", [16, 128], F32)

    C = {}
    C["ones_col_f"] = p.sbuf("ones_col_f", [128, 1], F32)
    C["ones_col_bf"] = p.sbuf("ones_col_bf", [128, 1], BF16)
    C["ones_row_f"] = p.sbuf("ones_row_f", [1, 128], F32)
    C["idn_f"] = p.sbuf("idn_f", [128, 128], F32)
    co = p.inp("c_ones_col", [128, 1], F32)
    cidn = p.inp("c_idn", [128, 128], F32)
    ct = [p.dma(C["ones_col_f"][:, :], co, eng="sync"),
          p.dma(C["ones_col_bf"][:, :], co, eng="gpsimd"),
          p.dma(C["ones_row_f"][0:1, :], co.rearrange("p one -> one p"),
                eng="sync"),
          p.dma(C["idn_f"][:, :], cidn, eng="sync")]
    biases = p.sbuf("biases2", [128, 16], F32)
    ct.append(p.dma(biases[:, 0:16], bo_r, eng="sync"))
    bov = lambda m: biases[:, m:m + 1]
    beta_sb = p.sbuf("beta_sb", [1, 16], F32)
    ct.append(p.dma(beta_sb[0:1, :], beta_in, eng="sync"))
    C["eps1"] = p.sbuf("eps1", [1, 1], F32)
    ct.append(p.memset(C["eps1"][0:1, 0:1], EPS))
    C["ct"] = ct
    for e in ENGINES:
        p.op(e, lambda eng: eng.nop(), deps=ct)

    P = {}
    P["acc"] = [(p.psum("accA", [128, 512], F32), Res()),
                (p.psum("accB", [128, 512], F32), Res())]
    P["bcA"] = (p.psum("bcA", [128, 512], F32), Res())
    P["bcB"] = (p.psum("bcB", [128, 512], F32), Res())
    P["sum"] = (p.psum("psumrow", [1, 512], F32), Res())
    P["sq"] = (p.psum("psqrow", [1, 512], F32), Res())
    P["tr32"] = (p.psum("tr32", [16, 128], F32), Res())
    C["P"] = P
    scoresP = P["bcA"]
    avP = P["bcB"]
    sigP = P["sum"]
    nbcP = P["sq"]

    kh = p.sbuf("kh", [128, S], BF16)
    kh_r = Res()
    vh = p.sbuf("vh", [128, 32 * 128], BF16)
    vh_r = Res()
    qh = p.sbuf("qh", [128, 1024], BF16)
    qh_r = Res()
    sqq = p.sbuf("sqq", [128, 1024], BF16)
    sqq_r = Res()
    probs = [(p.sbuf("probsA", [128, 512], BF16), Res()),
             (p.sbuf("probsB", [128, 512], BF16), Res())]
    attn = p.sbuf("attn", [128, 16 * 1024], BF16)
    attn_r = Res()
    av_sb = (p.sbuf("av_sb", [128, 512], F32), Res())
    rowbuf = p.sbuf("rowbuf", [1, 1024], F32)
    rowbuf_r = Res()
    tiny = p.sbuf("tiny", [1, 8], F32)
    tiny_r = Res()
    cbias = p.sbuf("cbias", [128, 1], F32)
    cbias_r = Res()
    srow = p.sbuf("srow", [1, 512], F32)
    srow_r = Res()

    xs = p.sbuf("xs", [128, 16 * 512], F32)
    xs_res = Res()
    xn1 = p.sbuf("xn1", [128, NCH * 1024], BF16)
    xn1_res = Res()
    C["sq_tile"] = (p.sbuf("sq_tile", [128, 512], F32), Res())
    C["ln_tmp"] = (p.sbuf("ln_tmp", [128, 512], F32), Res())
    rows = p.sbuf("rows", [1, 4 * 512], F32)
    C["rows"] = {nm: (rows[0:1, i * 512:(i + 1) * 512], Res())
                 for i, nm in enumerate(("mean", "var", "a", "b"))}
    wstage = [(p.sbuf("wsA", [128, NCH * 128], BF16), Res()),
              (p.sbuf("wsB", [128, NCH * 128], BF16), Res())]
    ev_bf = [(p.sbuf("evbfA", [128, 512], BF16), Res()),
             (p.sbuf("evbfB", [128, 512], BF16), Res())]
    r1tile = [(p.sbuf("r1tA", [128, 512], F32), Res()),
              (p.sbuf("r1tB", [128, 512], F32), Res())]
    rptile = [(p.sbuf("rptA", [128, 512], F32), Res()),
              (p.sbuf("rptB", [128, 512], F32), Res())]
    v_sb = p.sbuf("v_sb", [128, 16 * 1024], BF16)
    v_sb_r = Res()
    lam_big = p.sbuf("lam_big", [128, 1024], F32)
    lam_big_r = Res()
    ones_big = p.sbuf("ones_big", [128, 1024], F32)
    ct2 = p.memset(ones_big[:, :], 1.0)
    lam_row = p.sbuf("lam_row", [1, 16], F32)
    lam_row_r = Res()
    lamcol = p.sbuf("lamcol", [128, 1], F32)
    lamcol_r = Res()
    st_sc = p.sbuf("st_sc", [128, 1024], F32)
    st_sc_r = Res()
    carry_sb = p.sbuf("carry_sb", [128, 16], F32)
    carry_sb_r = Res()
    carry_ev = p.sbuf("carry_ev", [16, 128], F32)
    carry_ev_r = Res()

    SC = 1.0 / np.sqrt(128.0)

    for h in range(NH):
        lk = p.dma(kh[:, :], k_full[h * 128:(h + 1) * 128, :],
                   deps=kh_r.wdeps(), eng="sync")
        kh_r.note_w(lk)
        lv = p.dma(vh[:, :].rearrange("p (c dh) -> p c dh", dh=128),
                   vt_full[:, h * 128:(h + 1) * 128]
                   .rearrange("(c p) dh -> p c dh", p=128),
                   deps=vh_r.wdeps(), eng="sync")
        vh_r.note_w(lv)
        lq = p.dma(qh[:, :], q_in[h * 128:(h + 1) * 128, :],
                   deps=qh_r.wdeps(), eng="sync")
        qh_r.note_w(lq)
        t_sq = p.act(sqq[:, 0:1024], qh[:, :], AF.Square,
                     deps=[lq] + sqq_r.wdeps())
        sqq_r.note_w(t_sq)
        qh_r.note_r(t_sq)
        sgt, sg_r = sigP
        for half in range(2):
            tok = p.mm(sgt[0:1, 0:512], C["ones_col_bf"][0:128, 0:1],
                       sqq[:, half * 512:(half + 1) * 512],
                       start=True, stop=True,
                       deps=[t_sq] + sg_r.wdeps())
            sg_r.note_w(tok)
            sqq_r.note_r(tok)
            cv = p.copyv(rowbuf[0:1, half * 512:(half + 1) * 512],
                         sgt[0:1, 0:512], deps=[tok] + rowbuf_r.wdeps())
            sg_r.note_r(cv)
            rowbuf_r.note_w(cv)
        t_mq = p.red(tiny[0:1, 0:1], rowbuf[0:1, 0:1024], ALU.max,
                     deps=rowbuf_r.rdeps() + tiny_r.wdeps())
        tiny_r.note_w(t_mq)
        rowbuf_r.note_r(t_mq)
        t_mk_parts = []
        for piece in range(4):
            t_sqk = p.act(sqq[:, 0:1024], kh[:, piece * 1024:(piece + 1) * 1024],
                          AF.Square, deps=[lk, t_mq] + sqq_r.wdeps())
            sqq_r.note_w(t_sqk)
            kh_r.note_r(t_sqk)
            for half in range(2):
                tok = p.mm(sgt[0:1, 0:512], C["ones_col_bf"][0:128, 0:1],
                           sqq[:, half * 512:(half + 1) * 512],
                           start=True, stop=True,
                           deps=[t_sqk] + sg_r.wdeps())
                sg_r.note_w(tok)
                sqq_r.note_r(tok)
                cv = p.copyv(rowbuf[0:1, half * 512:(half + 1) * 512],
                             sgt[0:1, 0:512], deps=[tok] + rowbuf_r.wdeps())
                sg_r.note_r(cv)
                rowbuf_r.note_w(cv)
            t_mk = p.red(tiny[0:1, 1 + piece:2 + piece], rowbuf[0:1, 0:1024],
                         ALU.max, deps=rowbuf_r.rdeps() + [t_mq])
            rowbuf_r.note_r(t_mk)
            t_mk_parts.append(t_mk)
        t_mk4 = p.red(tiny[0:1, 5:6], tiny[0:1, 1:5], ALU.max,
                      deps=t_mk_parts)
        t_c1 = p.tt(tiny[0:1, 6:7], tiny[0:1, 0:1], tiny[0:1, 5:6], ALU.mult,
                    deps=[t_mq, t_mk4])
        t_c2 = p.act(tiny[0:1, 7:8], tiny[0:1, 6:7], AF.Sqrt, scale=1.0 / 128,
                     deps=[t_c1])
        t_c3 = p.ts(tiny[0:1, 7:8], tiny[0:1, 7:8], -1.0, None, ALU.mult,
                    ALU.bypass, deps=[t_c2])
        tiny_r.note_w(t_c3)
        pa0, pa0_r = P["acc"][0]
        t_bc = p.mm(pa0[:, 0:1], C["ones_row_f"][0:1, 0:128], tiny[0:1, 7:8],
                    start=True, stop=True, deps=[t_c3] + pa0_r.wdeps())
        pa0_r.note_w(t_bc)
        tiny_r.note_r(t_bc)
        t_cb = p.copyv(cbias[:, 0:1], pa0[:, 0:1],
                       deps=[t_bc] + cbias_r.wdeps())
        cbias_r.note_w(t_cb)
        pa0_r.note_r(t_cb)
        spt, sp_r = scoresP
        avt, av_r = avP
        for qp in range(2):
            for kc in range(32):
                t_s = p.mm(spt[:, 0:512], kh[:, kc * 128:(kc + 1) * 128],
                           qh[:, qp * 512:(qp + 1) * 512],
                           start=True, stop=True,
                           deps=[lk, lq] + sp_r.wdeps())
                sp_r.note_w(t_s)
                kh_r.note_r(t_s)
                qh_r.note_r(t_s)
                pb, pb_r = probs[kc % 2]
                t_e = p.act(pb[:, :], spt[:, 0:512], AF.Exp, bias=cbias[:, 0:1],
                            scale=SC, deps=[t_s, t_cb] + pb_r.wdeps())
                pb_r.note_w(t_e)
                sp_r.note_r(t_e)
                cbias_r.note_r(t_e)
                t_av = p.mm(avt[:, 0:512], vh[:, kc * 128:(kc + 1) * 128],
                            pb[:, :], start=(kc == 0), stop=(kc == 31),
                            deps=[lv, t_e] + (av_r.wdeps() if kc == 0 else []))
                pb_r.note_r(t_av)
                vh_r.note_r(t_av)
                t_sg = p.mm(sgt[0:1, 0:512], C["ones_col_bf"][0:128, 0:1],
                            pb[:, :], start=(kc == 0), stop=(kc == 31),
                            deps=[t_e] + (sg_r.wdeps() if kc == 0 else []))
                pb_r.note_r(t_sg)
            av_r.note_w(t_av)
            sg_r.note_w(t_sg)
            t_r = p.recip(srow[0:1, 0:512], sgt[0:1, 0:512],
                          deps=[t_sg] + srow_r.wdeps())
            srow_r.note_w(t_r)
            sg_r.note_r(t_r)
            t_bcr = p.mm(pa0[:, 0:512], C["ones_row_f"][0:1, 0:128],
                         srow[0:1, 0:512], start=True, stop=True,
                         deps=[t_r] + pa0_r.wdeps())
            pa0_r.note_w(t_bcr)
            srow_r.note_r(t_bcr)
            avs, avs_r = av_sb
            t_ev = p.act(avs[:, :], avt[:, 0:512], AF.Copy,
                         deps=[t_av] + avs_r.wdeps())
            avs_r.note_w(t_ev)
            av_r.note_r(t_ev)
            t_mul = p.tt(attn[:, h * 1024 + qp * 512: h * 1024 + qp * 512 + 512],
                         avs[:, :], pa0[:, 0:512], ALU.mult,
                         deps=[t_ev, t_bcr] + attn_r.rdeps())
            avs_r.note_r(t_mul)
            pa0_r.note_r(t_mul)
            attn_r.note_r(t_mul)
    attn_r.note_w(t_mul)

    r1_store_toks = []

    def evac_o(mi, m, it, tt, pa, deps):
        eb, eb_r = r1tile[it % 2]
        tok = p.ts(eb[:, 0:tt], pa, bov(m), None, ALU.add, ALU.bypass,
                   deps=deps + eb_r.wdeps())
        eb_r.note_w(tok)
        rp, rp_r = rptile[it % 2]
        lr = p.dma(rp[:, 0:tt],
                   r1p_in[m * 128:(m + 1) * 128, it * 512:it * 512 + tt],
                   deps=rp_r.wdeps(), eng="sync")
        rp_r.note_w(lr)
        t2 = p.tt(eb[:, 0:tt], eb[:, 0:tt], rp[:, 0:tt], ALU.add,
                  deps=[tok, lr])
        eb_r.note_w(t2)
        rp_r.note_r(t2)
        d = p.dma(r1_out[m * 128:(m + 1) * 128, it * 512:it * 512 + tt],
                  eb[:, 0:tt], deps=[t2], eng="sync")
        eb_r.note_r(d)
        r1_store_toks.append(d)
        return t2

    emit_mm_block(p, C, None, woT, list(range(16)), attn, attn_r, 0, 1024,
                  wstage, None, evac_o)

    emit_ln(p, C, r1_out, 0, 1024, xn1, xn1_res, 0, xs=xs, xs_res=xs_res,
            src_deps=r1_store_toks[-2:])
    for c in range(NCH):
        d = p.dma(xn1_out[c * 128:(c + 1) * 128, :],
                  xn1[:, c * 1024:(c + 1) * 1024], deps=xn1_res.rdeps(),
                  eng="sync")
        xn1_res.note_r(d)

    def evac_v(mi, m, it, tt, pa, deps):
        eb, eb_r = ev_bf[it % 2]
        tok = p.act(eb[:, 0:tt], pa, AF.Copy, deps=deps + eb_r.wdeps())
        eb_r.note_w(tok)
        cv = p.copyv(v_sb[:, m * 1024 + it * 512: m * 1024 + it * 512 + tt],
                     eb[:, 0:tt], deps=[tok] + v_sb_r.rdeps())
        eb_r.note_r(cv)
        v_sb_r.note_r(cv)
        d = p.dma(v_out[m * 128:(m + 1) * 128, it * 512:it * 512 + tt],
                  eb[:, 0:tt], deps=[tok], eng="sync")
        eb_r.note_r(d)
        return cv

    emit_mm_block(p, C, None, wv1, list(range(16)), xn1, xn1_res, 0, 1024,
                  wstage, None, evac_v)
    v_sb_r.note_w(v_sb_r.reads[-1] if v_sb_r.reads else None)

    vhead[0][1].reads.extend(res_chain(h_res).reads)
    t_lam = p.act(lam_row[0:1, 0:16], beta_sb[0:1, 0:16], AF.Sigmoid,
                  deps=C["ct"] + lam_row_r.wdeps())
    lam_row_r.note_w(t_lam)
    pa0, pa0_r = P["acc"][0]
    for h in range(NH):
        t_bc = p.mm(pa0[:, 0:1], C["ones_row_f"][0:1, 0:128],
                    lam_row[0:1, h:h + 1], start=True, stop=True,
                    deps=[t_lam] + pa0_r.wdeps())
        pa0_r.note_w(t_bc)
        lam_row_r.note_r(t_bc)
        t_lc = p.copyv(lamcol[:, 0:1], pa0[:, 0:1],
                       deps=[t_bc] + lamcol_r.wdeps())
        lamcol_r.note_w(t_lc)
        pa0_r.note_r(t_lc)
        t_lb = p.ts(lam_big[:, :], ones_big[:, :], lamcol[:, 0:1], None,
                    ALU.mult, ALU.bypass,
                    deps=[t_lc, ct2] + lam_big_r.wdeps())
        lam_big_r.note_w(t_lb)
        lamcol_r.note_r(t_lb)
        t_scan = p.scan(st_sc[:, :], lam_big[:, :],
                        v_sb[:, h * 1024:(h + 1) * 1024], 0.0,
                        deps=[t_lb] + v_sb_r.rdeps() + st_sc_r.wdeps())
        st_sc_r.note_w(t_scan)
        lam_big_r.note_r(t_scan)
        v_sb_r.note_r(t_scan)
        t_cp = p.copyv(carry_sb[:, h:h + 1], st_sc[:, 1023:1024],
                       deps=[t_scan] + carry_sb_r.rdeps())
        st_sc_r.note_r(t_cp)
        carry_sb_r.note_r(t_cp)
    carry_sb_r.note_w(t_cp)
    carry_sb_r.reads = []
    trt, trt_r = P["tr32"]
    t_tr = p.trans(trt[0:16, 0:128], carry_sb[:, 0:16], C["idn_f"][:, :],
                   deps=[t_cp] + trt_r.wdeps())
    trt_r.note_w(t_tr)
    carry_sb_r.note_r(t_tr)
    t_ce = p.copyv(carry_ev[0:16, :], trt[0:16, 0:128],
                   deps=[t_tr] + carry_ev_r.wdeps())
    carry_ev_r.note_w(t_ce)
    trt_r.note_r(t_ce)
    d = p.dma(carry_out, carry_ev[0:16, :], deps=[t_ce], eng="sync")
    carry_ev_r.note_r(d)

    return p.build()


def run_launch2(inputs, l1res):
    x = np.asarray(inputs["L0_attn_x"])
    nc = build_launch2()
    w = {
        "woT": np.ascontiguousarray(inputs["L0_attn_w_o"].T),
        "bo_r": np.ascontiguousarray(inputs["L0_attn_b_o"].reshape(16, 128).T),
        "wv1": np.ascontiguousarray(inputs["L1_attn_Wv"]),
        "beta_in": np.ascontiguousarray(inputs["L1_attn_beta"].reshape(1, 16)),
        "c_ones_col": np.ones((128, 1), np.float32),
        "c_idn": np.eye(128, dtype=np.float32),
    }
    kf = {}
    vf = {}
    for b in range(2):
        kf[b] = np.concatenate([l1res[4 * b + g]["k_out"] for g in range(4)],
                               axis=1)
        vf[b] = np.concatenate([l1res[4 * b + g]["vt_out"] for g in range(4)],
                               axis=0)
    in_maps = []
    for c in range(8):
        b, g = c // 4, c % 4
        m = {"q_in": l1res[c]["q_out"], "k_full": kf[b], "vt_full": vf[b],
             "r1p_in": l1res[c]["r1p_out"]}
        m.update(w)
        in_maps.append(m)
    res = run_bass_kernel_spmd(nc, in_maps, core_ids=list(range(8)))
    return res.results





HSH = 8192
NE = 8


def build_launch3():
    p = Prog()
    r1_in = p.inp("r1_in", [D, 1024], F32)
    xn1_in = p.inp("xn1_in", [D, 1024], BF16)
    v_in = p.inp("v_in", [D, 1024], BF16)
    car_in = p.inp("car_in", [3, 16, 128], F32)
    wq1 = p.inp("wq1", [D, D], F32)
    wg1 = p.inp("wg1", [D, D], F32)
    wo1 = p.inp("wo1", [D, D], F32)
    beta_row = p.inp("beta_row", [1, 16], F32)
    beta_col = p.inp("beta_col", [16, 1], F32)
    lng_r = p.inp("lng_r", [128, 16], F32)
    lnb_r = p.inp("lnb_r", [128, 16], F32)
    w12T = p.inp("w12T", [NE, D, 2048], F32)
    w3T = p.inp("w3T", [NE, 1024, D], F32)
    routerT = p.inp("routerT", [D, 8], F32)
    ebias_col = p.inp("ebias_col", [8, 1], F32)
    sw1T = p.inp("sw1T", [D, HSH], F32)
    sw2T = p.inp("sw2T", [D, HSH], F32)
    sw3T = p.inp("sw3T", [HSH, D], F32)

    out_sl = p.outp("out_sl", [D, 1024], F32)
    comb_dram = p.dram("comb_dram", [8, 1024], F32)

    C = {}
    C["ones_col_f"] = p.sbuf("ones_col_f", [128, 1], F32)
    C["ones_col_bf"] = p.sbuf("ones_col_bf", [128, 1], BF16)
    C["ones_row_f"] = p.sbuf("ones_row_f", [1, 128], F32)
    C["idn_f"] = p.sbuf("idn_f", [128, 128], F32)
    co = p.inp("c_ones_col", [128, 1], F32)
    cidn = p.inp("c_idn", [128, 128], F32)
    ct = [p.dma(C["ones_col_f"][:, :], co, eng="sync"),
          p.dma(C["ones_col_bf"][:, :], co, eng="gpsimd"),
          p.dma(C["ones_row_f"][0:1, :], co.rearrange("p one -> one p"),
                eng="sync"),
          p.dma(C["idn_f"][:, :], cidn, eng="sync")]
    gbias = p.sbuf("gbias", [128, 32], F32)
    ct.append(p.dma(gbias[:, 0:16], lng_r, eng="sync"))
    ct.append(p.dma(gbias[:, 16:32], lnb_r, eng="sync"))
    gcol = lambda c: gbias[:, c:c + 1]
    bcol = lambda c: gbias[:, 16 + c:16 + c + 1]
    beta_sb = p.sbuf("beta_sb", [1, 16], F32)
    ct.append(p.dma(beta_sb[0:1, :], beta_row, eng="sync"))
    betac_sb = p.sbuf("betac_sb", [16, 1], F32)
    ct.append(p.dma(betac_sb[0:16, :], beta_col, eng="sync"))
    ebias_sb = p.sbuf("ebias_sb", [8, 1], F32)
    ct.append(p.dma(ebias_sb[0:8, :], ebias_col, eng="sync"))
    C["eps1"] = p.sbuf("eps1", [1, 1], F32)
    ct.append(p.memset(C["eps1"][0:1, 0:1], EPS))
    C["ct"] = ct
    for e in ENGINES:
        p.op(e, lambda eng: eng.nop(), deps=ct)

    P = {}
    P["acc"] = [(p.psum("accA", [128, 512], F32), Res()),
                (p.psum("accB", [128, 512], F32), Res())]
    P["bcA"] = (p.psum("bcA", [128, 512], F32), Res())
    P["bcB"] = (p.psum("bcB", [128, 512], F32), Res())
    P["sum"] = (p.psum("psumrow", [1, 512], F32), Res())
    P["sq"] = (p.psum("psqrow", [1, 512], F32), Res())
    P["tr"] = (p.psum("ptr", [128, 128], F32), Res())
    C["P"] = P

    xn1 = p.sbuf("xn1", [128, NCH * 1024], BF16)
    xn1_res = Res()
    vyn = p.sbuf("vyn", [128, NCH * 1024], BF16)
    vyn_res = Res()
    yz = p.sbuf("yz", [128, NCH * 1024], BF16)
    yz_res = Res()
    acc_rt = p.sbuf("acc_rt", [128, NCH * 512], F32)
    acc_rt_res = Res()
    C["sq_tile"] = (p.sbuf("sq_tile", [128, 512], F32), Res())
    C["ln_tmp"] = (p.sbuf("ln_tmp", [128, 512], F32), Res())
    rows = p.sbuf("rows", [1, 4 * 512], F32)
    C["rows"] = {nm: (rows[0:1, i * 512:(i + 1) * 512], Res())
                 for i, nm in enumerate(("mean", "var", "a", "b"))}
    wstage = [(p.sbuf("wsA", [128, NCH * 128], BF16), Res()),
              (p.sbuf("wsB", [128, NCH * 128], BF16), Res())]
    w3stage = [(p.sbuf("w3A", [128, 8 * 128], BF16), Res()),
               (p.sbuf("w3B", [128, 8 * 128], BF16), Res())]
    _sw3 = (p.sbuf("sw3A", [128, 32 * 128], BF16), Res())
    sw3stage = [_sw3, _sw3]
    g_t = (p.sbuf("g_t", [128, 512], F32), Res())
    u_t = (p.sbuf("u_t", [128, 512], F32), Res())
    o_t = [(p.sbuf("o_tA", [128, 512], F32), Res()),
           (p.sbuf("o_tB", [128, 512], F32), Res())]
    rtile = [(p.sbuf("rtA", [128, 512], F32), Res()),
             (p.sbuf("rtB", [128, 512], F32), Res())]
    hrt = p.sbuf("hrt", [128, 8 * 512], BF16)
    hrt_res = Res()
    st_sc = [(p.sbuf("st0", [128, 1024], F32), Res()),
             (p.sbuf("st1", [128, 1024], F32), Res())]
    lam_big = acc_rt[:, 1024:2048]
    lam_big_r = Res()
    ones_big = acc_rt[:, 0:1024]
    ct2 = p.memset(ones_big[:, :], 1.0)
    lam_row = p.sbuf("lam_row", [1, 16], F32)
    lam_row_r = Res()
    lamcol = p.sbuf("lamcol", [128, 1], F32)
    lamcol_r = Res()
    car_sb = p.sbuf("car_sb", [16, 3 * 128], F32)
    car_r = Res()
    lamc = p.sbuf("lamc", [16, 4], F32)
    lamc_r = Res()
    fac = p.sbuf("fac", [16, 3], F32)
    fac_r = Res()
    sinT = p.sbuf("sinT", [16, 128], F32)
    sinT_r = Res()
    s_in = p.sbuf("s_in", [128, 16], F32)
    s_in_r = Res()
    scr8 = A3[0:8, 0:1024]
    scr8_r = Res()
    sel8 = A3[0:8, 1024:2048]
    sel8_r = Res()
    selT = p.sbuf("selT", [128, 8], F32)
    selT_r = Res()
    scrT = p.sbuf("scrT", [128, 8], F32)
    scrT_r = Res()
    m8 = p.sbuf("m8", [128, 8], F32)
    m8_r = Res()
    msk = p.sbuf("msk", [128, 24], F32)
    msk_r = Res()
    s12 = p.sbuf("s12", [128, 2], F32)
    s12_r = Res()
    combT = p.sbuf("combT", [128, 8], F32)
    combT_r = Res()
    comb8 = A3[0:8, 2048:3072]
    comb8_r = Res()
    comb_row = p.sbuf("comb_row", [1, 512], F32)
    comb_row_r = Res()

    lds = []
    for c in range(NCH):
        lds.append(p.dma(xn1[:, c * 1024:(c + 1) * 1024],
                         xn1_in[c * 128:(c + 1) * 128, :], eng="sync"))
        lds.append(p.dma(vyn[:, c * 1024:(c + 1) * 1024],
                         v_in[c * 128:(c + 1) * 128, :], eng="sync"))
    xn1_res.note_w(lds[-2])
    vyn_res.note_w(lds[-1])

    lcar = p.dma(car_sb[0:16, :].rearrange("p (j dh) -> p j dh", dh=128),
                 car_in.rearrange("j p dh -> p j dh"), eng="sync")
    car_r.note_w(lcar)
    t_lamc = p.act(lamc[0:16, 0:1], betac_sb[0:16, 0:1], AF.Sigmoid,
                   deps=C["ct"] + lamc_r.wdeps())
    t_lnl = p.act(lamc[0:16, 1:2], lamc[0:16, 0:1], AF.Ln, deps=[t_lamc])
    lamc_r.note_w(t_lnl)
    facto = []
    for j in range(3):
        tok = p.act(fac[0:16, j:j + 1], lamc[0:16, 1:2], AF.Exp,
                    scale=float(1024 * (j + 1)),
                    deps=[t_lnl] + fac_r.wdeps())
        lamc_r.note_r(tok)
        facto.append(tok)
    fac_r.note_w(facto[-1])
    t_s0 = p.ts(sinT[0:16, :], car_sb[0:16, 0:128], fac[0:16, 0:1], None,
                ALU.mult, ALU.bypass,
                deps=[lcar, facto[0]] + sinT_r.wdeps())
    sinT_r.note_w(t_s0)
    car_r.note_r(t_s0)
    fac_r.note_r(t_s0)
    for j in range(1, 3):
        t_s0 = p.stt(sinT[0:16, :], car_sb[0:16, j * 128:(j + 1) * 128],
                     fac[0:16, j:j + 1], sinT[0:16, :], ALU.mult, ALU.add,
                     deps=[lcar, facto[j], t_s0])
        sinT_r.note_w(t_s0)
        car_r.note_r(t_s0)
        fac_r.note_r(t_s0)
    trt, trt_r = P["tr"]
    t_tr = p.trans(trt[0:128, 0:16], sinT[0:16, :], C["idn_f"][0:16, 0:16],
                   deps=[t_s0] + trt_r.wdeps() + C["ct"])
    trt_r.note_w(t_tr)
    sinT_r.note_r(t_tr)
    t_si = p.copyv(s_in[:, 0:16], trt[0:128, 0:16],
                   deps=[t_tr] + s_in_r.wdeps())
    s_in_r.note_w(t_si)
    trt_r.note_r(t_si)

    vhead[0][1].reads.extend(res_chain(h_res).reads)
    t_lam = p.act(lam_row[0:1, 0:16], beta_sb[0:1, 0:16], AF.Sigmoid,
                  deps=C["ct"] + lam_row_r.wdeps())
    lam_row_r.note_w(t_lam)

    for h in range(NH):
        pa0, pa0_r = P["bcA"]
        t_bc = p.mm(pa0[:, 0:1], C["ones_row_f"][0:1, 0:128],
                    lam_row[0:1, h:h + 1], start=True, stop=True,
                    deps=[t_lam] + pa0_r.wdeps())
        pa0_r.note_w(t_bc)
        lam_row_r.note_r(t_bc)
        t_lc = p.copyv(lamcol[:, 0:1], pa0[:, 0:1],
                       deps=[t_bc] + lamcol_r.wdeps())
        lamcol_r.note_w(t_lc)
        pa0_r.note_r(t_lc)
        t_lb = p.ts(lam_big[:, :], ones_big[:, :], lamcol[:, 0:1], None,
                    ALU.mult, ALU.bypass,
                    deps=[t_lc, ct2] + lam_big_r.wdeps())
        lam_big_r.note_w(t_lb)
        lamcol_r.note_r(t_lb)
        st, st_r = st_sc[h % 2]
        t_scan = p.scan(st[:, :], lam_big[:, :],
                        vyn[:, h * 1024:(h + 1) * 1024], s_in[:, h:h + 1],
                        deps=[t_lb, t_si] + vyn_res.rdeps() + st_r.wdeps())
        st_r.note_w(t_scan)
        lam_big_r.note_r(t_scan)
        vyn_res.note_r(t_scan)
        s_in_r.note_r(t_scan)
        ws, ws_r = wstage[h % 2]
        lw = p.dma(ws[:, 0:NCH * 128].rearrange("p (c m) -> p c m", m=128),
                   wq1[:, h * 128:(h + 1) * 128]
                   .rearrange("(c p) m -> p c m", p=128),
                   deps=ws_r.wdeps(), eng="gpsimd")
        ws_r.note_w(lw)
        for it in range(2):
            pq, pq_r = P["acc"][it % 2]
            for k in range(NCH):
                tok = p.mm(pq[:, 0:512], ws[:, k * 128:(k + 1) * 128],
                           xn1[:, k * 1024 + it * 512:k * 1024 + it * 512 + 512],
                           start=(k == 0), stop=(k == NCH - 1),
                           deps=[lw] + xn1_res.rdeps()
                           + (pq_r.wdeps() if k == 0 else []))
                ws_r.note_r(tok)
                xn1_res.note_r(tok)
            pq_r.note_w(tok)
            t_y = p.tt(yz[:, h * 1024 + it * 512: h * 1024 + it * 512 + 512],
                       pq[:, 0:512], st[:, it * 512:(it + 1) * 512], ALU.mult,
                       deps=[tok, t_scan] + yz_res.rdeps())
            pq_r.note_r(t_y)
            st_r.note_r(t_y)
            yz_res.note_r(t_y)
    y_done = t_y
    yz_res.note_w(t_y)

    vyn_res2 = Res()
    vyn_res2.reads = list(vyn_res.reads)
    emit_ln(p, C, None, 0, 1024, vyn, vyn_res2, 0,
            src_view=lambda c, it, tt: yz[:, c * 1024 + it * 512:
                                          c * 1024 + it * 512 + tt],
            src_deps=[y_done], gb=(gcol, bcol))

    for h in range(NCH):
        ws, ws_r = wstage[h % 2]
        lw = p.dma(ws[:, 0:NCH * 128].rearrange("p (c m) -> p c m", m=128),
                   wg1[:, h * 128:(h + 1) * 128]
                   .rearrange("(c p) m -> p c m", p=128),
                   deps=ws_r.wdeps(), eng="gpsimd")
        ws_r.note_w(lw)
        for it in range(2):
            pq, pq_r = P["acc"][it % 2]
            for k in range(NCH):
                tok = p.mm(pq[:, 0:512], ws[:, k * 128:(k + 1) * 128],
                           xn1[:, k * 1024 + it * 512:k * 1024 + it * 512 + 512],
                           start=(k == 0), stop=(k == NCH - 1),
                           deps=[lw] + xn1_res.rdeps()
                           + (pq_r.wdeps() if k == 0 else []))
                ws_r.note_r(tok)
                xn1_res.note_r(tok)
            pq_r.note_w(tok)
            gt, gt_r = g_t
            t_g = p.act(gt[:, :], pq[:, 0:512], AF.Silu,
                        deps=[tok] + gt_r.wdeps())
            gt_r.note_w(t_g)
            pq_r.note_r(t_g)
            t_z = p.tt(yz[:, h * 1024 + it * 512: h * 1024 + it * 512 + 512],
                       gt[:, :],
                       vyn[:, h * 1024 + it * 512: h * 1024 + it * 512 + 512],
                       ALU.mult,
                       deps=[t_g] + vyn_res2.rdeps() + yz_res.wdeps())
            gt_r.note_r(t_z)
            vyn_res2.note_r(t_z)
            yz_res.note_r(t_z)
    z_done = t_z
    yz_res.note_w(t_z)

    out_res = [[Res() for _ in range(2)] for _ in range(NCH)]

    def evac_o(mi, m, it, tt, pa, deps):
        ot, ot_r = o_t[it % 2]
        tok = p.act(ot[:, 0:tt], pa, AF.Copy, deps=deps + ot_r.wdeps())
        ot_r.note_w(tok)
        rp, rp_r = rtile[it % 2]
        lr = p.dma(rp[:, 0:tt],
                   r1_in[m * 128:(m + 1) * 128, it * 512:it * 512 + tt],
                   deps=rp_r.wdeps(), eng="sync")
        rp_r.note_w(lr)
        t2 = p.tt(ot[:, 0:tt], ot[:, 0:tt], rp[:, 0:tt], ALU.add,
                  deps=[tok, lr])
        ot_r.note_w(t2)
        rp_r.note_r(t2)
        d = p.dma(out_sl[m * 128:(m + 1) * 128, it * 512:it * 512 + tt],
                  ot[:, 0:tt], deps=[t2] + out_res[m][it].wdeps(), eng="sync")
        ot_r.note_r(d)
        out_res[m][it].note_w(d)
        return t2

    emit_mm_block(p, C, None, wo1, list(range(16)), yz, yz_res, 0, 1024,
                  wstage, None, evac_o)

    pa, pa_r = P["acc"][0]
    ws, ws_r = wstage[0]
    lw = p.dma(ws[:, 0:NCH * 8].rearrange("p (c m) -> p c m", m=8),
               routerT.rearrange("(c p) m -> p c m", p=128),
               deps=ws_r.wdeps(), eng="gpsimd")
    ws_r.note_w(lw)
    for it in range(2):
        for k in range(NCH):
            tok = p.mm(pa[0:8, 0:512], ws[:, k * 8:(k + 1) * 8],
                       xn1[:, k * 1024 + it * 512:k * 1024 + it * 512 + 512],
                       start=(k == 0), stop=(k == NCH - 1),
                       deps=[lw] + xn1_res.rdeps()
                       + (pa_r.wdeps() if k == 0 else []))
            ws_r.note_r(tok)
            xn1_res.note_r(tok)
        pa_r.note_w(tok)
        t_sc = p.act(scr8[0:8, it * 512:(it + 1) * 512], pa[0:8, 0:512],
                     AF.Sigmoid, deps=[tok] + scr8_r.wdeps())
        scr8_r.note_w(t_sc)
        pa_r.note_r(t_sc)
    t_sel = p.ts(sel8[0:8, :], scr8[0:8, :], ebias_sb[0:8, 0:1], None,
                 ALU.add, ALU.bypass,
                 deps=scr8_r.rdeps() + C["ct"] + sel8_r.wdeps())
    sel8_r.note_w(t_sel)
    scr8_r.note_r(t_sel)
    trt, trt_r = P["tr"]
    for ctk in range(8):
        t_t1 = p.trans(trt[0:128, 0:8], sel8[0:8, ctk * 128:(ctk + 1) * 128],
                       C["idn_f"][0:8, 0:8], deps=[t_sel] + trt_r.wdeps())
        trt_r.note_w(t_t1)
        sel8_r.note_r(t_t1)
        t_c1 = p.copyv(selT[:, 0:8], trt[0:128, 0:8],
                       deps=[t_t1] + selT_r.wdeps())
        selT_r.note_w(t_c1)
        trt_r.note_r(t_c1)
        t_t2 = p.trans(trt[0:128, 0:8], scr8[0:8, ctk * 128:(ctk + 1) * 128],
                       C["idn_f"][0:8, 0:8],
                       deps=scr8_r.rdeps() + trt_r.wdeps())
        trt_r.note_w(t_t2)
        scr8_r.note_r(t_t2)
        t_c2 = p.copyv(scrT[:, 0:8], trt[0:128, 0:8],
                       deps=[t_t2] + scrT_r.wdeps())
        scrT_r.note_w(t_c2)
        trt_r.note_r(t_c2)
        t_m8 = p.max8(m8[:, 0:8], selT[:, 0:8], deps=[t_c1] + m8_r.wdeps())
        m8_r.note_w(t_m8)
        selT_r.note_r(t_m8)
        t_k1 = p.ts(msk[:, 0:8], selT[:, 0:8], m8[:, 0:1], None,
                    ALU.is_equal, ALU.bypass, deps=[t_m8, t_c1] + msk_r.wdeps())
        t_k2 = p.ts(msk[:, 8:16], selT[:, 0:8], m8[:, 1:2], None,
                    ALU.is_equal, ALU.bypass, deps=[t_m8, t_c1, t_k1])
        t_k3 = p.tt(msk[:, 0:8], msk[:, 0:8], msk[:, 8:16], ALU.add,
                    deps=[t_k1, t_k2])
        t_num = p.tt(msk[:, 16:24], msk[:, 0:8], scrT[:, 0:8], ALU.mult,
                     deps=[t_k3, t_c2])
        msk_r.note_w(t_num)
        m8_r.note_r(t_k2)
        selT_r.note_r(t_k2)
        scrT_r.note_r(t_num)
        t_s12 = p.red(s12[:, 0:1], msk[:, 16:24], ALU.add,
                      deps=[t_num] + s12_r.wdeps())
        t_d = p.ts(s12[:, 1:2], s12[:, 0:1], 1e-9, None, ALU.add, ALU.bypass,
                   deps=[t_s12])
        t_rc = p.recip(s12[:, 1:2], s12[:, 1:2], deps=[t_d])
        s12_r.note_w(t_rc)
        msk_r.note_r(t_s12)
        t_cmb = p.ts(combT[:, 0:8], msk[:, 16:24], s12[:, 1:2], None,
                     ALU.mult, ALU.bypass,
                     deps=[t_rc, t_num] + combT_r.wdeps())
        combT_r.note_w(t_cmb)
        s12_r.note_r(t_cmb)
        msk_r.note_r(t_cmb)
        t_t3 = p.trans(trt[0:8, 0:128], combT[:, 0:8], C["idn_f"][:, :],
                       deps=[t_cmb] + trt_r.wdeps())
        trt_r.note_w(t_t3)
        combT_r.note_r(t_t3)
        t_c3 = p.copyv(comb8[0:8, ctk * 128:(ctk + 1) * 128], trt[0:8, 0:128],
                       deps=[t_t3] + comb8_r.rdeps())
        comb8_r.note_r(t_c3)
        trt_r.note_r(t_c3)
    comb8_r.note_w(t_c3)
    d1 = p.dma(comb_dram, comb8[0:8, :], deps=comb8_r.rdeps(), eng="sync")
    comb8_r.note_r(d1)
    comb_done = d1

    hsh_res = Res()
    hsh_res.reads = list(yz_res.reads)
    NTT = 4
    for it in range(NTT):
        for m in range(HSH // 128):
            wsg, wsg_r = wstage[m % 2]
            lg = p.dma(wsg[:, 0:NCH * 128].rearrange("p (c m) -> p c m", m=128),
                       sw1T[:, m * 128:(m + 1) * 128]
                       .rearrange("(c p) m -> p c m", p=128),
                       deps=wsg_r.wdeps(), eng="gpsimd")
            wsg_r.note_w(lg)
            wsu, wsu_r = w3stage[m % 2]
            wsu, wsu_r = sw3stage[m % 2]
            lu = p.dma(wsu[:, 0:NCH * 128].rearrange("p (c m) -> p c m", m=128),
                       sw2T[:, m * 128:(m + 1) * 128]
                       .rearrange("(c p) m -> p c m", p=128),
                       deps=wsu_r.wdeps(), eng="gpsimd")
            wsu_r.note_w(lu)
            pa, pa_r = P["acc"][0]
            pb, pb_r = P["acc"][1]
            for k in range(NCH):
                tok = p.mm(pa[:, 0:256], wsg[:, k * 128:(k + 1) * 128],
                           xn1[:, k * 1024 + it * 256:k * 1024 + it * 256 + 256],
                           start=(k == 0), stop=(k == NCH - 1),
                           deps=[lg] + xn1_res.rdeps()
                           + (pa_r.wdeps() if k == 0 else []))
                wsg_r.note_r(tok)
                xn1_res.note_r(tok)
            pa_r.note_w(tok)
            tg = tok
            for k in range(NCH):
                tok = p.mm(pb[:, 0:256], wsu[:, k * 128:(k + 1) * 128],
                           xn1[:, k * 1024 + it * 256:k * 1024 + it * 256 + 256],
                           start=(k == 0), stop=(k == NCH - 1),
                           deps=[lu] + xn1_res.rdeps()
                           + (pb_r.wdeps() if k == 0 else []))
                wsu_r.note_r(tok)
                xn1_res.note_r(tok)
            pb_r.note_w(tok)
            tu = tok
            gt, gt_r = g_t
            ut, ut_r = u_t
            t1 = p.act(gt[:, 0:256], pa[:, 0:256], AF.Silu,
                       deps=[tg] + gt_r.wdeps())
            gt_r.note_w(t1)
            pa_r.note_r(t1)
            t2 = p.act(ut[:, 0:256], pb[:, 0:256], AF.Copy,
                       deps=[tu] + ut_r.wdeps())
            ut_r.note_w(t2)
            pb_r.note_r(t2)
            t3 = p.tt(yz[:, m * 256:(m + 1) * 256], gt[:, 0:256],
                      ut[:, 0:256], ALU.mult,
                      deps=[t1, t2] + hsh_res.wdeps())
            gt_r.note_r(t3)
            ut_r.note_r(t3)
            hsh_res.note_r(t3)
        hsh_res.note_w(t3)
        hsh_res.reads = []
        h_last = t3
        for m2 in range(NCH):
            wd, wd_r = sw3stage[m2 % 2]
            lw3 = p.dma(wd[:, 0:32 * 128]
                        .rearrange("p (c m) -> p c m", m=128),
                        sw3T[0:HSH // 2, m2 * 128:(m2 + 1) * 128]
                        .rearrange("(c p) m -> p c m", p=128),
                        deps=wd_r.wdeps(), eng="gpsimd")
            wd_r.note_w(lw3)
            pa, pa_r = P["acc"][m2 % 2]
            for k in range(32):
                tok = p.mm(pa[:, 0:256], wd[:, k * 128:(k + 1) * 128],
                           yz[:, k * 256:(k + 1) * 256],
                           start=(k == 0), stop=False,
                           deps=[lw3, h_last]
                           + (pa_r.wdeps() if k == 0 else []))
                wd_r.note_r(tok)
                hsh_res.note_r(tok)
            lw3b = p.dma(wd[:, 0:32 * 128]
                         .rearrange("p (c m) -> p c m", m=128),
                         sw3T[HSH // 2:HSH, m2 * 128:(m2 + 1) * 128]
                         .rearrange("(c p) m -> p c m", p=128),
                         deps=wd_r.wdeps(), eng="gpsimd")
            wd_r.note_w(lw3b)
            for k in range(32):
                tok = p.mm(pa[:, 0:256], wd[:, k * 128:(k + 1) * 128],
                           yz[:, (32 + k) * 256:(33 + k) * 256],
                           start=False, stop=(k == 31),
                           deps=[lw3b, h_last])
                wd_r.note_r(tok)
                hsh_res.note_r(tok)
            pa_r.note_w(tok)
            ot, ot_r = o_t[m2 % 2]
            rp, rp_r = rtile[m2 % 2]
            lr = p.dma(rp[:, 0:256],
                       out_sl[m2 * 128:(m2 + 1) * 128,
                              it * 256:(it + 1) * 256],
                       deps=rp_r.wdeps() + out_res[m2][it // 2].rdeps(),
                       eng="sync")
            rp_r.note_w(lr)
            t4 = p.tt(ot[:, 0:256], rp[:, 0:256], pa[:, 0:256], ALU.add,
                      deps=[lr, tok] + ot_r.wdeps())
            ot_r.note_w(t4)
            rp_r.note_r(t4)
            pa_r.note_r(t4)
            d = p.dma(out_sl[m2 * 128:(m2 + 1) * 128,
                             it * 256:(it + 1) * 256], ot[:, 0:256],
                      deps=[t4] + out_res[m2][it // 2].wdeps(), eng="sync")
            ot_r.note_r(d)
            out_res[m2][it // 2].note_w(d)

    for it in range(2):
        zt = p.memset(acc_rt[:, :], 0.0,
                      deps=acc_rt_res.wdeps() + [y_done])
        acc_rt_res.note_w(zt)
        for e in range(NE):
            lcr = p.dma(comb_row[0:1, 0:512],
                        comb_dram[e:e + 1, it * 512:(it + 1) * 512],
                        deps=[comb_done] + comb_row_r.wdeps(), eng="sync")
            comb_row_r.note_w(lcr)
            pc, pc_r = P["bcB"]
            t_cb = p.mm(pc[:, 0:512], C["ones_row_f"][0:1, 0:128],
                        comb_row[0:1, 0:512],
                        start=True, stop=True,
                        deps=[lcr] + pc_r.wdeps())
            pc_r.note_w(t_cb)
            comb_row_r.note_r(t_cb)
            for m in range(8):
                wsg, wsg_r = wstage[m % 2]
                lg = p.dma(wsg[:, 0:NCH * 128]
                           .rearrange("p (c m) -> p c m", m=128),
                           w12T[e, :, m * 128:(m + 1) * 128]
                           .rearrange("(c p) m -> p c m", p=128),
                           deps=wsg_r.wdeps(), eng="gpsimd")
                wsg_r.note_w(lg)
                wsu, wsu_r = sw3stage[m % 2]
                lu = p.dma(wsu[:, 0:NCH * 128]
                           .rearrange("p (c m) -> p c m", m=128),
                           w12T[e, :, 1024 + m * 128:1024 + (m + 1) * 128]
                           .rearrange("(c p) m -> p c m", p=128),
                           deps=wsu_r.wdeps(), eng="gpsimd")
                wsu_r.note_w(lu)
                pa, pa_r = P["acc"][0]
                pb, pb_r = P["acc"][1]
                for k in range(NCH):
                    tok = p.mm(pa[:, 0:512], wsg[:, k * 128:(k + 1) * 128],
                               xn1[:, k * 1024 + it * 512:
                                   k * 1024 + it * 512 + 512],
                               start=(k == 0), stop=(k == NCH - 1),
                               deps=[lg] + xn1_res.rdeps()
                               + (pa_r.wdeps() if k == 0 else []))
                    wsg_r.note_r(tok)
                    xn1_res.note_r(tok)
                pa_r.note_w(tok)
                tg = tok
                for k in range(NCH):
                    tok = p.mm(pb[:, 0:512], wsu[:, k * 128:(k + 1) * 128],
                               xn1[:, k * 1024 + it * 512:
                                   k * 1024 + it * 512 + 512],
                               start=(k == 0), stop=(k == NCH - 1),
                               deps=[lu] + xn1_res.rdeps()
                               + (pb_r.wdeps() if k == 0 else []))
                    wsu_r.note_r(tok)
                    xn1_res.note_r(tok)
                pb_r.note_w(tok)
                tu = tok
                gt, gt_r = g_t
                ut, ut_r = u_t
                t1 = p.act(gt[:, :], pa[:, 0:512], AF.Silu,
                           deps=[tg] + gt_r.wdeps())
                gt_r.note_w(t1)
                pa_r.note_r(t1)
                t2 = p.act(ut[:, :], pb[:, 0:512], AF.Copy,
                           deps=[tu] + ut_r.wdeps())
                ut_r.note_w(t2)
                pb_r.note_r(t2)
                t3 = p.tt(gt[:, :], gt[:, :], ut[:, :], ALU.mult,
                          deps=[t1, t2])
                gt_r.note_w(t3)
                ut_r.note_r(t3)
                t4 = p.tt(hrt[:, m * 512:(m + 1) * 512], gt[:, :],
                          pc[:, 0:512], ALU.mult,
                          deps=[t3, t_cb] + hrt_res.wdeps())
                gt_r.note_r(t4)
                pc_r.note_r(t4)
                hrt_res.note_r(t4)
            hrt_res.note_w(t4)
            hrt_res.reads = []
            hrt_last = t4
            for m2 in range(NCH):
                wd, wd_r = w3stage[m2 % 2]
                lw3 = p.dma(wd[:, 0:8 * 128]
                            .rearrange("p (c m) -> p c m", m=128),
                            w3T[e, :, m2 * 128:(m2 + 1) * 128]
                            .rearrange("(c p) m -> p c m", p=128),
                            deps=wd_r.wdeps(), eng="gpsimd")
                wd_r.note_w(lw3)
                pa, pa_r = P["acc"][m2 % 2]
                for k in range(8):
                    tok = p.mm(pa[:, 0:512], wd[:, k * 128:(k + 1) * 128],
                               hrt[:, k * 512:(k + 1) * 512],
                               start=(k == 0), stop=(k == 7),
                               deps=[lw3, hrt_last]
                               + (pa_r.wdeps() if k == 0 else []))
                    wd_r.note_r(tok)
                    hrt_res.note_r(tok)
                pa_r.note_w(tok)
                t5 = p.tt(acc_rt[:, m2 * 512:(m2 + 1) * 512],
                          acc_rt[:, m2 * 512:(m2 + 1) * 512], pa[:, 0:512],
                          ALU.add, deps=[tok, zt] + acc_rt_res.wdeps())
                acc_rt_res.note_r(t5)
                pa_r.note_r(t5)
        acc_rt_res.note_w(t5)
        for m2 in range(NCH):
            ot, ot_r = o_t[m2 % 2]
            rp, rp_r = rtile[m2 % 2]
            lr = p.dma(rp[:, 0:512],
                       out_sl[m2 * 128:(m2 + 1) * 128,
                              it * 512:(it + 1) * 512],
                       deps=rp_r.wdeps() + out_res[m2][it].rdeps(),
                       eng="sync")
            rp_r.note_w(lr)
            t6 = p.tt(ot[:, 0:512], rp[:, 0:512],
                      acc_rt[:, m2 * 512:(m2 + 1) * 512], ALU.add,
                      deps=[lr, t5] + ot_r.wdeps())
            ot_r.note_w(t6)
            rp_r.note_r(t6)
            acc_rt_res.note_r(t6)
            d = p.dma(out_sl[m2 * 128:(m2 + 1) * 128,
                             it * 512:(it + 1) * 512], ot[:, 0:512],
                      deps=[t6] + out_res[m2][it].wdeps(), eng="sync")
            ot_r.note_r(d)
            out_res[m2][it].note_w(d)

    return p.build()


def run_launch3(inputs, l2res):
    nc = build_launch3()
    w = {
        "wq1": np.ascontiguousarray(inputs["L1_attn_Wq"]),
        "wg1": np.ascontiguousarray(inputs["L1_attn_Wg"]),
        "wo1": np.ascontiguousarray(inputs["L1_attn_Wo"]),
        "beta_row": np.ascontiguousarray(
            inputs["L1_attn_beta"].reshape(1, 16)),
        "beta_col": np.ascontiguousarray(
            inputs["L1_attn_beta"].reshape(16, 1)),
        "lng_r": np.ascontiguousarray(
            inputs["L1_attn_ln_g"].reshape(16, 128).T),
        "lnb_r": np.ascontiguousarray(
            inputs["L1_attn_ln_b"].reshape(16, 128).T),
        "w12T": np.ascontiguousarray(inputs["L1_ffn_w12"].transpose(0, 2, 1)),
        "w3i": np.ascontiguousarray(
            inputs["L1_ffn_w3"].transpose(0, 2, 1).reshape(NE * 1024, D)),
        "routerT": np.ascontiguousarray(inputs["L1_ffn_router_w"].T),
        "ebias_col": np.ascontiguousarray(
            inputs["L1_ffn_expert_bias"].reshape(8, 1)),
        "sw1T": np.ascontiguousarray(inputs["L1_ffn_sw1"].T),
        "sw2T": np.ascontiguousarray(inputs["L1_ffn_sw2"].T),
        "sw3T": np.ascontiguousarray(inputs["L1_ffn_sw3"].T),
        "c_ones_col": np.ones((128, 1), np.float32),
        "c_idn": np.eye(128, dtype=np.float32),
    }
    in_maps = []
    for c in range(8):
        b, g = c // 4, c % 4
        car = np.zeros((3, 16, 128), np.float32)
        for j in range(3):
            gp = g - 1 - j
            if gp >= 0:
                car[j] = l2res[4 * b + gp]["carry_out"]
        m = {"r1_in": l2res[c]["r1_out"], "xn1_in": l2res[c]["xn1_out"],
             "v_in": l2res[c]["v_out"], "car_in": car}
        m.update(w)
        in_maps.append(m)
    res = run_bass_kernel_spmd(nc, in_maps, core_ids=list(range(8)))
    return res.results


def kernel(**inputs):
    inputs = {k: np.asarray(v) for k, v in inputs.items()}
    l1 = run_launch1(inputs)
    l2 = run_launch2(inputs, l1)
    l3 = run_launch3(inputs, l2)
    outs = []
    for b in range(2):
        outs.append(np.concatenate(
            [l3[4 * b + g]["out_sl"] for g in range(4)], axis=1))
    out_fm = np.stack(outs, axis=0)
    return np.ascontiguousarray(out_fm.transpose(0, 2, 1))





HT = 2048


def res_chain(*olds):
    r = Res()
    for o in olds:
        if o is None:
            continue
        if o.w is not None:
            r.reads.append(o.w)
        r.reads.extend(o.reads)
    return r


def build_fused():
    p = Prog()
    x_half = p.inp("x_half", [D, HT], F32)
    parcol = p.inp("parcol", [16, 1], F32)
    nonce = p.inp("nonce", [1, 4], I32)
    wqkvT = p.inp("wqkvT", [D, 3 * D], F32)
    bqkv_r = p.inp("bqkv_r", [128, 48], F32)
    wguT = p.inp("wguT", [D, 2 * INTER], F32)
    wdT = p.inp("wdT", [INTER, D], F32)
    gs_r = p.inp("gs_r", [128, 44], F32)
    us_r = p.inp("us_r", [128, 44], F32)
    ds_r = p.inp("ds_r", [128, 16], F32)
    woT = p.inp("woT", [D, D], F32)
    bo_r = p.inp("bo_r", [128, 16], F32)
    wv1 = p.inp("wv1", [D, D], F32)
    wq1 = p.inp("wq1", [D, D], F32)
    wg1 = p.inp("wg1", [D, D], F32)
    wo1 = p.inp("wo1", [D, D], F32)
    beta_row = p.inp("beta_row", [1, 16], F32)
    lng_r = p.inp("lng_r", [128, 16], F32)
    lnb_r = p.inp("lnb_r", [128, 16], F32)
    w12T = p.inp("w12T", [8, D, 2048], F32)
    w3i = p.inp("w3i", [NE * 1024, D], F32)
    routerT = p.inp("routerT", [D, 8], F32)
    ebias_col = p.inp("ebias_col", [8, 1], F32)
    swguT = p.inp("swguT", [D, 2 * HSH], F32)
    sw3T = p.inp("sw3T", [HSH, D], F32)
    co = p.inp("c_ones_col", [128, 1], F32)
    cidn = p.inp("c_idn", [128, 128], F32)

    out_sl = p.outp("out_sl", [D, 1024], F32)

    K_sh = p.dram("K_sh", [D, S], BF16, addr_space="Shared")
    Vt_sh = p.dram("Vt_sh", [S, D], BF16, addr_space="Shared")
    car_sh = p.dram("car_sh", [2, 16, 128], F32, addr_space="Shared")
    kvflag = p.dram("kvflag", [2, 128], I32, addr_space="Shared")
    carflag = p.dram("carflag", [2, 128], I32, addr_space="Shared")
    q_dram = p.dram("q_dram", [D, HT], BF16)
    r1_dram = p.dram("r1_dram", [D, HT], F32)
    xn1_dram = p.dram("xn1_dram", [D, HT], BF16)
    v_dram = p.dram("v_dram", [D, HT], BF16)
    comb_dram = p.dram("comb_dram", [8, 1024], F32)

    C = {}
    C["ones_col_f"] = p.sbuf("ones_col_f", [128, 1], F32)
    C["ones_col_bf"] = p.sbuf("ones_col_bf", [128, 1], BF16)
    C["ones_row_f"] = p.sbuf("ones_row_f", [1, 128], F32)
    C["idn_f"] = p.sbuf("idn_f", [128, 128], F32)
    C["idn_bf"] = p.sbuf("idn_bf", [128, 128], BF16)
    ct = [p.dma(C["ones_col_f"][:, :], co, eng="sync"),
          p.dma(C["ones_col_bf"][:, :], co, eng="gpsimd"),
          p.dma(C["ones_row_f"][0:1, :], co.rearrange("p one -> one p"),
                eng="sync"),
          p.dma(C["idn_f"][:, :], cidn, eng="sync"),
          p.dma(C["idn_bf"][:, :], cidn, eng="gpsimd")]
    biases = p.sbuf("biases", [128, 200], F32)
    boff = {}
    _o = 0
    for nm, ap_, w_ in (("bqkv", bqkv_r, 48), ("gs", gs_r, 44),
                        ("us", us_r, 44), ("ds", ds_r, 16),
                        ("bo", bo_r, 16), ("lng", lng_r, 16),
                        ("lnb", lnb_r, 16)):
        ct.append(p.dma(biases[:, _o:_o + w_], ap_, eng="sync"))
        boff[nm] = _o
        _o += w_
    bview = lambda nm, m: biases[:, boff[nm] + m: boff[nm] + m + 1]
    beta_sb = p.sbuf("beta_sb", [1, 16], F32)
    ct.append(p.dma(beta_sb[0:1, :], beta_row, eng="sync"))
    parc_sb = p.sbuf("parc_sb", [16, 1], F32)
    ct.append(p.dma(parc_sb[0:16, :], parcol, eng="sync"))
    ebias_sb = p.sbuf("ebias_sb", [8, 1], F32)
    ct.append(p.dma(ebias_sb[0:8, :], ebias_col, eng="sync"))
    C["eps1"] = p.sbuf("eps1", [1, 1], F32)
    ct.append(p.memset(C["eps1"][0:1, 0:1], EPS))
    C["ct"] = ct
    for e in ENGINES:
        p.op(e, lambda eng: eng.nop(), deps=ct)

    P = {}
    P["acc"] = [(p.psum("accA", [128, 512], F32), Res()),
                (p.psum("accB", [128, 512], F32), Res())]
    P["bcA"] = (p.psum("bcA", [128, 512], F32), Res())
    P["bcB"] = (p.psum("bcB", [128, 512], F32), Res())
    P["sum"] = (p.psum("psumrow", [1, 512], F32), Res())
    P["sq"] = (p.psum("psqrow", [1, 512], F32), Res())
    P["tr"] = (p.psum("ptr", [128, 128], F32), Res())
    P["trbf"] = (p.psum("ptrbf", [128, 512], BF16), Res())
    C["P"] = P

    A1 = p.sbuf("A1", [128, 32768], BF16)
    A2 = p.sbuf("A2", [128, 22528], BF16)
    A3 = p.sbuf("A3", [128, 8192], F32)
    A3bf = A3.bitcast(BF16)
    wgustage = [(A3bf[:, 4096:8192], Res()), (A3bf[:, 8192:12288], Res()),
                (A3bf[:, 12288:16384], Res())]
    _stc = (p.sbuf("stcA", [128, 1024], F32), Res())
    stc = [_stc, _stc]
    C["sq_tile"] = (p.sbuf("sq_tile", [128, 512], F32), Res())
    C["ln_tmp"] = (p.sbuf("ln_tmp", [128, 512], F32), Res())
    rows = p.sbuf("rows", [1, 4 * 512], F32)
    C["rows"] = {nm: (rows[0:1, i * 512:(i + 1) * 512], Res())
                 for i, nm in enumerate(("mean", "var", "a", "b"))}
    wstage = [(p.sbuf("wsA", [128, NCH * 128], BF16), Res()),
              (p.sbuf("wsB", [128, NCH * 128], BF16), Res())]
    _big = p.sbuf("bigA", [128, 44 * 128], BF16)
    bigstage = (_big, Res())
    bigust = [(_big[:, 0:2048], Res()), (_big[:, 2048:4096], Res())]
    bighalf = [(_big[:, 0:2816], Res()), (_big[:, 2816:5632], Res())]
    w3stage = [(p.sbuf("w3A", [128, 8 * 128], BF16), Res()),
               (p.sbuf("w3B", [128, 8 * 128], BF16), Res())]
    ev_bf = [(p.sbuf("evbfA", [128, 512], BF16), Res()),
             (p.sbuf("evbfB", [128, 512], BF16), Res())]
    vt_ev = [(p.sbuf("vtevA", [128, 128], BF16), Res()),
             (p.sbuf("vtevB", [128, 128], BF16), Res())]
    g_t = (p.sbuf("g_t", [128, 512], F32), Res())
    u_t = (p.sbuf("u_t", [128, 512], F32), Res())
    o_t = [(p.sbuf("o_tA", [128, 512], F32), Res()),
           (p.sbuf("o_tB", [128, 512], F32), Res())]
    rtile = [(p.sbuf("rtA", [128, 512], F32), Res()),
             (p.sbuf("rtB", [128, 512], F32), Res())]
    _xre = (p.sbuf("xreA", [128, 512], F32), Res())
    xre = [_xre, _xre]
    sqq = p.sbuf("sqq", [128, 512], BF16)
    sqq_r = Res()
    probs = [(p.sbuf("probsA", [128, 512], BF16), Res()),
             (p.sbuf("probsB", [128, 512], BF16), Res())]
    av_sb = (A3[:, 1536:2048], Res())
    rowbuf = A3[0:1, 0:1024]
    rowbuf_r = Res()
    tiny = p.sbuf("tiny", [1, 16], F32)
    tiny_r = Res()
    cbias = p.sbuf("cbias", [128, 1], F32)
    cbias_r = Res()
    srow = A3[0:1, 1024:1536]
    srow_r = Res()
    lam_row = p.sbuf("lam_row", [1, 16], F32)
    lam_row_r = Res()
    lamcol = p.sbuf("lamcol", [128, 1], F32)
    lamcol_r = Res()
    carry_sb = p.sbuf("carry_sb", [128, 16], F32)
    carry_sb_r = Res()
    carry_ev = p.sbuf("carry_ev", [16, 128], F32)
    carry_ev_r = Res()
    car0 = p.sbuf("car0", [16, 128], F32)
    car0_r = Res()
    sinT = p.sbuf("sinT", [16, 128], F32)
    sinT_r = Res()
    s_in = p.sbuf("s_in", [128, 16], F32)
    s_in_r = Res()
    scr8 = A3[0:8, 0:1024]
    scr8_r = Res()
    sel8 = A3[0:8, 1024:2048]
    sel8_r = Res()
    selT = p.sbuf("selT", [128, 8], F32)
    selT_r = Res()
    scrT = p.sbuf("scrT", [128, 8], F32)
    scrT_r = Res()
    m8 = p.sbuf("m8", [128, 8], F32)
    m8_r = Res()
    msk = p.sbuf("msk", [128, 24], F32)
    msk_r = Res()
    s12 = p.sbuf("s12", [128, 2], F32)
    s12_r = Res()
    combT = p.sbuf("combT", [128, 8], F32)
    combT_r = Res()
    comb8 = A3[0:8, 2048:3072]
    comb8_r = Res()
    comb_row = p.sbuf("comb_row", [1, 512], F32)
    comb_row_r = Res()
    _vh = (A2[:, 20480:22528], Res())
    vhead = [_vh, _vh]

    SC = 1.0 / np.sqrt(128.0)

    xn0_res = Res()
    xs_res = Res()
    xs_pair = [(A3bf[:, 0:8192], Res()), (A3bf[:, 8192:16384], Res())]
    C["ab_bf"] = (p.sbuf("ab_bf", [128, 1024], BF16), Res())
    C["ln_tmp_bf"] = (p.sbuf("ln_tmp_bf", [128, 512], BF16), Res())
    emit_ln(p, C, x_half, 0, HT, A1, xn0_res, 0, xs=xs_pair, xs_res=None)

    kv_store_toks = []

    def evac_qkv(mi, m, it, tt, pa, deps):
        eb, eb_r = ev_bf[it % 2]
        tok = p.ts(eb[:, 0:tt], pa, bview("bqkv", m), None, ALU.add,
                   ALU.bypass, deps=deps + eb_r.wdeps())
        eb_r.note_w(tok)
        if m < 16:
            d = p.dma(q_dram[m * 128:(m + 1) * 128, it * 512:it * 512 + tt],
                      eb[:, 0:tt], deps=[tok], eng="sync")
            eb_r.note_r(d)
            kv_store_toks.append(d)
        elif m < 32:
            mm = m - 16
            d = p.dma2(K_sh[mm * 128:(mm + 1) * 128, it * 512:it * 512 + tt],
                       eb[:, 0:tt],
                       K_sh[mm * 128:(mm + 1) * 128,
                            2048 + it * 512:2048 + it * 512 + tt],
                       eb[:, 0:tt], deps=[tok], eng="sync")
            eb_r.note_r(d)
            kv_store_toks.append(d)
        else:
            mm = m - 32
            trp, trp_r = P["trbf"]
            for j in range(cdiv(tt, 128)):
                tj = p.trans(trp[:, j * 128:(j + 1) * 128],
                             eb[:, j * 128:(j + 1) * 128], C["idn_bf"][:, :],
                             deps=[tok] + trp_r.wdeps())
                eb_r.note_r(tj)
                trp_r.note_w(tj)
                ve, ve_r = vt_ev[j % 2]
                cv = p.copyv(ve[:, :], trp[:, j * 128:(j + 1) * 128],
                             deps=[tj] + ve_r.wdeps())
                ve_r.note_w(cv)
                trp_r.note_r(cv)
                r0 = it * 512 + j * 128
                d = p.dma2(Vt_sh[r0:r0 + 128, mm * 128:(mm + 1) * 128],
                           ve[:, :],
                           Vt_sh[2048 + r0:2048 + r0 + 128,
                                 mm * 128:(mm + 1) * 128],
                           ve[:, :], deps=[cv], eng="sync")
                ve_r.note_r(d)
                kv_store_toks.append(d)
        return tok

    emit_mm_block(p, C, None, wqkvT, list(range(48)), A1, xn0_res, 0, HT,
                  wstage, None, evac_qkv, kchunk=HT)
    fkv = p.dma2(kvflag[0:1, 0:4], nonce, kvflag[1:2, 0:4], nonce,
                 deps=kv_store_toks, eng="sync", ch="flags")

    for _j in range(3):
        wgustage[_j][1].reads.extend(xs_pair[0][1].wdeps())
        wgustage[_j][1].reads.extend(xs_pair[1][1].wdeps())
    h_res = Res()
    r1w = [[Res() for _ in range(4)] for _ in range(NCH)]
    for it in range(4):
        for _j in range(2):
            bigust[_j][1].reads.extend(bighalf[0][1].wdeps())
            bigust[_j][1].reads.extend(bighalf[1][1].wdeps())
        for m in range(44):
            wsg, wsg_r = wgustage[m % 3]
            lg = p.dma(wsg[:, 0:NCH * 256].rearrange("p (c m) -> p c m", m=256),
                       wguT[:, m * 256:(m + 1) * 256]
                       .rearrange("(c p) m -> p c m", p=128),
                       deps=wsg_r.wdeps(), eng="gpsimd", ch=f"wgu{m % 3}")
            wsg_r.note_w(lg)
            lu = lg
            pa, pa_r = P["acc"][0]
            pb, pb_r = P["acc"][1]
            for k in range(NCH):
                tok = p.mm(pa[:, 0:512], wsg[:, k * 256:k * 256 + 128],
                           A1[:, k * HT + it * 512: k * HT + it * 512 + 512],
                           start=(k == 0), stop=(k == NCH - 1),
                           deps=[lg] + xn0_res.rdeps()
                           + (pa_r.wdeps() if k == 0 else []))
                wsg_r.note_r(tok)
                xn0_res.note_r(tok)
            pa_r.note_w(tok)
            tg = tok
            for k in range(NCH):
                tok = p.mm(pb[:, 0:512], wsg[:, k * 256 + 128:(k + 1) * 256],
                           A1[:, k * HT + it * 512: k * HT + it * 512 + 512],
                           start=(k == 0), stop=(k == NCH - 1),
                           deps=[lu] + xn0_res.rdeps()
                           + (pb_r.wdeps() if k == 0 else []))
                wsg_r.note_r(tok)
                xn0_res.note_r(tok)
            pb_r.note_w(tok)
            tu = tok
            gt, gt_r = g_t
            ut, ut_r = u_t
            t1 = p.act(gt[:, :], pa[:, 0:512], AF.Silu, scale=bview("gs", m),
                       deps=[tg] + gt_r.wdeps())
            gt_r.note_w(t1)
            pa_r.note_r(t1)
            t2 = p.act(ut[:, :], pb[:, 0:512], AF.Copy, scale=bview("us", m),
                       deps=[tu] + ut_r.wdeps())
            ut_r.note_w(t2)
            pb_r.note_r(t2)
            t3 = p.tt(A2[:, m * 512:(m + 1) * 512], gt[:, :], ut[:, :],
                      ALU.mult, deps=[t1, t2] + h_res.wdeps())
            gt_r.note_r(t3)
            ut_r.note_r(t3)
            h_res.note_r(t3)
        h_res.note_w(t3)
        h_res.reads = []
        h_last = t3
        for _j in range(2):
            bighalf[_j][1].reads.extend(bigust[0][1].wdeps())
            bighalf[_j][1].reads.extend(bigust[1][1].wdeps())
        for m2 in range(NCH):
            wda, wda_r = bighalf[0]
            wdb, wdb_r = bighalf[1]
            lwa = p.dma(wda[:, 0:22 * 128].rearrange("p (c m) -> p c m", m=128),
                        wdT[0:2816, m2 * 128:(m2 + 1) * 128]
                        .rearrange("(c p) m -> p c m", p=128),
                        deps=wda_r.wdeps(), eng="gpsimd", ch="wda")
            wda_r.note_w(lwa)
            lwb = p.dma(wdb[:, 0:22 * 128].rearrange("p (c m) -> p c m", m=128),
                        wdT[2816:5632, m2 * 128:(m2 + 1) * 128]
                        .rearrange("(c p) m -> p c m", p=128),
                        deps=wdb_r.wdeps(), eng="gpsimd", ch="wdb")
            wdb_r.note_w(lwb)
            pa, pa_r = P["acc"][m2 % 2]
            for k in range(44):
                if k < 22:
                    wd, wd_r, lw = wda, wda_r, lwa
                    kk = k
                else:
                    wd, wd_r, lw = wdb, wdb_r, lwb
                    kk = k - 22
                tok = p.mm(pa[:, 0:512], wd[:, kk * 128:(kk + 1) * 128],
                           A2[:, k * 512:(k + 1) * 512],
                           start=(k == 0), stop=(k == 43),
                           deps=[lw, h_last]
                           + (pa_r.wdeps() if k == 0 else []))
                wd_r.note_r(tok)
                h_res.note_r(tok)
            pa_r.note_w(tok)
            dt, dt_r = o_t[m2 % 2]
            t1 = p.act(dt[:, :], pa[:, 0:512], AF.Copy, scale=bview("ds", m2),
                       deps=[tok] + dt_r.wdeps())
            dt_r.note_w(t1)
            pa_r.note_r(t1)
            xr, xr_r = xre[m2 % 2]
            lx = p.dma(xr[:, :],
                       x_half[m2 * 128:(m2 + 1) * 128,
                              it * 512:(it + 1) * 512],
                       deps=xr_r.wdeps(), eng="sync")
            xr_r.note_w(lx)
            rt, rt_r = rtile[m2 % 2]
            t2 = p.tt(rt[:, :], dt[:, :], xr[:, :], ALU.add,
                      deps=[t1, lx] + rt_r.wdeps())
            rt_r.note_w(t2)
            dt_r.note_r(t2)
            xr_r.note_r(t2)
            d = p.dma(r1_dram[m2 * 128:(m2 + 1) * 128,
                              it * 512:(it + 1) * 512], rt[:, :],
                      deps=[t2] + r1w[m2][it].wdeps(), eng="sync")
            rt_r.note_r(d)
            r1w[m2][it].note_w(d)

    pk0 = p.poll(kvflag[0:1, 0:1], nonce[0:1, 0:1], deps=[fkv], eng="sync")
    pk1 = p.poll(kvflag[1:2, 0:1], nonce[0:1, 0:1], deps=[fkv], eng="sync")
    attn_res = res_chain(xn0_res)
    for _r in (av_sb[1], rowbuf_r, srow_r):
        _r.reads.extend(xs_res.wdeps())
    kv_arena = res_chain(h_res)
    kh_r = res_chain()
    vh_r = res_chain()
    qh_r = res_chain()
    kh_r.reads = list(kv_arena.reads)
    vh_r.reads = list(kv_arena.reads)
    qh_r.reads = list(kv_arena.reads)
    sgt, sg_r = P["sum"]
    kh = A2[:, 0:4096]
    vh = A2[:, 4096:8192]
    qh = A2[:, 8192:10240]
    for h in range(NH):
        lk = p.dma(kh, K_sh[h * 128:(h + 1) * 128, :],
                   deps=kh_r.wdeps() + [pk0, pk1], eng="sync", ch="kh")
        kh_r.note_w(lk)
        lv = p.dma(vh.rearrange("p (c dh) -> p c dh", dh=128),
                   Vt_sh[:, h * 128:(h + 1) * 128]
                   .rearrange("(c p) dh -> p c dh", p=128),
                   deps=vh_r.wdeps() + [pk0, pk1], eng="sync", ch="vh")
        vh_r.note_w(lv)
        lq = p.dma(qh, q_dram[h * 128:(h + 1) * 128, :],
                   deps=qh_r.wdeps(), eng="sync", ch="qh")
        qh_r.note_w(lq)
        def emit_norms(src_fn, npieces, tiny_off, base_deps):
            outs = []
            for piece in range(npieces):
                ap_, dps = src_fn(piece)
                t_sqp = p.act(sqq[:, 0:512], ap_, AF.Square,
                              deps=dps + base_deps + sqq_r.wdeps())
                sqq_r.note_w(t_sqp)
                tok = p.mm(sgt[0:1, 0:512], C["ones_col_bf"][0:128, 0:1],
                           sqq[:, 0:512], start=True, stop=True,
                           deps=[t_sqp] + sg_r.wdeps())
                sg_r.note_w(tok)
                sqq_r.note_r(tok)
                cv = p.copyv(rowbuf[0:1, (piece % 2) * 512:
                                    ((piece % 2) + 1) * 512],
                             sgt[0:1, 0:512], deps=[tok] + rowbuf_r.wdeps())
                sg_r.note_r(cv)
                rowbuf_r.note_w(cv)
                if piece % 2 == 1:
                    ti = tiny_off + piece // 2
                    t_m = p.red(tiny[0:1, ti:ti + 1], rowbuf[0:1, 0:1024],
                                ALU.max,
                                deps=rowbuf_r.rdeps() + tiny_r.wdeps())
                    rowbuf_r.note_r(t_m)
                    outs.append(t_m)
            return outs

        qms = emit_norms(lambda pc: (qh[:, pc * 512:(pc + 1) * 512], [lq]),
                         4, 0, [])
        for t in qms:
            qh_r.note_r(t)
        t_mq2 = p.red(tiny[0:1, 8:9], tiny[0:1, 0:2], ALU.max, deps=qms)
        kms = emit_norms(lambda pc: (kh[:, pc * 512:(pc + 1) * 512], [lk]),
                         8, 2, [t_mq2])
        for t in kms:
            kh_r.note_r(t)
        t_mk4 = p.red(tiny[0:1, 9:10], tiny[0:1, 2:6], ALU.max, deps=kms)
        t_c1 = p.tt(tiny[0:1, 10:11], tiny[0:1, 8:9], tiny[0:1, 9:10],
                    ALU.mult, deps=[t_mq2, t_mk4])
        t_c2 = p.act(tiny[0:1, 11:12], tiny[0:1, 10:11], AF.Sqrt,
                     scale=1.0 / 128, deps=[t_c1])
        t_c3 = p.ts(tiny[0:1, 11:12], tiny[0:1, 11:12], -1.0, None, ALU.mult,
                    ALU.bypass, deps=[t_c2])
        tiny_r.note_w(t_c3)
        trt, trt_r = P["tr"]
        t_bc = p.mm(trt[0:128, 0:1], C["ones_row_f"][0:1, 0:128],
                    tiny[0:1, 11:12], start=True, stop=True,
                    deps=[t_c3] + trt_r.wdeps())
        trt_r.note_w(t_bc)
        tiny_r.note_r(t_bc)
        t_cb = p.copyv(cbias[:, 0:1], trt[0:128, 0:1],
                       deps=[t_bc] + cbias_r.wdeps())
        cbias_r.note_w(t_cb)
        trt_r.note_r(t_cb)
        scores_banks = [P["bcA"], P["acc"][1]]
        avt, av_r = P["bcB"]
        for qp in range(4):
            pend = None
            for kc in range(33):
                if kc < 32:
                    spt, sp_r = scores_banks[kc % 2]
                    t_s = p.mm(spt[:, 0:512], kh[:, kc * 128:(kc + 1) * 128],
                               qh[:, qp * 512:(qp + 1) * 512],
                               start=True, stop=True,
                               deps=[lk, lq] + sp_r.wdeps())
                    sp_r.note_w(t_s)
                    kh_r.note_r(t_s)
                    qh_r.note_r(t_s)
                    pb, pb_r = probs[kc % 2]
                    t_e = p.act(pb[:, :], spt[:, 0:512], AF.Exp,
                                bias=cbias[:, 0:1], scale=SC,
                                deps=[t_s, t_cb] + pb_r.wdeps())
                    pb_r.note_w(t_e)
                    sp_r.note_r(t_e)
                    cbias_r.note_r(t_e)
                if pend is not None:
                    kp, t_ep, pbp, pbp_r = pend
                    t_av = p.mm(avt[:, 0:512], vh[:, kp * 128:(kp + 1) * 128],
                                pbp[:, :], start=(kp == 0), stop=(kp == 31),
                                deps=[lv, t_ep]
                                + (av_r.wdeps() if kp == 0 else []))
                    pbp_r.note_r(t_av)
                    vh_r.note_r(t_av)
                    t_sg = p.mm(sgt[0:1, 0:512], C["ones_col_bf"][0:128, 0:1],
                                pbp[:, :], start=(kp == 0), stop=(kp == 31),
                                deps=[t_ep] + (sg_r.wdeps() if kp == 0 else []))
                    pbp_r.note_r(t_sg)
                pend = (kc, t_e, pb, pb_r) if kc < 32 else None
            av_r.note_w(t_av)
            sg_r.note_w(t_sg)
            t_cps = p.copyv(srow[0:1, 0:512], sgt[0:1, 0:512],
                            deps=[t_sg] + srow_r.wdeps())
            sg_r.note_r(t_cps)
            t_r = p.recip(srow[0:1, 0:512], srow[0:1, 0:512], deps=[t_cps])
            srow_r.note_w(t_r)
            pa0, pa0_r = P["acc"][0]
            t_bcr = p.mm(pa0[:, 0:512], C["ones_row_f"][0:1, 0:128],
                         srow[0:1, 0:512], start=True, stop=True,
                         deps=[t_r] + pa0_r.wdeps())
            pa0_r.note_w(t_bcr)
            srow_r.note_r(t_bcr)
            avs, avs_r = av_sb
            t_ev = p.act(avs[:, :], avt[:, 0:512], AF.Copy,
                         deps=[t_av] + avs_r.wdeps())
            avs_r.note_w(t_ev)
            av_r.note_r(t_ev)
            t_mul = p.tt(A1[:, h * HT + qp * 512: h * HT + qp * 512 + 512],
                         avs[:, :], pa0[:, 0:512], ALU.mult,
                         deps=[t_ev, t_bcr] + attn_res.wdeps())
            avs_r.note_r(t_mul)
            pa0_r.note_r(t_mul)
            attn_res.note_r(t_mul)
    attn_res.note_w(t_mul)
    attn_res.reads = []
    xs_res.note_r(t_mul)
    for _r in (av_sb[1], rowbuf_r, srow_r):
        for _t in _r.wdeps():
            xs_res.note_r(_t)

    def evac_o(mi, m, it, tt, pa, deps):
        ot, ot_r = o_t[it % 2]
        tok = p.ts(ot[:, 0:tt], pa, bview("bo", m), None, ALU.add, ALU.bypass,
                   deps=deps + ot_r.wdeps())
        ot_r.note_w(tok)
        rp, rp_r = rtile[it % 2]
        lr = p.dma(rp[:, 0:tt],
                   r1_dram[m * 128:(m + 1) * 128, it * 512:it * 512 + tt],
                   deps=rp_r.wdeps() + r1w[m][it].rdeps(), eng="sync")
        rp_r.note_w(lr)
        t2 = p.tt(ot[:, 0:tt], ot[:, 0:tt], rp[:, 0:tt], ALU.add,
                  deps=[tok, lr])
        ot_r.note_w(t2)
        rp_r.note_r(t2)
        d = p.dma(r1_dram[m * 128:(m + 1) * 128, it * 512:it * 512 + tt],
                  ot[:, 0:tt], deps=[t2] + r1w[m][it].wdeps(), eng="sync")
        ot_r.note_r(d)
        r1w[m][it].note_w(d)
        return t2

    emit_mm_block(p, C, None, woT, list(range(16)), A1, attn_res, 0, HT,
                  wstage, None, evac_o, kchunk=HT)

    xn1h_res = res_chain(attn_res)
    r1_deps = [r1w[m][it].rdeps()[0] for m in range(NCH) for it in range(4)]
    for _x in xs_pair:
        _x[1].reads.extend(av_sb[1].wdeps() + rowbuf_r.wdeps()
                           + srow_r.wdeps())
        for _j in range(3):
            _x[1].reads.extend(wgustage[_j][1].wdeps())
    emit_ln(p, C, r1_dram, 0, HT, A1, xn1h_res, 0,
            xs=xs_pair, xs_res=None, src_deps=r1_deps)
    for c in range(NCH):
        d = p.dma(xn1_dram[c * 128:(c + 1) * 128, :],
                  A1[:, c * HT:(c + 1) * HT], deps=xn1h_res.rdeps(),
                  eng="sync", ch="xn1st")
        xn1h_res.note_r(d)

    t_lam = p.act(lam_row[0:1, 0:16], beta_sb[0:1, 0:16], AF.Sigmoid,
                  deps=C["ct"] + lam_row_r.wdeps())
    lam_row_r.note_w(t_lam)
    ones_big = A3[:, 0:2048]
    lam_big = A3[:, 2048:4096]
    a3_res = res_chain(xs_pair[0][1], xs_pair[1][1], C["sq_tile"][1],
                       wgustage[0][1], wgustage[1][1])
    st_half_pre = wgustage[2][1]
    ct2 = p.memset(ones_big, 1.0, deps=a3_res.wdeps())
    a3_res.note_w(ct2)
    lam_big_r = Res()
    st_half = [(A3[:, 4096:6144], Res()), (A3[:, 6144:8192], Res())]
    st_half[0][1].reads = [ct2]
    st_half[1][1].reads = [ct2] + st_half_pre.wdeps()

    def lam_prep(h):
        trt, trt_r = P["tr"]
        t_bc = p.mm(trt[0:128, 0:1], C["ones_row_f"][0:1, 0:128],
                    lam_row[0:1, h:h + 1], start=True, stop=True,
                    deps=[t_lam] + trt_r.wdeps())
        trt_r.note_w(t_bc)
        lam_row_r.note_r(t_bc)
        t_lc = p.copyv(lamcol[:, 0:1], trt[0:128, 0:1],
                       deps=[t_bc] + lamcol_r.wdeps())
        lamcol_r.note_w(t_lc)
        trt_r.note_r(t_lc)
        t_lb = p.ts(lam_big, ones_big, lamcol[:, 0:1], None,
                    ALU.mult, ALU.bypass,
                    deps=[t_lc, ct2] + lam_big_r.wdeps())
        lam_big_r.note_w(t_lb)
        lamcol_r.note_r(t_lb)
        return t_lb

    vres, vres_r = vhead[0]
    vres_r.reads.extend(res_chain(h_res).reads)

    def evac_v(mi, m, it, tt, pa, deps):
        tok = p.act(vres[:, it * 512:it * 512 + tt], pa, AF.Copy,
                    deps=deps + (vres_r.wdeps() if it == 0 else []))
        vres_r.note_w(tok)
        d = p.dma(v_dram[m * 128:(m + 1) * 128, it * 512:it * 512 + tt],
                  vres[:, it * 512:it * 512 + tt], deps=[tok], eng="sync",
                  ch="vst")
        vres_r.note_r(d)
        if it == 3:
            t_lb = lam_prep(m)
            st, st_r = st_half[m % 2]
            t_scan = p.scan(st, lam_big, vres[:, :], 0.0,
                            deps=[t_lb, tok] + st_r.wdeps())
            st_r.note_w(t_scan)
            lam_big_r.note_r(t_scan)
            vres_r.note_r(t_scan)
            t_cp = p.copyv(carry_sb[:, m:m + 1], st[:, HT - 1:HT],
                           deps=[t_scan] + carry_sb_r.rdeps())
            st_r.note_r(t_cp)
            carry_sb_r.note_r(t_cp)
            p._last_cp = t_cp
        return tok

    emit_mm_block(p, C, None, wv1, list(range(16)), A1, xn1h_res, 0, HT,
                  wstage, None, evac_v, kchunk=HT)
    t_cp = p._last_cp
    carry_sb_r.note_w(t_cp)
    carry_sb_r.reads = []
    trt, trt_r = P["tr"]
    t_tr = p.trans(trt[0:16, 0:128], carry_sb[:, 0:16], C["idn_f"][:, :],
                   deps=[t_cp] + trt_r.wdeps())
    trt_r.note_w(t_tr)
    carry_sb_r.note_r(t_tr)
    t_ce = p.copyv(carry_ev[0:16, :], trt[0:16, 0:128],
                   deps=[t_tr] + carry_ev_r.wdeps())
    carry_ev_r.note_w(t_ce)
    trt_r.note_r(t_ce)
    dcar = p.dma2(car_sh[0, :, :], carry_ev[0:16, :],
                  car_sh[1, :, :], carry_ev[0:16, :],
                  deps=[t_ce], eng="sync")
    carry_ev_r.note_r(dcar)
    fcar = p.dma2(carflag[0:1, 0:4], nonce, carflag[1:2, 0:4], nonce,
                  deps=[dcar], eng="sync", ch="flags")

    xn1_res = res_chain(xn1h_res)
    xn1st_last = xn1h_res.reads[-1] if xn1h_res.reads else None
    for c in range(NCH):
        d = p.dma2(A1[:, c * 1024:(c + 1) * 1024],
                   xn1_dram[c * 128:(c + 1) * 128, 0:1024],
                   A1[:, c * 1024:(c + 1) * 1024],
                   xn1_dram[c * 128:(c + 1) * 128, 1024:2048],
                   deps=xn1_res.wdeps() + [xn1st_last], eng="sync",
                   ch="xn1can", bit="off")
        xn1_res.note_w(d)
    xn1_can_last = d

    pcar = p.poll(carflag[0:1, 0:1], nonce[0:1, 0:1], deps=[fcar], eng="sync")
    lc0 = p.dma(car0[0:16, :], car_sh[0, :, :], deps=[pcar] + car0_r.wdeps(),
                eng="sync")
    car0_r.note_w(lc0)
    t_s0 = p.ts(sinT[0:16, :], car0[0:16, :], parc_sb[0:16, 0:1], None,
                ALU.mult, ALU.bypass,
                deps=[lc0] + C["ct"] + sinT_r.wdeps())
    sinT_r.note_w(t_s0)
    car0_r.note_r(t_s0)
    trt, trt_r = P["tr"]
    t_tr2 = p.trans(trt[0:128, 0:16], sinT[0:16, :], C["idn_f"][0:16, 0:16],
                    deps=[t_s0] + trt_r.wdeps())
    trt_r.note_w(t_tr2)
    sinT_r.note_r(t_tr2)
    t_si = p.copyv(s_in[:, 0:16], trt[0:128, 0:16],
                   deps=[t_tr2] + s_in_r.wdeps())
    s_in_r.note_w(t_si)
    trt_r.note_r(t_si)

    yz = A1[:, 16384:32768]
    yz_res = Res()
    for h in range(NH):
        t_lb = lam_prep(h)
        vhd, vhd_r = vhead[h % 2]
        lvh = p.dma(vhd[:, :], v_dram[h * 128:(h + 1) * 128, :],
                    deps=vhd_r.wdeps(), eng="sync", ch="vhead")
        vhd_r.note_w(lvh)
        st, st_r = st_half[h % 2]
        t_scan = p.scan(st, lam_big, vhd[:, :], s_in[:, h:h + 1],
                        deps=[t_lb, lvh, t_si] + st_r.wdeps())
        st_r.note_w(t_scan)
        lam_big_r.note_r(t_scan)
        vhd_r.note_r(t_scan)
        s_in_r.note_r(t_scan)
        sc, sc_r = stc[h % 2]
        dsc = p.dma2(sc[:, :], st[:, 0:1024], sc[:, :], st[:, 1024:2048],
                     deps=sc_r.wdeps() + [t_scan], eng="sync",
                     ch=f"stc{h % 2}", bit="off")
        sc_r.note_w(dsc)
        st_r.note_r(dsc)
        ws, ws_r = wstage[h % 2]
        lw = p.dma(ws[:, 0:NCH * 128].rearrange("p (c m) -> p c m", m=128),
                   wq1[:, h * 128:(h + 1) * 128]
                   .rearrange("(c p) m -> p c m", p=128),
                   deps=ws_r.wdeps(), eng="gpsimd")
        ws_r.note_w(lw)
        for it in range(2):
            pq, pq_r = P["acc"][it % 2]
            for k in range(NCH):
                tok = p.mm(pq[:, 0:512], ws[:, k * 128:(k + 1) * 128],
                           A1[:, k * 1024 + it * 512:k * 1024 + it * 512 + 512],
                           start=(k == 0), stop=(k == NCH - 1),
                           deps=[lw, xn1_can_last]
                           + (pq_r.wdeps() if k == 0 else []))
                ws_r.note_r(tok)
                xn1_res.note_r(tok)
            pq_r.note_w(tok)
            t_y = p.tt(yz[:, h * 1024 + it * 512: h * 1024 + it * 512 + 512],
                       pq[:, 0:512], sc[:, it * 512:(it + 1) * 512], ALU.mult,
                       deps=[tok, dsc] + yz_res.rdeps())
            pq_r.note_r(t_y)
            sc_r.note_r(t_y)
            yz_res.note_r(t_y)
    y_done = t_y
    yz_res.note_w(t_y)
    yz_res.reads = []

    yn_res = res_chain(kh_r, vh_r, qh_r)
    yn = A2[:, 0:16384]
    emit_ln(p, C, None, 0, 1024, A2, yn_res, 0,
            src_view=lambda c, it, tt: yz[:, c * 1024 + it * 512:
                                          c * 1024 + it * 512 + tt],
            src_deps=[y_done],
            gb=(lambda c: bview("lng", c), lambda c: bview("lnb", c)))

    for h in range(NCH):
        ws, ws_r = wstage[h % 2]
        lw = p.dma(ws[:, 0:NCH * 128].rearrange("p (c m) -> p c m", m=128),
                   wg1[:, h * 128:(h + 1) * 128]
                   .rearrange("(c p) m -> p c m", p=128),
                   deps=ws_r.wdeps(), eng="gpsimd")
        ws_r.note_w(lw)
        for it in range(2):
            pq, pq_r = P["acc"][it % 2]
            for k in range(NCH):
                tok = p.mm(pq[:, 0:512], ws[:, k * 128:(k + 1) * 128],
                           A1[:, k * 1024 + it * 512:k * 1024 + it * 512 + 512],
                           start=(k == 0), stop=(k == NCH - 1),
                           deps=[lw, xn1_can_last]
                           + (pq_r.wdeps() if k == 0 else []))
                ws_r.note_r(tok)
                xn1_res.note_r(tok)
            pq_r.note_w(tok)
            gt, gt_r = g_t
            t_g = p.act(gt[:, :], pq[:, 0:512], AF.Silu,
                        deps=[tok] + gt_r.wdeps())
            gt_r.note_w(t_g)
            pq_r.note_r(t_g)
            t_z = p.tt(yz[:, h * 1024 + it * 512: h * 1024 + it * 512 + 512],
                       gt[:, :],
                       yn[:, h * 1024 + it * 512: h * 1024 + it * 512 + 512],
                       ALU.mult,
                       deps=[t_g] + yn_res.rdeps() + yz_res.wdeps())
            gt_r.note_r(t_z)
            yn_res.note_r(t_z)
            yz_res.note_r(t_z)
    z_done = t_z
    yz_res.note_w(t_z)
    yz_res.reads = []

    out_res = [[Res() for _ in range(2)] for _ in range(NCH)]

    def evac_o1(mi, m, it, tt, pa, deps):
        ot, ot_r = o_t[it % 2]
        tok = p.act(ot[:, 0:tt], pa, AF.Copy, deps=deps + ot_r.wdeps())
        ot_r.note_w(tok)
        rp, rp_r = rtile[it % 2]
        lr = p.dma2(rp[:, 0:tt],
                    r1_dram[m * 128:(m + 1) * 128, it * 512:it * 512 + tt],
                    rp[:, 0:tt],
                    r1_dram[m * 128:(m + 1) * 128,
                            1024 + it * 512:1024 + it * 512 + tt],
                    deps=rp_r.wdeps() + r1w[m][it].rdeps()
                    + r1w[m][it + 2].rdeps(), eng="sync",
                    ch=f"rt{it % 2}", bit="off")
        rp_r.note_w(lr)
        t2 = p.tt(ot[:, 0:tt], ot[:, 0:tt], rp[:, 0:tt], ALU.add,
                  deps=[tok, lr])
        ot_r.note_w(t2)
        rp_r.note_r(t2)
        d = p.dma(out_sl[m * 128:(m + 1) * 128, it * 512:it * 512 + tt],
                  ot[:, 0:tt], deps=[t2] + out_res[m][it].wdeps(), eng="sync")
        ot_r.note_r(d)
        out_res[m][it].note_w(d)
        return t2

    emit_mm_block(p, C, None, wo1, list(range(16)), A1, yz_res, 16384, 1024,
                  wstage, None, evac_o1, kchunk=1024)

    for _r in (scr8_r, sel8_r, comb8_r):
        _r.reads.append(y_done)
        _r.reads.extend(a3_res.wdeps())
    pa, pa_r = P["acc"][0]
    ws, ws_r = wstage[0]
    lw = p.dma(ws[:, 0:NCH * 8].rearrange("p (c m) -> p c m", m=8),
               routerT.rearrange("(c p) m -> p c m", p=128),
               deps=ws_r.wdeps(), eng="gpsimd")
    ws_r.note_w(lw)
    for it in range(2):
        for k in range(NCH):
            tok = p.mm(pa[0:8, 0:512], ws[:, k * 8:(k + 1) * 8],
                       A1[:, k * 1024 + it * 512:k * 1024 + it * 512 + 512],
                       start=(k == 0), stop=(k == NCH - 1),
                       deps=[lw, xn1_can_last]
                       + (pa_r.wdeps() if k == 0 else []))
            ws_r.note_r(tok)
            xn1_res.note_r(tok)
        pa_r.note_w(tok)
        t_sc = p.act(scr8[0:8, it * 512:(it + 1) * 512], pa[0:8, 0:512],
                     AF.Sigmoid, deps=[tok] + scr8_r.wdeps())
        scr8_r.note_w(t_sc)
        pa_r.note_r(t_sc)
    t_sel = p.ts(sel8[0:8, :], scr8[0:8, :], ebias_sb[0:8, 0:1], None,
                 ALU.add, ALU.bypass,
                 deps=scr8_r.rdeps() + C["ct"] + sel8_r.wdeps())
    sel8_r.note_w(t_sel)
    scr8_r.note_r(t_sel)
    trt, trt_r = P["tr"]
    for ctk in range(8):
        t_t1 = p.trans(trt[0:128, 0:8], sel8[0:8, ctk * 128:(ctk + 1) * 128],
                       C["idn_f"][0:8, 0:8], deps=[t_sel] + trt_r.wdeps())
        trt_r.note_w(t_t1)
        sel8_r.note_r(t_t1)
        t_c1 = p.copyv(selT[:, 0:8], trt[0:128, 0:8],
                       deps=[t_t1] + selT_r.wdeps())
        selT_r.note_w(t_c1)
        trt_r.note_r(t_c1)
        t_t2 = p.trans(trt[0:128, 0:8], scr8[0:8, ctk * 128:(ctk + 1) * 128],
                       C["idn_f"][0:8, 0:8],
                       deps=scr8_r.rdeps() + trt_r.wdeps())
        trt_r.note_w(t_t2)
        scr8_r.note_r(t_t2)
        t_c2 = p.copyv(scrT[:, 0:8], trt[0:128, 0:8],
                       deps=[t_t2] + scrT_r.wdeps())
        scrT_r.note_w(t_c2)
        trt_r.note_r(t_c2)
        t_m8 = p.max8(m8[:, 0:8], selT[:, 0:8], deps=[t_c1] + m8_r.wdeps())
        m8_r.note_w(t_m8)
        selT_r.note_r(t_m8)
        t_k1 = p.ts(msk[:, 0:8], selT[:, 0:8], m8[:, 0:1], None,
                    ALU.is_equal, ALU.bypass,
                    deps=[t_m8, t_c1] + msk_r.wdeps())
        t_k2 = p.ts(msk[:, 8:16], selT[:, 0:8], m8[:, 1:2], None,
                    ALU.is_equal, ALU.bypass, deps=[t_m8, t_c1, t_k1])
        t_k3 = p.tt(msk[:, 0:8], msk[:, 0:8], msk[:, 8:16], ALU.add,
                    deps=[t_k1, t_k2])
        t_num = p.tt(msk[:, 16:24], msk[:, 0:8], scrT[:, 0:8], ALU.mult,
                     deps=[t_k3, t_c2])
        msk_r.note_w(t_num)
        m8_r.note_r(t_k2)
        selT_r.note_r(t_k2)
        scrT_r.note_r(t_num)
        t_s12 = p.red(s12[:, 0:1], msk[:, 16:24], ALU.add,
                      deps=[t_num] + s12_r.wdeps())
        t_d = p.ts(s12[:, 1:2], s12[:, 0:1], 1e-9, None, ALU.add, ALU.bypass,
                   deps=[t_s12])
        t_rc = p.recip(s12[:, 1:2], s12[:, 1:2], deps=[t_d])
        s12_r.note_w(t_rc)
        msk_r.note_r(t_s12)
        t_cmb = p.ts(combT[:, 0:8], msk[:, 16:24], s12[:, 1:2], None,
                     ALU.mult, ALU.bypass,
                     deps=[t_rc, t_num] + combT_r.wdeps())
        combT_r.note_w(t_cmb)
        s12_r.note_r(t_cmb)
        msk_r.note_r(t_cmb)
        t_t3 = p.trans(trt[0:8, 0:128], combT[:, 0:8], C["idn_f"][:, :],
                       deps=[t_cmb] + trt_r.wdeps())
        trt_r.note_w(t_t3)
        combT_r.note_r(t_t3)
        t_c3 = p.copyv(comb8[0:8, ctk * 128:(ctk + 1) * 128], trt[0:8, 0:128],
                       deps=[t_t3] + comb8_r.rdeps())
        comb8_r.note_r(t_c3)
        trt_r.note_r(t_c3)
    comb8_r.note_w(t_c3)
    comb_done = p.dma(comb_dram, comb8[0:8, :], deps=comb8_r.rdeps(),
                      eng="sync", ch="comb8")
    comb8_r.note_r(comb_done)

    for _j in range(2):
        bigust[_j][1].reads.extend(bighalf[0][1].wdeps())
        bigust[_j][1].reads.extend(bighalf[1][1].wdeps())
    for _j in range(3):
        wgustage[_j][1].reads.append(comb_done)
        wgustage[_j][1].reads.extend(st_half[0][1].wdeps())
        wgustage[_j][1].reads.extend(st_half[1][1].wdeps())
        wgustage[_j][1].reads.extend(lam_big_r.wdeps())
    hsh_res = res_chain(yn_res)
    hsh2_res = res_chain(yz_res)

    def hview(k):
        if k < 32:
            return A2[:, k * 512:(k + 1) * 512]
        return A1[:, 16384 + (k - 32) * 512:16384 + (k - 31) * 512]

    for it in range(2):
        for m in range(HSH // 128):
            wsg, wsg_r = wgustage[m % 3]
            lg = p.dma(wsg[:, 0:NCH * 256].rearrange("p (c m) -> p c m", m=256),
                       swguT[:, m * 256:(m + 1) * 256]
                       .rearrange("(c p) m -> p c m", p=128),
                       deps=wsg_r.wdeps(), eng="gpsimd", ch=f"wgu{m % 3}")
            wsg_r.note_w(lg)
            lu = lg
            pa, pa_r = P["acc"][0]
            pb, pb_r = P["bcA"]
            for k in range(NCH):
                tok = p.mm(pa[:, 0:512], wsg[:, k * 256:k * 256 + 128],
                           A1[:, k * 1024 + it * 512:k * 1024 + it * 512 + 512],
                           start=(k == 0), stop=(k == NCH - 1),
                           deps=[lg, xn1_can_last]
                           + (pa_r.wdeps() if k == 0 else []))
                wsg_r.note_r(tok)
                xn1_res.note_r(tok)
            pa_r.note_w(tok)
            tg = tok
            for k in range(NCH):
                tok = p.mm(pb[:, 0:512], wsg[:, k * 256 + 128:(k + 1) * 256],
                           A1[:, k * 1024 + it * 512:k * 1024 + it * 512 + 512],
                           start=(k == 0), stop=(k == NCH - 1),
                           deps=[lu, xn1_can_last]
                           + (pb_r.wdeps() if k == 0 else []))
                wsg_r.note_r(tok)
                xn1_res.note_r(tok)
            pb_r.note_w(tok)
            tu = tok
            gt, gt_r = g_t
            ut, ut_r = u_t
            t1 = p.act(gt[:, 0:512], pa[:, 0:512], AF.Silu,
                       deps=[tg] + gt_r.wdeps())
            gt_r.note_w(t1)
            pa_r.note_r(t1)
            t2 = p.act(ut[:, 0:512], pb[:, 0:512], AF.Copy,
                       deps=[tu] + ut_r.wdeps())
            ut_r.note_w(t2)
            pb_r.note_r(t2)
            hres_m = hsh_res if m < 32 else hsh2_res
            t3 = p.tt(hview(m), gt[:, 0:512], ut[:, 0:512], ALU.mult,
                      deps=[t1, t2] + hres_m.wdeps())
            gt_r.note_r(t3)
            ut_r.note_r(t3)
            hres_m.note_r(t3)
        hsh_res.note_w(t3)
        hsh_res.reads = []
        hsh2_res.note_w(t3)
        hsh2_res.reads = []
        h_last = t3
        for m2 in range(NCH):
            pa, pa_r = P["acc"][m2 % 2]
            for q in range(4):
                wq_, wq_r = wstage[q % 2]
                lw3 = p.dma(wq_[:, 0:16 * 128]
                            .rearrange("p (c m) -> p c m", m=128),
                            sw3T[q * 2048:(q + 1) * 2048,
                                 m2 * 128:(m2 + 1) * 128]
                            .rearrange("(c p) m -> p c m", p=128),
                            deps=wq_r.wdeps(), eng="gpsimd", ch=f"wsg{q % 2}")
                wq_r.note_w(lw3)
                for kq in range(16):
                    k = q * 16 + kq
                    tok = p.mm(pa[:, 0:512], wq_[:, kq * 128:(kq + 1) * 128],
                               hview(k),
                               start=(k == 0), stop=(k == 63),
                               deps=[lw3, h_last]
                               + (pa_r.wdeps() if k == 0 else []))
                    wq_r.note_r(tok)
                    (hsh_res if k < 32 else hsh2_res).note_r(tok)
            pa_r.note_w(tok)
            ot, ot_r = o_t[m2 % 2]
            rp, rp_r = rtile[m2 % 2]
            lr = p.dma(rp[:, 0:512],
                       out_sl[m2 * 128:(m2 + 1) * 128, it * 512:(it + 1) * 512],
                       deps=rp_r.wdeps() + out_res[m2][it].rdeps(),
                       eng="sync", ch=f"rt{m2 % 2}")
            rp_r.note_w(lr)
            t4 = p.tt(ot[:, 0:512], rp[:, 0:512], pa[:, 0:512], ALU.add,
                      deps=[lr, tok] + ot_r.wdeps())
            ot_r.note_w(t4)
            rp_r.note_r(t4)
            pa_r.note_r(t4)
            d = p.dma(out_sl[m2 * 128:(m2 + 1) * 128,
                             it * 512:(it + 1) * 512], ot[:, 0:512],
                      deps=[t4] + out_res[m2][it].wdeps(), eng="sync")
            ot_r.note_r(d)
            out_res[m2][it].note_w(d)

    hallA = res_chain(hsh_res)
    hallB = res_chain(hsh2_res)

    def hview2(k):
        if k < 32:
            return A2[:, k * 512:(k + 1) * 512]
        return A1[:, 16384 + (k - 32) * 512:16384 + (k - 31) * 512]

    for it in range(2):
        for e in range(NE):
            lcr = p.dma(comb_row[0:1, 0:512],
                        comb_dram[e:e + 1, it * 512:(it + 1) * 512],
                        deps=[comb_done] + comb_row_r.wdeps(), eng="sync")
            comb_row_r.note_w(lcr)
            pc, pc_r = P["bcB"]
            t_cb = p.mm(pc[:, 0:512], C["ones_row_f"][0:1, 0:128],
                        comb_row[0:1, 0:512], start=True, stop=True,
                        deps=[lcr] + pc_r.wdeps())
            pc_r.note_w(t_cb)
            comb_row_r.note_r(t_cb)
            for m in range(8):
                wsg, wsg_r = wstage[m % 2]
                lg = p.dma(wsg[:, 0:NCH * 128]
                           .rearrange("p (c m) -> p c m", m=128),
                           w12T[e, :, m * 128:(m + 1) * 128]
                           .rearrange("(c p) m -> p c m", p=128),
                           deps=wsg_r.wdeps(), eng="gpsimd")
                wsg_r.note_w(lg)
                wsu, wsu_r = bigust[m % 2]
                lu = p.dma(wsu[:, 0:NCH * 128]
                           .rearrange("p (c m) -> p c m", m=128),
                           w12T[e, :, 1024 + m * 128:1024 + (m + 1) * 128]
                           .rearrange("(c p) m -> p c m", p=128),
                           deps=wsu_r.wdeps(), eng="gpsimd", ch=f"wsu{m % 2}")
                wsu_r.note_w(lu)
                pa, pa_r = P["acc"][0]
                pb, pb_r = P["acc"][1]
                for k in range(NCH):
                    tok = p.mm(pa[:, 0:512], wsg[:, k * 128:(k + 1) * 128],
                               A1[:, k * 1024 + it * 512:
                                   k * 1024 + it * 512 + 512],
                               start=(k == 0), stop=(k == NCH - 1),
                               deps=[lg, xn1_can_last]
                               + (pa_r.wdeps() if k == 0 else []))
                    wsg_r.note_r(tok)
                    xn1_res.note_r(tok)
                pa_r.note_w(tok)
                tg = tok
                for k in range(NCH):
                    tok = p.mm(pb[:, 0:512], wsu[:, k * 128:(k + 1) * 128],
                               A1[:, k * 1024 + it * 512:
                                   k * 1024 + it * 512 + 512],
                               start=(k == 0), stop=(k == NCH - 1),
                               deps=[lu, xn1_can_last]
                               + (pb_r.wdeps() if k == 0 else []))
                    wsu_r.note_r(tok)
                    xn1_res.note_r(tok)
                pb_r.note_w(tok)
                tu = tok
                gt, gt_r = g_t
                ut, ut_r = u_t
                t1 = p.act(gt[:, :], pa[:, 0:512], AF.Silu,
                           deps=[tg] + gt_r.wdeps())
                gt_r.note_w(t1)
                pa_r.note_r(t1)
                t2 = p.act(ut[:, :], pb[:, 0:512], AF.Copy,
                           deps=[tu] + ut_r.wdeps())
                ut_r.note_w(t2)
                pb_r.note_r(t2)
                t3 = p.tt(gt[:, :], gt[:, :], ut[:, :], ALU.mult,
                          deps=[t1, t2])
                gt_r.note_w(t3)
                ut_r.note_r(t3)
                kk = e * 8 + m
                hres_k = hallA if kk < 32 else hallB
                t4 = p.tt(hview2(kk), gt[:, :],
                          pc[:, 0:512], ALU.mult,
                          deps=[t3, t_cb] + hres_k.wdeps())
                gt_r.note_r(t4)
                pc_r.note_r(t4)
                hres_k.note_r(t4)
        hallA.note_w(t4)
        hallA.reads = []
        hallB.note_w(t4)
        hallB.reads = []
        hall_last = t4
        for m2 in range(NCH):
            pa, pa_r = P["acc"][m2 % 2]
            for q in range(4):
                wq_, wq_r = wstage[q % 2]
                lw3 = p.dma(wq_[:, 0:16 * 128]
                            .rearrange("p (c m) -> p c m", m=128),
                            w3i[q * 2048:(q + 1) * 2048,
                                m2 * 128:(m2 + 1) * 128]
                            .rearrange("(c p) m -> p c m", p=128),
                            deps=wq_r.wdeps(), eng="gpsimd", ch=f"wsg{q % 2}")
                wq_r.note_w(lw3)
                for kq in range(16):
                    k = q * 16 + kq
                    tok = p.mm(pa[:, 0:512], wq_[:, kq * 128:(kq + 1) * 128],
                               hview2(k), start=(k == 0), stop=(k == 63),
                               deps=[lw3, hall_last]
                               + (pa_r.wdeps() if k == 0 else []))
                    wq_r.note_r(tok)
                    (hallA if k < 32 else hallB).note_r(tok)
            pa_r.note_w(tok)
            ot, ot_r = o_t[m2 % 2]
            rp, rp_r = rtile[m2 % 2]
            lr = p.dma(rp[:, 0:512],
                       out_sl[m2 * 128:(m2 + 1) * 128,
                              it * 512:(it + 1) * 512],
                       deps=rp_r.wdeps() + out_res[m2][it].rdeps(),
                       eng="sync", ch=f"rt{m2 % 2}")
            rp_r.note_w(lr)
            t6 = p.tt(ot[:, 0:512], rp[:, 0:512], pa[:, 0:512], ALU.add,
                      deps=[lr, tok] + ot_r.wdeps())
            ot_r.note_w(t6)
            rp_r.note_r(t6)
            pa_r.note_r(t6)
            d = p.dma(out_sl[m2 * 128:(m2 + 1) * 128,
                             it * 512:(it + 1) * 512], ot[:, 0:512],
                      deps=[t6] + out_res[m2][it].wdeps(), eng="sync")
            ot_r.note_r(d)
            out_res[m2][it].note_w(d)

    return p.build()


def fused_inmaps(inputs):
    inputs = {k: np.asarray(v) for k, v in inputs.items()}
    x = inputs["L0_attn_x"]
    x_fm = np.ascontiguousarray(x.transpose(0, 2, 1))
    w = _prep_weights_l1(inputs)
    _g = w.pop("wgT").reshape(D, INTER // 128, 128)
    _u = w.pop("wuT").reshape(D, INTER // 128, 128)
    w["wguT"] = np.ascontiguousarray(
        np.concatenate([_g, _u], axis=2).reshape(D, 2 * INTER))
    _s1 = np.ascontiguousarray(inputs["L1_ffn_sw1"].T).reshape(
        D, HSH // 128, 128)
    _s2 = np.ascontiguousarray(inputs["L1_ffn_sw2"].T).reshape(
        D, HSH // 128, 128)
    w["swguT"] = np.ascontiguousarray(
        np.concatenate([_s1, _s2], axis=2).reshape(D, 2 * HSH))
    w.update({
        "woT": np.ascontiguousarray(inputs["L0_attn_w_o"].T),
        "bo_r": np.ascontiguousarray(inputs["L0_attn_b_o"].reshape(16, 128).T),
        "wv1": np.ascontiguousarray(inputs["L1_attn_Wv"]),
        "wq1": np.ascontiguousarray(inputs["L1_attn_Wq"]),
        "wg1": np.ascontiguousarray(inputs["L1_attn_Wg"]),
        "wo1": np.ascontiguousarray(inputs["L1_attn_Wo"]),
        "beta_row": np.ascontiguousarray(inputs["L1_attn_beta"].reshape(1, 16)),
        "lng_r": np.ascontiguousarray(
            inputs["L1_attn_ln_g"].reshape(16, 128).T),
        "lnb_r": np.ascontiguousarray(
            inputs["L1_attn_ln_b"].reshape(16, 128).T),
        "w12T": np.ascontiguousarray(inputs["L1_ffn_w12"].transpose(0, 2, 1)),
        "w3i": np.ascontiguousarray(
            inputs["L1_ffn_w3"].transpose(0, 2, 1).reshape(NE * 1024, D)),
        "routerT": np.ascontiguousarray(inputs["L1_ffn_router_w"].T),
        "ebias_col": np.ascontiguousarray(
            inputs["L1_ffn_expert_bias"].reshape(8, 1)),
        "sw3T": np.ascontiguousarray(inputs["L1_ffn_sw3"].T),
        "c_ones_col": np.ones((128, 1), np.float32),
        "c_idn": np.eye(128, dtype=np.float32),
    })
    nonce_val = np.asarray(
        [[np.random.randint(1, 2 ** 30), 0, 0, 0]], dtype=np.int32)
    in_maps = []
    for c in range(8):
        b, par = c // 4, c % 2
        m = {"x_half": np.ascontiguousarray(
                 x_fm[b][:, par * 2048:(par + 1) * 2048]),
             "parcol": np.full((16, 1), float(par), np.float32),
             "nonce": nonce_val}
        m.update(w)
        in_maps.append(m)
    return in_maps


def kernel(**inputs):
    in_maps = fused_inmaps(inputs)
    nc = build_fused()
    res = run_bass_kernel_spmd(nc, in_maps, core_ids=list(range(8)))
    out_fm = np.empty((2, D, S), np.float32)
    for c in range(8):
        b, par, pairbit = c // 4, c % 2, (c % 4) // 2
        t0 = par * 2048 + pairbit * 1024
        out_fm[b][:, t0:t0 + 1024] = res.results[c]["out_sl"]
    return np.ascontiguousarray(out_fm.transpose(0, 2, 1))
